# version 17
# baseline (speedup 1.0000x reference)
import numpy as np
import concourse.bass as bass
import concourse.mybir as mybir
from concourse.bass_utils import run_bass_kernel_spmd

F32 = mybir.dt.float32
BF16 = mybir.dt.bfloat16
AF = mybir.ActivationFunctionType
ALU = mybir.AluOpType
AX = mybir.AxisListType.X

D, E, H, C, DH, DC = 1024, 8, 16, 128, 64, 8


def build_program(T=2048, P=6144, F=4096, LB=512, n_cores=8):
    assert T % LB == 0 and P % LB == 0 and LB % C == 0 and F % 512 == 0
    NBP, NBM = P // LB, T // LB
    CPB = LB // C
    NCHT = T // C
    TT = T // 2
    NTC = TT // 128
    FB = F // 2
    NF = FB // 128
    W1S = max(NF // 2, 1)
    NS1 = NF // W1S
    NTH = max(TT // 512, 1)
    H512 = TT // NTH

    nc = bass.Bass("TRN2", target_bir_lowering=False)
    xTm = nc.dram_tensor("xTm", [D, T + 1], F32, kind="ExternalInput").ap()
    x_tm = nc.dram_tensor("x_tm", [T, D], F32, kind="ExternalInput").ap()
    gapcol = nc.dram_tensor("gapcol", [128, 8], F32, kind="ExternalInput").ap()
    maskadd = nc.dram_tensor("maskadd", [128, 8], F32, kind="ExternalInput").ap()
    cc_in = nc.dram_tensor("cc_in", [128, DC * DH], F32, kind="Internal").ap()
    NGRP = 4 if n_cores == 8 else n_cores
    cc_out = nc.dram_tensor("cc_out", [NGRP * 128, DC * DH], F32, kind="Internal").ap()
    WwT = nc.dram_tensor("WwT", [D, D], F32, kind="ExternalInput").ap()
    WrT = nc.dram_tensor("WrT", [D, D], F32, kind="ExternalInput").ap()
    gateT = nc.dram_tensor("gateT", [D, E], F32, kind="ExternalInput").ap()
    dcol = nc.dram_tensor("dcol", [128, H], F32, kind="ExternalInput").ap()
    lacol = nc.dram_tensor("lacol", [128, H], F32, kind="ExternalInput").ap()
    w1t = nc.dram_tensor("w1t", [E, D, F], F32, kind="ExternalInput").ap()
    w2t = nc.dram_tensor("w2t", [E, F, D], F32, kind="ExternalInput").ap()
    b1r = nc.dram_tensor("b1r", [128, E, F // 128], F32, kind="ExternalInput").ap()
    b2 = nc.dram_tensor("b2", [E, D], F32, kind="ExternalInput").ap()
    out = nc.dram_tensor("out", [T, D], F32, kind="ExternalOutput").ap()
    xres_hbm = nc.dram_tensor("xres_hbm", [T, D], F32, kind="Internal").ap()

    def sb(name, shape, dt):
        g = nc.sbuf_tensor(name, shape, dt)
        return g, g.__enter__()

    def sbp(name, shape, dt):
        return sb(name, shape, dt)[1]

    identb = sbp("identb", [128, 128], BF16)
    w_tm = sbp("w_tm", [128, 2 * NTC, E], F32)
    w_bfT = sbp("w_bfT", [128, E], BF16)
    w_fm = sbp("w_fm", [E, 2 * NTC, 128], BF16)
    m0 = sbp("m0", [128, 1], F32)
    m1c = sbp("m1c", [128, 1], F32)
    m2c = sbp("m2c", [128, 1], F32)
    sm_t = sbp("sm_t", [128, E], F32)
    b1_sb = sbp("b1_sb", [128, E, F // 128], F32)
    b2_sb = sbp("b2_sb", [E, D], BF16)
    gateT_sb = sbp("gateT_sb", [128, DC, E], BF16)
    xres_t = [sbp(f"xres_t{i}", [128, D], F32) for i in range(2)]
    xl2tm = [sbp(f"xl2tm{i}", [128, D], BF16) for i in range(2)]
    bn6 = sbp("bn6", [128, 12], F32)
    mv2 = sbp("mv2", [128, 2], F32)
    outb = [sbp(f"outb{i}", [128, D], F32) for i in range(2)]

    gA = []

    def sba(name, shape, dt):
        g, t = sb(name, shape, dt)
        gA.append(g)
        return t

    causal = sba("causal", [128, 128], F32)
    iotaD = sba("iotaD", [128, 128], F32)
    iotaC = sba("iotaC", [128, 128], F32)
    iota_rev = sba("iota_rev", [128, 1], F32)
    ones_bf = sba("ones_bf", [128, 1], BF16)
    onesb_row = sba("onesb_row", [1, 128], BF16)
    dcol_sb = sba("dcol_sb", [128, H], F32)
    lacol_sb = sba("lacol_sb", [128, H], F32)
    lg_cols = sba("lg_cols", [128, H], F32)
    al_cols = sba("al_cols", [128, H], F32)
    al_pair = sba("al_pair", [128, DC], F32)
    lgsel = sba("lgsel", [128, DC], F32)
    lgfull = sba("lgfull", [128, H, DH], F32)
    gCst = sba("gCst", [128, DC, DH], F32)
    gwfull = sba("gwfull", [128, H, DH], BF16)
    MT = sba("MT", [128, 2, DC, 128], BF16)
    gp_all = sba("gp_all", [128, DC, 128], BF16)
    gCmk = sba("gCmk", [128, DC, 16], F32)
    gw_tmp = sba("gw_tmp", [128, H, DH], F32)
    gp_tmp = sba("gp_tmp", [128, DC, 128], F32)
    stateT = sba("stateT", [128, DC, DH], F32)
    stateT_bf = sba("stateT_bf", [128, H, DH], BF16)
    v_tm = sba("v_tm", [128, DC, 128], BF16)
    vgw = sba("vgw", [128, DC, 128], BF16)
    wk_tm = sba("wk_tm", [128, DC, 128], BF16)
    stil = [sba(f"stil{g}", [128, 4, 128], BF16) for g in range(4)]
    rkgp = sba("rkgp", [128, DC, 128], BF16)
    arow = sba("arow", [1, LB], BF16)
    brow = sba("brow", [1, LB], BF16)
    a_sb = sba("a_sb", [128, LB], BF16)
    b_sb = sba("b_sb", [128, LB], BF16)
    sq_sb = sba("sq_sb", [128, DC, LB], BF16)
    r0 = sba("r0", [1, LB], F32)
    r1 = sba("r1", [1, LB], F32)
    r2 = sba("r2", [1, LB], F32)
    xln_m = sba("xln_m", [128, DC, T + 8], BF16)
    reads_fm = sba("reads_fm", [128, DC, T], BF16)
    WwT_sb = sba("WwT_sb", [128, DC, D], BF16)
    WrT_sb = sba("WrT_sb", [128, DC, D], BF16)
    gth_sb = sba("gth_sb", [128, 8, DC * DH], F32)
    lgst = sba("lgst", [128, DC, DH], F32)
    gapc_sb = sba("gapc_sb", [128, 8], F32)
    maskc_sb = sba("maskc_sb", [128, 8], F32)
    mt_tmp = sba("mt_tmp", [128, H // 2, 128], F32)
    for g in reversed(gA):
        g.__exit__(None, None, None)
    gB = []
    gg, xln2_fm = sb("xln2_fm", [128, DC, T], BF16); gB.append(gg)
    gg, g_sb = sb("g_sb", [128, NF, TT], BF16); gB.append(gg)
    gg, w1b = sb("w1b", [128, 2, DC, W1S * 128], BF16); gB.append(gg)
    gg, w2b = sb("w2b", [128, 2, NF, 512], BF16); gB.append(gg)
    gg, y_acc = sb("y_acc", [128, NTC, D], F32); gB.append(gg)
    for g in reversed(gB):
        g.__exit__(None, None, None)

    def psm(name, shape, dt):
        g = nc.psum_tensor(name, shape, dt)
        return g, g.__enter__()

    gPA = []
    gg, v_ps = psm("v_ps", [128, 1024], F32); gPA.append(gg)
    gg, wk_ps = psm("wk_ps", [128, 1024], BF16); gPA.append(gg)
    gg, delta_ps = psm("delta_ps", [128, 512], F32); gPA.append(gg)
    gLN = []
    gg, stx_ps = psm("stx_ps", [1, 512], F32); gLN.append(gg)
    gg, sts_ps = psm("sts_ps", [1, 512], F32); gLN.append(gg)
    gg, bca_ps = psm("bca_ps", [128, 512], F32); gLN.append(gg)
    gg, bcb_ps = psm("bcb_ps", [128, 512], F32); gLN.append(gg)
    for g in reversed(gLN):
        g.__exit__(None, None, None)
    gMS = []
    gg, st_pse = psm("st_pse", [128, 512], F32); gMS.append(gg)
    gg, st_pso = psm("st_pso", [128, 512], F32); gMS.append(gg)
    gg, reads_ps = psm("reads_ps", [128, 1024], F32); gMS.append(gg)
    for g in reversed(gMS):
        g.__exit__(None, None, None)
    for g in reversed(gPA):
        g.__exit__(None, None, None)
    gP3 = []
    gg, op_ps = psm("op_ps", [128, 2, 1024], F32); gP3.append(gg)
    gg, tr_ps = psm("tr_ps", [128, 2, 1024], BF16); gP3.append(gg)
    for g in reversed(gP3):
        g.__exit__(None, None, None)
    gP4 = []
    gg, h_ps = psm("h_ps", [128, 2, 512], F32); gP4.append(gg)
    gg, y_ps = psm("y_ps", [128, 2, 512], F32); gP4.append(gg)
    gg, rt_ps = psm("rt_ps", [128, 512], F32); gP4.append(gg)
    gg, wf_ps = psm("wf_ps", [8, 2 * NTC * 128], BF16); gP4.append(gg)
    for g in reversed(gP4):
        g.__exit__(None, None, None)

    sems = {}
    for n in ["TS", "VS", "SS", "GC", "SDP", "GDP", "SR",
              "GX0", "GX1", "SX0", "SX1", "SO0", "SO1",
              "GW10", "GW11", "GW20", "GW21", "CCI", "CCS", "CCO"]:
        g = nc.semaphore(n)
        sems[n] = g.__enter__()
    blk = nc.Block()
    block = blk.__enter__()

    def stream(which, eng):
        cnt = {k: 0 for k in sems}

        def pub(name, n=1):
            cnt[name] += n
            return cnt[name]

        def wait(name, val):
            if eng is not None and val > 0:
                eng.wait_ge(sems[name], val)

        def me(o):
            return which == o

        if me('gpsimd'):
            eng.memset(ones_bf[:], 1.0)
            eng.memset(onesb_row[:], 1.0)
            eng.iota(iotaD[:], pattern=[[1, 128]], base=-1, channel_multiplier=-1,
                     allow_small_or_imprecise_dtypes=True)
            eng.drain()
            eng.tensor_scalar_max(iotaD[:], iotaD[:], 0.0)
            eng.iota(iotaC[:], pattern=[[1, 128]], base=0, channel_multiplier=0,
                     allow_small_or_imprecise_dtypes=True)
            eng.iota(iota_rev[:], pattern=[[0, 1]], base=C - 1, channel_multiplier=-1,
                     allow_small_or_imprecise_dtypes=True)
            eng.memset(causal[:], 0.0)
            eng.memset(identb[:], 0.0)
            eng.memset(xln_m[:, :, T + 1:T + 8], 0.0)
            eng.drain()
            eng.affine_select(out=causal[:], in_=causal[:], compare_op=ALU.is_ge,
                              fill=1.0, base=0, pattern=[[-1, 128]], channel_multiplier=1)
            eng.affine_select(out=identb[:], in_=identb[:], compare_op=ALU.not_equal,
                              fill=1.0, base=0, pattern=[[-1, 128]], channel_multiplier=1)
            eng.drain().then_inc(sems['GC'], 1)
        pub('GC')
        GC_c = cnt['GC']

        if me('gpsimd'):
            eng.dma_start(out=xln_m[:, :, 0:T + 1],
                          in_=xTm.rearrange("(a p) t -> p a t", p=128)).then_inc(sems['GDP'], 16)
            eng.dma_start(out=WwT_sb[:],
                          in_=WwT.rearrange("(a p) i -> p a i", p=128)).then_inc(sems['GDP'], 16)
            eng.dma_start(out=WrT_sb[:],
                          in_=WrT.rearrange("(a p) i -> p a i", p=128)).then_inc(sems['GDP'], 16)
            eng.dma_start(out=gateT_sb[:],
                          in_=gateT.rearrange("(a p) e -> p a e", p=128)).then_inc(sems['GDP'], 16)
            eng.dma_start(out=b2_sb[:], in_=b2).then_inc(sems['GDP'], 16)
        pub('GDP', 16 * 5)
        GD_pre = cnt['GDP']

        if me('sync'):
            eng.dma_start(out=dcol_sb[:], in_=dcol).then_inc(sems['SDP'], 16)
            eng.dma_start(out=lacol_sb[:], in_=lacol).then_inc(sems['SDP'], 16)
            eng.dma_start(out=b1_sb[:], in_=b1r).then_inc(sems['SDP'], 16)
            eng.dma_start(out=gapc_sb[:], in_=gapcol).then_inc(sems['SDP'], 16)
            eng.dma_start(out=maskc_sb[:], in_=maskadd).then_inc(sems['SDP'], 16)
        pub('SDP', 16 * 5)
        SD_pre = cnt['SDP']

        if me('scalar'):
            wait('SDP', SD_pre)
            eng.activation(lg_cols[:], dcol_sb[:], AF.Sigmoid)
            eng.drain()
            eng.activation(lg_cols[:], lg_cols[:], AF.Ln)
            eng.activation(al_cols[:], lacol_sb[:], AF.Exp)
            eng.drain().then_inc(sems['SS'], 1)
        pub('SS')
        SS_rows = cnt['SS']

        if me('vector'):
            wait('SS', SS_rows)
            wait('GC', GC_c)
            eng.tensor_copy(lgfull[:], lg_cols.ap().unsqueeze(2).broadcast_to((128, H, DH)))
            eng.drain()
            eng.tensor_copy(al_pair[0:64, :], al_cols[0:64, 0:H:2])
            eng.tensor_copy(al_pair[64:128, :], al_cols[64:128, 1:H:2])
            eng.tensor_copy(lgsel[0:64, :], lg_cols[0:64, 0:H:2])
            eng.tensor_copy(lgsel[64:128, :], lg_cols[64:128, 1:H:2])
            eng.tensor_copy(lgst[0:64, :, :], lgfull[0:64, 0:H:2, :])
            eng.tensor_copy(lgst[64:128, :, :], lgfull[64:128, 1:H:2, :])
            eng.drain()
            for h in range(H // 2):
                eng.tensor_scalar(mt_tmp[:, h, :], iotaD[:], lg_cols[:, h:h + 1], None, ALU.mult)
            eng.tensor_tensor(gp_tmp[:], iotaC.ap().unsqueeze(1).broadcast_to((128, DC, 128)),
                              lgsel.ap().unsqueeze(2).broadcast_to((128, DC, 128)), ALU.mult)
            eng.tensor_scalar(gw_tmp[:], lgfull[:], iota_rev[:, 0:1], None, ALU.mult)
            eng.tensor_tensor(gCmk[:], lgsel.ap().unsqueeze(2).broadcast_to((128, DC, 16)),
                              iotaC.ap()[:, 0:16].unsqueeze(1).broadcast_to((128, DC, 16)),
                              ALU.mult)
            eng.drain().then_inc(sems['VS'], 1)
            eng.memset(stateT[:], 0.0)
            eng.memset(stateT_bf[:], 0.0)
            eng.drain().then_inc(sems['VS'], 1)
        pub('VS'); VS_pre1 = cnt['VS']
        pub('VS'); VS_state0 = cnt['VS']

        if me('scalar'):
            wait('VS', VS_pre1)
            eng.activation(mt_tmp[:], mt_tmp[:], AF.Exp)
            eng.activation(gp_all.ap(), gp_tmp[:], AF.Exp)
            eng.activation(gwfull.ap(), gw_tmp[:], AF.Exp)
            eng.activation(gCmk[:], gCmk[:], AF.Exp, scale=float(C))
            eng.activation(gCst[0:64, :, :], lgfull[0:64, 0:H:2, :], AF.Exp, scale=float(C))
            eng.activation(gCst[64:128, :, :], lgfull[64:128, 1:H:2, :], AF.Exp, scale=float(C))
            eng.drain().then_inc(sems['SS'], 1)
        pub('SS'); SS_exps = cnt['SS']

        if me('vector'):
            wait('SS', SS_exps)
            eng.tensor_tensor(MT.ap()[:, :, 0:4, :].transpose([0, 2, 1, 3]),
                              mt_tmp.ap().rearrange("p (a b) c -> p a b c", b=2),
                              causal.ap().unsqueeze(1).unsqueeze(1).broadcast_to((128, 4, 2, 128)),
                              ALU.mult)
            eng.drain()
            for h in range(H // 2, H):
                eng.tensor_scalar(mt_tmp[:, h - H // 2, :], iotaD[:],
                                  lg_cols[:, h:h + 1], None, ALU.mult)
            eng.drain().then_inc(sems['VS'], 1)
        pub('VS'); VS_mt2 = cnt['VS']
        if me('scalar'):
            wait('VS', VS_mt2)
            eng.activation(mt_tmp[:], mt_tmp[:], AF.Exp)
            eng.drain().then_inc(sems['SS'], 1)
        pub('SS'); SS_mt2 = cnt['SS']
        if me('vector'):
            wait('SS', SS_mt2)
            eng.tensor_tensor(MT.ap()[:, :, 4:8, :].transpose([0, 2, 1, 3]),
                              mt_tmp.ap().rearrange("p (a b) c -> p a b c", b=2),
                              causal.ap().unsqueeze(1).unsqueeze(1).broadcast_to((128, 4, 2, 128)),
                              ALU.mult)
            eng.drain().then_inc(sems['VS'], 1)
        pub('VS'); VS_MT = cnt['VS']

        lnv = dict(rows=0, bc=0, sqfree=0)

        def ln_block(buf, c0, n, dep=None):
            if me('scalar'):
                if dep is not None:
                    wait(dep[0], dep[1])
                wait('TS', lnv['sqfree'])
                for dj in range(DC):
                    eng.activation(sq_sb[:, dj, 0:n], buf[:, dj, c0:c0 + n], AF.Square)
                eng.drain().then_inc(sems['SS'], 1)
            pub('SS'); ss_sq = cnt['SS']
            if me('tensor'):
                if dep is not None:
                    wait(dep[0], dep[1])
                wait('SS', ss_sq)
                wait('VS', lnv['rows'])
                for dj in range(DC):
                    eng.matmul(stx_ps[:, 0:n], ones_bf[:], buf[:, dj, c0:c0 + n],
                               start=(dj == 0), stop=(dj == DC - 1))
                for dj in range(DC):
                    inst = eng.matmul(sts_ps[:, 0:n], ones_bf[:], sq_sb[:, dj, 0:n],
                                      start=(dj == 0), stop=(dj == DC - 1))
                inst.then_inc(sems['TS'], 1)
            pub('TS'); ts_st = cnt['TS']; lnv['sqfree'] = ts_st
            if me('vector'):
                wait('TS', ts_st)
                eng.tensor_scalar(r0[:, 0:n], stx_ps[:, 0:n], 1.0 / D, None, ALU.mult)
                eng.tensor_scalar(r1[:, 0:n], sts_ps[:, 0:n], 1.0 / D, 1e-5, ALU.mult, ALU.add)
                eng.drain()
                eng.tensor_tensor(r2[:, 0:n], r0[:, 0:n], r0[:, 0:n], ALU.mult)
                eng.drain()
                eng.tensor_tensor(r1[:, 0:n], r1[:, 0:n], r2[:, 0:n], ALU.subtract)
                eng.drain().then_inc(sems['VS'], 1)
            pub('VS'); vs_var = cnt['VS']; lnv['rows'] = vs_var
            if me('scalar'):
                wait('VS', vs_var)
                eng.activation(r1[:, 0:n], r1[:, 0:n], AF.Ln)
                eng.drain()
                eng.activation(r1[:, 0:n], r1[:, 0:n], AF.Exp, scale=-0.5)
                eng.drain().then_inc(sems['SS'], 1)
            pub('SS'); ss_rs = cnt['SS']
            if me('vector'):
                wait('SS', ss_rs)
                eng.tensor_copy(arow[:, 0:n], r1[:, 0:n])
                eng.tensor_tensor(r2[:, 0:n], r0[:, 0:n], r1[:, 0:n], ALU.mult)
                eng.drain()
                eng.tensor_scalar(brow[:, 0:n], r2[:, 0:n], -1.0, None, ALU.mult)
                eng.drain().then_inc(sems['VS'], 1)
            pub('VS'); vs_ab = cnt['VS']
            if me('tensor'):
                wait('VS', vs_ab)
                wait('VS', lnv['bc'])
                eng.matmul(bca_ps[:, 0:n], onesb_row[:], arow[:, 0:n], start=True, stop=True)
                inst = eng.matmul(bcb_ps[:, 0:n], onesb_row[:], brow[:, 0:n], start=True, stop=True)
                inst.then_inc(sems['TS'], 1)
            pub('TS'); ts_bc = cnt['TS']
            if me('vector'):
                wait('TS', ts_bc)
                eng.tensor_copy(a_sb[:, 0:n], bca_ps[:, 0:n])
                eng.tensor_copy(b_sb[:, 0:n], bcb_ps[:, 0:n])
                eng.drain()
                for dj in range(DC):
                    eng.tensor_tensor(buf[:, dj, c0:c0 + n], buf[:, dj, c0:c0 + n],
                                      a_sb[:, 0:n], ALU.mult)
                eng.drain()
                for dj in range(DC):
                    eng.tensor_tensor(buf[:, dj, c0:c0 + n], buf[:, dj, c0:c0 + n],
                                      b_sb[:, 0:n], ALU.add)
                eng.drain().then_inc(sems['VS'], 1)
            pub('VS'); lnv['bc'] = cnt['VS']
            return cnt['VS']

        scn = dict(vA=0, vA2=0, tRD=0, tID=0, vST={}, vRD=0, vstate=VS_state0, t0=0)

        def scan_chunk(buf, wk0, full, cast, pre_vs=0, ps_alt=False):
            if ps_alt:
                vps = reads_ps
                wps = st_pso.ap().bitcast(BF16)
            else:
                vps = v_ps
                wps = wk_ps.ap()
            if me('tensor'):
                wait('VS', scn['vA2'] if ps_alt else scn['vA'])
                wait('GDP', GD_pre)
                if pre_vs:
                    wait('VS', pre_vs)
                for ih in range(2):
                    for dj in range(DC):
                        eng.matmul(vps[:, ih * 512:(ih + 1) * 512],
                                   buf[:, dj, wk0 + 1:wk0 + 1 + C],
                                   WwT_sb[:, dj, ih * 512:(ih + 1) * 512],
                                   start=(dj == 0), stop=(dj == DC - 1))
                for dc in range(DC):
                    inst = eng.transpose(
                        wps.rearrange("p (a c) -> p a c", a=DC)[:, dc, :],
                        buf[:, dc, wk0:wk0 + C], identb[:])
                inst.then_inc(sems['TS'], 1)
            pub('TS'); tA = cnt['TS']
            if me('vector'):
                wait('TS', tA)
                if scn['tRD']:
                    wait('TS', scn['tRD'])
                eng.tensor_copy(v_tm.ap().rearrange("p a c -> p (a c)"), vps[:])
                eng.tensor_copy(wk_tm.ap().rearrange("p a c -> p (a c)"), wps)
                eng.tensor_tensor(vgw.ap().rearrange("p a c -> p (a c)"),
                                  vps[:],
                                  gwfull.ap().rearrange("p h i -> p (h i)"), ALU.mult)
                eng.drain().then_inc(sems['VS'], 1)
            pub('VS')
            if ps_alt:
                scn['vA2'] = cnt['VS']
            else:
                scn['vA'] = cnt['VS']
            vA = cnt['VS']

            if full:
                for g in range(4):
                    heads = [4 * g + j for j in range(4)]
                    if me('tensor'):
                        wait('VS', scn['vST'].get(g - 2 if g >= 2 else g + 2, 0))
                        if g < 2:
                            wait('VS', VS_MT)
                        for j, h in enumerate(heads):
                            par, dcb = h % 2, h // 2
                            slot = (g % 2) * 256 + (j // 2) * 128
                            bank = st_pse if par == 0 else st_pso
                            inst = eng.matmul(bank[:, slot:slot + 128],
                                              buf[par * 64:par * 64 + 64, dcb, wk0:wk0 + C],
                                              buf[par * 64:par * 64 + 64, dcb, wk0 + 1:wk0 + 1 + C],
                                              start=True, stop=True)
                        inst.then_inc(sems['TS'], 1)
                    pub('TS'); tst = cnt['TS']
                    if me('vector'):
                        wait('TS', tst)
                        sl0 = (g % 2) * 256
                        eng.tensor_tensor(
                            stil[g].ap()[:, 0:2, :],
                            st_pse.ap()[:, sl0:sl0 + 256].rearrange("p (a c) -> p a c", a=2),
                            MT[:, 0, 2 * g:2 * g + 2, :], ALU.mult)
                        eng.tensor_tensor(
                            stil[g].ap()[:, 2:4, :],
                            st_pso.ap()[:, sl0:sl0 + 256].rearrange("p (a c) -> p a c", a=2),
                            MT[:, 1, 2 * g:2 * g + 2, :], ALU.mult)
                        eng.tensor_tensor(rkgp[0:64, 2 * g:2 * g + 2, :],
                                          buf[0:64, 2 * g:2 * g + 2, wk0 + 1:wk0 + 1 + C],
                                          gp_all[0:64, 2 * g:2 * g + 2, :], ALU.mult)
                        eng.tensor_tensor(rkgp[64:128, 2 * g:2 * g + 2, :],
                                          buf[64:128, 2 * g:2 * g + 2, wk0 + 1:wk0 + 1 + C],
                                          gp_all[64:128, 2 * g:2 * g + 2, :], ALU.mult)
                        eng.drain().then_inc(sems['VS'], 1)
                    pub('VS'); scn['vST'][g] = cnt['VS']
                    if me('tensor'):
                        if g >= 1 and full:
                            wait('VS', vA)
                            for _ in range(2):
                                eng.matmul(vps[:, 0:512], WwT_sb[:, 0, 0:128],
                                           WwT_sb[:, 1, 0:512], start=True, stop=True)
                        wait('VS', scn['vST'][g])
                        wait('VS', vA)
                        wait('VS', scn['vstate'])
                        if g == 0:
                            wait('SS', scn['vRD'])
                        rview = reads_ps.ap().rearrange("p (a c) -> p a c", a=DC)
                        dview = delta_ps.ap().rearrange("p (a c) -> p a c", a=DC)
                        for j, h in enumerate(heads):
                            par, dcb = h % 2, h // 2
                            pb = par * 64
                            eng.matmul(rview[pb:pb + 64, dcb, :], stateT_bf[:, h, :],
                                       rkgp[:, dcb, :], start=True, stop=False)
                            eng.matmul(rview[pb:pb + 64, dcb, :],
                                       v_tm[:, dcb, pb:pb + 64],
                                       stil[g][:, (j % 2) * 2 + j // 2, :],
                                       start=False, stop=True)
                            inst = eng.matmul(dview[pb:pb + 64, dcb, :],
                                              wk_tm[:, dcb, pb:pb + 64], vgw[:, dcb, pb:pb + 64],
                                              start=True, stop=True)
                        inst.then_inc(sems['TS'], 1)
                    pub('TS'); scn['tID'] = cnt['TS']
                scn['tRD'] = scn['tID']
                t_inter = scn['tRD']
            else:
                if me('tensor'):
                    wait('VS', vA)
                    wait('VS', scn['vstate'])
                    dview = delta_ps.ap().rearrange("p (a c) -> p a c", a=DC)
                    for h in range(H):
                        par, dcb = h % 2, h // 2
                        pb = par * 64
                        inst = eng.matmul(dview[pb:pb + 64, dcb, :],
                                          wk_tm[:, dcb, pb:pb + 64], vgw[:, dcb, pb:pb + 64],
                                          start=True, stop=True)
                    inst.then_inc(sems['TS'], 1)
                pub('TS'); scn['tRD'] = cnt['TS']
                scn['tID'] = scn['tRD']
                t_inter = 0

            if me('vector'):
                wait('TS', scn['tID'])
                eng.tensor_tensor(stateT.ap().rearrange("p a i -> p (a i)"),
                                  stateT.ap().rearrange("p a i -> p (a i)"),
                                  gCst.ap().rearrange("p a i -> p (a i)"), ALU.mult)
                eng.drain()
                eng.tensor_tensor(stateT.ap().rearrange("p a i -> p (a i)"),
                                  stateT.ap().rearrange("p a i -> p (a i)"),
                                  delta_ps[:], ALU.add)
                eng.drain()
                if cast:
                    eng.tensor_copy(stateT_bf[0:64, 0:H:2, :], stateT[0:64, :, :])
                    eng.tensor_copy(stateT_bf[64:128, 1:16:2, :], stateT[64:128, :, :])
                    eng.drain()
                eng.drain().then_inc(sems['VS'], 1)
            pub('VS'); scn['vstate'] = cnt['VS']

            if full:
                if me('scalar'):
                    wait('TS', scn['tRD'])
                    rview = reads_ps.ap().rearrange("p (a c) -> p a c", a=DC)
                    for dcb in range(DC):
                        eng.activation(reads_fm[:, dcb, scn['t0']:scn['t0'] + C],
                                       rview[:, dcb, :], AF.Copy,
                                       scale=al_pair[:, dcb:dcb + 1])
                    eng.drain().then_inc(sems['SS'], 1)
                pub('SS'); scn['vRD'] = cnt['SS']

        ln_block(xln_m.ap(), 0, 1, dep=('GDP', GD_pre))
        for b in range(NBM):
            vs_lnb = ln_block(xln_m.ap(), 1 + b * LB, LB)
            for ci in range(CPB):
                m = b * CPB + ci
                scn['t0'] = m * C
                scan_chunk(xln_m.ap(), m * C, full=True, cast=True,
                           pre_vs=(vs_lnb if ci == 0 else 0))
        VS_lite = scn['vstate']
        SS_scan_end = cnt['SS']
        VS_scan_end0 = cnt['VS']
        if me('gpsimd'):
            wait('VS', VS_lite)
            eng.dma_start(out=cc_in,
                          in_=stateT.ap().rearrange("p a i -> p (a i)")).then_inc(sems['CCI'], 16)
            eng.wait_ge(sems['CCI'], 16)
            eng.collective_compute(
                "AllGather", mybir.AluOpType.bypass,
                replica_groups=([[0, 1, 2, 3], [4, 5, 6, 7]] if n_cores == 8
                                else [list(range(n_cores))]),
                ins=[cc_in.opt()], outs=[cc_out.opt()],
            ).then_inc(sems['CCS'])
            eng.wait_ge(sems['CCS'], 1)
            eng.dma_start(out=gth_sb[:, 0:NGRP, :],
                          in_=cc_out.rearrange("(r p) i -> p r i", p=128)).then_inc(sems['CCO'], 16)
        pub('CCI', 16); pub('CCS', 1); pub('CCO', 16)
        CCO_done = cnt['CCO']
        if me('tensor'):
            wait('VS', VS_lite)
            for _ in range(220):
                eng.matmul(v_ps[:, 0:512], WwT_sb[:, 0, 0:128],
                           WwT_sb[:, 1, 0:512], start=True, stop=True)
        vs_comb = 0
        if me('vector'):
            wait('CCO', CCO_done)
            eng.memset(stateT[:], 0.0)
            eng.drain().then_inc(sems['VS'], 1)
        pub('VS'); vs_m0 = cnt['VS']
        for r in range(NGRP):
            if me('vector'):
                if r == 0:
                    wait('VS', vs_m0)
                eng.tensor_scalar(gp_tmp.ap().rearrange("p a c -> p (a c)")[:, 0:DC * DH],
                                  lgst.ap().rearrange("p a i -> p (a i)"),
                                  gapc_sb[:, r:r + 1], maskc_sb[:, r:r + 1],
                                  ALU.mult, ALU.add)
                eng.drain().then_inc(sems['VS'], 1)
            pub('VS'); vs_e1 = cnt['VS']
            if me('scalar'):
                wait('VS', vs_e1)
                eng.activation(gp_tmp.ap().rearrange("p a c -> p (a c)")[:, 0:DC * DH],
                               gp_tmp.ap().rearrange("p a c -> p (a c)")[:, 0:DC * DH], AF.Exp)
                eng.drain().then_inc(sems['SS'], 1)
            pub('SS'); ss_e1 = cnt['SS']
            if me('vector'):
                wait('SS', ss_e1)
                eng.tensor_tensor(gp_tmp.ap().rearrange("p a c -> p (a c)")[:, DC * DH:2 * DC * DH],
                                  gp_tmp.ap().rearrange("p a c -> p (a c)")[:, 0:DC * DH],
                                  gth_sb[:, r, :], ALU.mult)
                eng.drain()
                eng.tensor_tensor(stateT.ap().rearrange("p a i -> p (a i)"),
                                  stateT.ap().rearrange("p a i -> p (a i)"),
                                  gp_tmp.ap().rearrange("p a c -> p (a c)")[:, DC * DH:2 * DC * DH],
                                  ALU.add)
                eng.drain().then_inc(sems['VS'], 1)
            pub('VS'); vs_comb = cnt['VS']
        if me('vector'):
            eng.tensor_copy(stateT_bf[0:64, 0:H:2, :], stateT[0:64, :, :])
            eng.tensor_copy(stateT_bf[64:128, 1:16:2, :], stateT[64:128, :, :])
            eng.drain().then_inc(sems['VS'], 1)
        pub('VS'); scn['vstate'] = cnt['VS']

        vs_corr = scn['vstate']
        ts_corr = 0
        for m in range(NCHT):
            if me('vector'):
                if m == 0:
                    wait('VS', scn['vstate'])
                else:
                    wait('TS', ts_corr)
                eng.tensor_tensor(vgw.ap()[:], gp_all[:],
                                  gCmk.ap()[:, :, m:m + 1].broadcast_to((128, DC, 128)),
                                  ALU.mult)
                eng.drain()
                eng.tensor_tensor(rkgp.ap()[:], xln_m[:, :, m * C + 1:m * C + 1 + C],
                                  vgw.ap()[:], ALU.mult)
                eng.drain().then_inc(sems['VS'], 1)
            pub('VS'); vs_rk3 = cnt['VS']
            if me('tensor'):
                wait('VS', vs_rk3)
                if m == 0:
                    wait('SS', SS_scan_end)
                else:
                    wait('SS', cnt['SS'])
                rview = reads_ps.ap().rearrange("p (a c) -> p a c", a=DC)
                for h in range(H):
                    par, dcb = h % 2, h // 2
                    pb = par * 64
                    inst = eng.matmul(rview[pb:pb + 64, dcb, :], stateT_bf[:, h, :],
                                      rkgp[:, dcb, :], start=True, stop=True)
                inst.then_inc(sems['TS'], 1)
            pub('TS'); ts_corr = cnt['TS']
            if me('scalar'):
                wait('TS', ts_corr)
                rview = reads_ps.ap().rearrange("p (a c) -> p a c", a=DC)
                for dcb in range(DC):
                    eng.activation(rkgp[:, dcb, :], rview[:, dcb, :], AF.Copy,
                                   scale=al_pair[:, dcb:dcb + 1])
                eng.drain().then_inc(sems['SS'], 1)
            pub('SS'); ss_corr = cnt['SS']
            if me('vector'):
                wait('SS', ss_corr)
                eng.tensor_tensor(reads_fm[:, :, m * C:m * C + C],
                                  reads_fm[:, :, m * C:m * C + C],
                                  rkgp.ap()[:], ALU.add)
                eng.drain().then_inc(sems['VS'], 1)
            pub('VS')
        VS_scan_end = cnt['VS']

        sd_x = {}
        ts_tr = {}
        vs_slot = {}
        vs_x2 = {}
        tr_emitted = {}
        for m in range(NCHT):
            sl = m % 2
            if me('sync'):
                if m >= 2:
                    wait('VS', vs_slot[m - 2])
                    wait('SR', 16 * m)
                eng.dma_start(out=xres_t[sl][:],
                              in_=x_tm[m * 128:(m + 1) * 128, :]).then_inc(sems['SX%d' % sl], 16)
            pub('SX%d' % sl, 16); sd_x[m] = cnt['SX%d' % sl]
            if me('tensor'):
                if m == 0:
                    wait('SS', SS_scan_end)
                    wait('VS', VS_scan_end)
                if m >= 2:
                    wait('VS', vs_slot[m - 2])
                for ih in range(2):
                    for dcb in range(DC):
                        eng.matmul(op_ps[:, sl, ih * 512:(ih + 1) * 512],
                                   reads_fm[:, dcb, m * 128:(m + 1) * 128],
                                   WrT_sb[:, dcb, ih * 512:(ih + 1) * 512],
                                   start=(dcb == 0), stop=False)
                    for q in range(4):
                        inst = eng.matmul(
                            op_ps[:, sl, ih * 512 + q * 128: ih * 512 + (q + 1) * 128],
                            xln_m[:, ih * 4 + q, m * 128 + 1: m * 128 + 129],
                            identb[:], start=False, stop=(q == 3))
                inst.then_inc(sems['TS'], 1)
            pub('TS'); ts_op = cnt['TS']
            if me('vector'):
                wait('TS', ts_op)
                wait('SX%d' % sl, sd_x[m])
                eng.tensor_tensor(xres_t[sl][:], op_ps.ap()[:, sl, :], xres_t[sl][:], ALU.add)
                eng.drain().then_inc(sems['VS'], 1)
            pub('VS'); vs_r0 = cnt['VS']
            if me('sync'):
                wait('VS', vs_r0)
                eng.dma_start(out=xres_hbm[m * 128:(m + 1) * 128, :],
                              in_=xres_t[sl][:]).then_inc(sems['SR'], 16)
            pub('SR', 16)
            if me('vector'):
                eng.bn_stats(bn6[:, 0:6], xres_t[sl][:, 0:512])
                eng.bn_stats(bn6[:, 6:12], xres_t[sl][:, 512:1024])
                eng.drain()
                eng.bn_aggr(mv2[:], bn6[:])
                eng.drain()
                eng.tensor_scalar(m1c[:], mv2[:, 1:2], 1.0, 1e-5, ALU.mult, ALU.add)
                eng.drain().then_inc(sems['VS'], 1)
            pub('VS'); vs_v2 = cnt['VS']
            if me('scalar'):
                wait('VS', vs_v2)
                eng.activation(m1c[:], m1c[:], AF.Ln)
                eng.drain()
                eng.activation(m1c[:], m1c[:], AF.Exp, scale=-0.5)
                eng.drain().then_inc(sems['SS'], 1)
            pub('SS'); ss_r2 = cnt['SS']
            if me('vector'):
                wait('SS', ss_r2)
                if m >= 2:
                    wait('TS', ts_tr[m - 2])
                eng.tensor_scalar(xl2tm[sl][:], xres_t[sl][:], mv2[:, 0:1], m1c[:, 0:1],
                                  ALU.subtract, ALU.mult)
                eng.drain().then_inc(sems['VS'], 1)
            pub('VS'); vs_slot[m] = cnt['VS']

            def emit_tr(mm):
                sl2 = mm % 2
                if me('tensor'):
                    wait('VS', vs_slot[mm])
                    if mm >= 2:
                        wait('VS', vs_x2[mm - 2])
                    for dj in range(DC):
                        inst = eng.transpose(tr_ps.ap()[:, sl2, dj * 128:(dj + 1) * 128],
                                             xl2tm[sl2][:, dj * 128:(dj + 1) * 128], identb[:])
                    inst.then_inc(sems['TS'], 1)
                pub('TS'); ts_tr[mm] = cnt['TS']
                if me('vector'):
                    wait('TS', ts_tr[mm])
                    for dj in range(DC):
                        eng.tensor_copy(xln2_fm[:, dj, mm * 128:(mm + 1) * 128],
                                        tr_ps.ap()[:, sl2, dj * 128:(dj + 1) * 128])
                    eng.drain().then_inc(sems['VS'], 1)
                pub('VS'); vs_x2[mm] = cnt['VS']

            tr_emitted[m] = False
            if m >= 1:
                emit_tr(m - 1)
            if m == NCHT - 1:
                emit_tr(m)
        VS_p3 = cnt['VS']
        TS_p3 = cnt['TS']

        vs_wprev = VS_p3
        ts_wtr = 0
        for tc in range(2 * NTC):
            if me('tensor'):
                wait('VS', vs_wprev)
                for dj in range(DC):
                    inst = eng.matmul(rt_ps[:, 0:E], xln2_fm[:, dj, tc * 128:(tc + 1) * 128],
                                      gateT_sb[:, dj, :], start=(dj == 0), stop=(dj == DC - 1))
                inst.then_inc(sems['TS'], 1)
            pub('TS'); ts_l = cnt['TS']
            if me('vector'):
                wait('TS', ts_l)
                eng.reduce_max(m0[:], rt_ps[:, 0:E], axis=AX)
                eng.drain()
                eng.tensor_scalar(sm_t[:], rt_ps[:, 0:E], m0[:, 0:1], None, ALU.subtract)
                eng.drain().then_inc(sems['VS'], 1)
            pub('VS'); vs_sm = cnt['VS']
            if me('scalar'):
                wait('VS', vs_sm)
                eng.activation(sm_t[:], sm_t[:], AF.Exp)
                eng.drain().then_inc(sems['SS'], 1)
            pub('SS'); ss_e = cnt['SS']
            if me('vector'):
                wait('SS', ss_e)
                eng.reduce_sum(m0[:], sm_t[:], axis=AX)
                eng.drain()
                eng.reciprocal(m1c[:], m0[:])
                eng.drain()
                eng.tensor_scalar(sm_t[:], sm_t[:], m1c[:, 0:1], None, ALU.mult)
                eng.drain()
                eng.reduce_max(m0[:], sm_t[:], axis=AX)
                eng.drain()
                eng.tensor_scalar(w_tm[:, tc, :], sm_t[:], m0[:, 0:1], None, ALU.is_lt)
                eng.drain()
                eng.tensor_tensor(w_tm[:, tc, :], w_tm[:, tc, :], sm_t[:], ALU.mult)
                eng.drain()
                eng.reduce_max(m1c[:], w_tm[:, tc, :], axis=AX)
                eng.drain()
                eng.tensor_scalar(w_tm[:, tc, :], sm_t[:], m1c[:, 0:1], None, ALU.is_ge)
                eng.drain()
                eng.tensor_tensor(w_tm[:, tc, :], w_tm[:, tc, :], sm_t[:], ALU.mult)
                eng.drain()
                wait('TS', ts_wtr)
                eng.tensor_copy(w_bfT[:], w_tm[:, tc, :])
                eng.drain().then_inc(sems['VS'], 1)
            pub('VS'); vs_wprev = cnt['VS']
            if me('tensor'):
                wait('VS', vs_wprev)
                eng.transpose(wf_ps[:, tc * 128:(tc + 1) * 128], w_bfT[:],
                              identb[:]).then_inc(sems['TS'], 1)
            pub('TS'); ts_wtr = cnt['TS']
        if me('vector'):
            wait('TS', ts_wtr)
            eng.tensor_copy(w_fm.ap().rearrange("p a c -> p (a c)"), wf_ps[:])
            eng.drain().then_inc(sems['VS'], 1)
        pub('VS'); VS_route = cnt['VS']

        vs_pass_done = 0
        vs_yend_prev = 0
        ts_y_last = 0
        gd_w1 = {}
        w1_free = {}
        gd_w2 = {}
        w2_free = {}
        w1_ld = 0
        w2_ld = 0
        sd_fin = {}
        vs_fin = {}
        for p in range(2):
            toff = p * TT
            tcoff = p * NTC
            vs_b2 = {}
            b2_state = dict(done=False, vs_yinit=0)

            def emit_b2init():
                for i in range(NTC * 2):
                    tc, dcl = i // 2, i % 2
                    yb = y_ps.ap()[:, i % 2, :]
                    if me('tensor'):
                        if i == 0:
                            wait('VS', VS_route)
                            if p == 1:
                                wait('VS', vs_yend_prev)
                        if i >= 2:
                            wait('VS', vs_b2[i - 2])
                        eng.matmul(yb, w_fm[:, tcoff + tc, :],
                                   b2_sb[:, dcl * 512:(dcl + 1) * 512],
                                   start=True, stop=True).then_inc(sems['TS'], 1)
                    pub('TS'); ts_b2 = cnt['TS']
                    if me('vector'):
                        wait('TS', ts_b2)
                        if i == 0 and p == 1:
                            wait('VS', vs_pass_done)
                        eng.tensor_copy(y_acc[:, tc, dcl * 512:(dcl + 1) * 512], yb)
                        eng.drain().then_inc(sems['VS'], 1)
                    pub('VS'); vs_b2[i] = cnt['VS']
                b2_state['done'] = True
                b2_state['vs_yinit'] = cnt['VS']

            vs_yevac = {}
            iy_g = 0
            ss_gelu = {}
            ih_g = 0
            for e in range(E):
                for fb in range(2):
                    fcol0 = fb * FB
                    slot_ids = []
                    for s in range(NS1):
                        slot = w1_ld % 2
                        slot_ids.append(slot)
                        if me('gpsimd'):
                            if w1_ld >= 2:
                                wait('TS', w1_free[w1_ld - 2])
                            else:
                                wait('TS', TS_p3)
                            eng.dma_start(
                                out=w1b[:, slot, :, :],
                                in_=w1t.rearrange("e (a p) f -> e p a f", p=128)[
                                    e, :, :, fcol0 + s * W1S * 128: fcol0 + (s + 1) * W1S * 128]
                            ).then_inc(sems['GW1%d' % slot], 16)
                        pub('GW1%d' % slot, 16)
                        gd_w1[w1_ld] = ('GW1%d' % slot, cnt['GW1%d' % slot])
                        w1_ld += 1
                    first_h = True
                    for fi in range(NF):
                        s_idx = fi // W1S
                        for th in range(NTH):
                            hb = h_ps.ap()[:, ih_g % 2, :]
                            if me('tensor'):
                                wait(*gd_w1[w1_ld - NS1 + s_idx])
                                if ih_g >= 2:
                                    wait('SS', ss_gelu[ih_g - 2])
                                for dj in range(DC):
                                    inst = eng.matmul(
                                        hb[:, 0:H512], w1b[:, slot_ids[s_idx], dj,
                                                (fi % W1S) * 128:(fi % W1S) * 128 + 128],
                                        xln2_fm[:, dj, toff + th * H512: toff + (th + 1) * H512],
                                        start=(dj == 0), stop=(dj == DC - 1))
                                inst.then_inc(sems['TS'], 1)
                            pub('TS'); ts_h = cnt['TS']
                            if fi % W1S == W1S - 1 and th == NTH - 1:
                                w1_free[w1_ld - NS1 + s_idx] = ts_h
                            if me('scalar'):
                                wait('TS', ts_h)
                                if first_h and ts_y_last:
                                    wait('TS', ts_y_last)
                                eng.activation(
                                    g_sb[:, fi, th * H512:(th + 1) * H512], hb[:, 0:H512],
                                    AF.Gelu_apprx_tanh,
                                    bias=b1_sb.ap()[:, e, fb * NF + fi:fb * NF + fi + 1])
                                first_h = False
                                eng.drain().then_inc(sems['SS'], 1)
                            pub('SS'); ss_gelu[ih_g] = cnt['SS']
                            ih_g += 1
                    ss_gelu_end = cnt['SS']
                    if not b2_state['done']:
                        emit_b2init()
                    for dcl in range(2):
                        slot2 = w2_ld % 2
                        if me('gpsimd'):
                            if w2_ld >= 2:
                                wait('TS', w2_free[w2_ld - 2])
                            else:
                                wait('TS', TS_p3)
                            eng.dma_start(
                                out=w2b[:, slot2, :, :],
                                in_=w2t.rearrange("e (a p) d -> e p a d", p=128)[
                                    e, :, fb * NF:(fb + 1) * NF,
                                    dcl * 512:(dcl + 1) * 512]
                            ).then_inc(sems['GW2%d' % slot2], 16)
                        pub('GW2%d' % slot2, 16)
                        gd_w2[w2_ld] = ('GW2%d' % slot2, cnt['GW2%d' % slot2])
                        for tc in range(NTC):
                            yb = y_ps.ap()[:, iy_g % 2, :]
                            if me('tensor'):
                                wait(*gd_w2[w2_ld])
                                if tc == 0 and dcl == 0:
                                    wait('SS', ss_gelu_end)
                                    wait('VS', b2_state['vs_yinit'])
                                wait('VS', vs_yevac.get(iy_g - 2, 0))
                                for fi in range(NF):
                                    inst = eng.matmul(
                                        yb, g_sb[:, fi, tc * 128:(tc + 1) * 128],
                                        w2b[:, slot2, fi, :],
                                        start=(fi == 0), stop=(fi == NF - 1))
                                inst.then_inc(sems['TS'], 1)
                            pub('TS'); ts_y = cnt['TS']
                            if tc == NTC - 1:
                                w2_free[w2_ld] = ts_y
                            if tc == NTC - 1 and dcl == 1:
                                ts_y_last = ts_y
                            if me('vector'):
                                wait('TS', ts_y)
                                eng.scalar_tensor_tensor(
                                    y_acc[:, tc, dcl * 512:(dcl + 1) * 512], yb,
                                    w_tm.ap()[:, tcoff + tc, e:e + 1],
                                    y_acc[:, tc, dcl * 512:(dcl + 1) * 512],
                                    ALU.mult, ALU.add)
                                eng.drain().then_inc(sems['VS'], 1)
                            pub('VS'); vs_yevac[iy_g] = cnt['VS']
                            iy_g += 1
                        w2_ld += 1
            vs_y_end = cnt['VS']

            for tc in range(NTC):
                fidx = p * NTC + tc
                sl = fidx % 2
                if me('sync'):
                    if fidx >= 2:
                        wait('VS', vs_fin[fidx - 2])
                    else:
                        wait('VS', VS_p3)
                    wait('SR', 16 * NCHT)
                    eng.dma_start(out=xres_t[sl][:],
                                  in_=xres_hbm[toff + tc * 128: toff + (tc + 1) * 128, :]
                                  ).then_inc(sems['SX%d' % sl], 16)
                pub('SX%d' % sl, 16); sd_f = cnt['SX%d' % sl]
                if me('vector'):
                    wait('SX%d' % sl, sd_f)
                    wait('VS', vs_y_end)
                    if fidx >= 2:
                        wait(*sd_fin[fidx - 2])
                    eng.tensor_tensor(outb[sl][:], y_acc[:, tc, :], xres_t[sl][:], ALU.add)
                    eng.drain().then_inc(sems['VS'], 1)
                pub('VS'); vs_fin[fidx] = cnt['VS']
                if me('sync'):
                    wait('VS', vs_fin[fidx])
                    eng.dma_start(out=out[toff + tc * 128: toff + (tc + 1) * 128, :],
                                  in_=outb[sl][:]).then_inc(sems['SO%d' % sl], 16)
                pub('SO%d' % sl, 16); sd_fin[fidx] = ('SO%d' % sl, cnt['SO%d' % sl])
            vs_pass_done = cnt['VS']
            vs_yend_prev = vs_y_end

        if me('sync'):
            for n in ["SDP", "SR", "SX0", "SX1", "SO0", "SO1"]:
                wait(n, cnt[n])
        if me('gpsimd'):
            for n in ["GDP", "GW10", "GW11", "GW20", "GW21", "CCI", "CCO"]:
                wait(n, cnt[n])
        return cnt

    @block.sync
    def _(e):
        stream('sync', e)

    @block.gpsimd
    def _(e):
        stream('gpsimd', e)

    @block.tensor
    def _(e):
        stream('tensor', e)

    @block.vector
    def _(e):
        stream('vector', e)

    @block.scalar
    def _(e):
        stream('scalar', e)

    blk.__exit__(None, None, None)
    return nc


def make_core_inputs(x_flat, n_cores, T, P, WwT, WrT, gateT, dcol, lacol,
                     w1t_a, w2t_a, b1r, b2_a, seq=None):
    Ttot = x_flat.shape[0]
    if seq is None:
        seq = Ttot // 2 if n_cores > 1 else Ttot
    xT = np.ascontiguousarray(x_flat.T)
    maps = []
    for c in range(n_cores):
        t0 = c * T
        bs = (t0 // seq) * seq
        p_c = t0 - bs
        xTm = np.zeros((D, T + 1), np.float32)
        xTm[:, 1:] = xT[:, t0:t0 + T]
        if p_c:
            xTm[:, 0] = xT[:, t0 - 1]
        pos_c = (t0 - bs) // T
        gap = np.zeros((128, 8), np.float32)
        madd = np.full((128, 8), -1e30, np.float32)
        grp = (4 if n_cores == 8 else n_cores)
        for r in range(n_cores):
            r0 = r * T
            rbs = (r0 // seq) * seq
            if rbs == bs and r0 < t0:
                pos_r = (r0 - rbs) // T
                gap[:, r % grp] = float(T * (pos_c - pos_r - 1))
                madd[:, r % grp] = 0.0
        maps.append(dict(
            xTm=xTm, x_tm=np.ascontiguousarray(x_flat[t0:t0 + T]),
            gapcol=gap, maskadd=madd,
            WwT=WwT, WrT=WrT, gateT=gateT, dcol=dcol, lacol=lacol,
            w1t=w1t_a, w2t=w2t_a, b1r=b1r, b2=b2_a))
    return maps


def kernel(L0_attn_out, L0_attn_W_write, L0_attn_W_read, L0_attn_decay,
           L0_attn_log_alpha, L0_ffn_gate_w, L0_ffn_w1, L0_ffn_b1,
           L0_ffn_w2, L0_ffn_b2):
    x = np.ascontiguousarray(np.asarray(L0_attn_out, np.float32).reshape(-1, D))
    WwT = np.ascontiguousarray(np.asarray(L0_attn_W_write, np.float32).T)
    WrT = np.ascontiguousarray(np.asarray(L0_attn_W_read, np.float32).T)
    gT = np.ascontiguousarray(np.asarray(L0_ffn_gate_w, np.float32).T)
    dcol = np.ascontiguousarray(np.broadcast_to(
        np.asarray(L0_attn_decay, np.float32), (128, H)))
    lacol = np.ascontiguousarray(np.broadcast_to(
        np.asarray(L0_attn_log_alpha, np.float32), (128, H)))
    w1 = np.asarray(L0_ffn_w1, np.float32)
    w2 = np.asarray(L0_ffn_w2, np.float32)
    F_ = w1.shape[0]
    w1t_a = np.ascontiguousarray(w1.transpose(1, 2, 0))
    w2t_a = np.ascontiguousarray(w2.transpose(1, 2, 0))
    b1 = np.asarray(L0_ffn_b1, np.float32)
    b1r = np.ascontiguousarray(b1.T.reshape(F_ // 128, 128, E).transpose(1, 2, 0))
    b2_a = np.ascontiguousarray(np.asarray(L0_ffn_b2, np.float32))

    T, P = 2048, 6144
    nc = build_program(T=T, P=P, F=F_)
    maps = make_core_inputs(x, 8, T, P, WwT, WrT, gT, dcol, lacol,
                            w1t_a, w2t_a, b1r, b2_a)
    res = run_bass_kernel_spmd(nc, maps, core_ids=list(range(8)))
    outp = np.concatenate([r["out"] for r in res.results], axis=0)
    return outp.reshape(np.asarray(L0_attn_out).shape).astype(np.float32)


# version 18
# speedup vs baseline: 1.0205x; 1.0205x over previous
import numpy as np
import concourse.bass as bass
import concourse.mybir as mybir
from concourse.bass_utils import run_bass_kernel_spmd

F32 = mybir.dt.float32
BF16 = mybir.dt.bfloat16
AF = mybir.ActivationFunctionType
ALU = mybir.AluOpType
AX = mybir.AxisListType.X

D, E, H, C, DH, DC = 1024, 8, 16, 128, 64, 8


def build_program(T=2048, P=6144, F=4096, LB=512, n_cores=8):
    assert T % LB == 0 and P % LB == 0 and LB % C == 0 and F % 512 == 0
    NBP, NBM = P // LB, T // LB
    CPB = LB // C
    NCHT = T // C
    TT = T // 2
    NTC = TT // 128
    FB = F // 2
    NF = FB // 128
    W1S = max(NF // 2, 1)
    NS1 = NF // W1S
    NTH = max(TT // 512, 1)
    H512 = TT // NTH

    nc = bass.Bass("TRN2", target_bir_lowering=False)
    xTm = nc.dram_tensor("xTm", [D, T + 1], F32, kind="ExternalInput").ap()
    x_tm = nc.dram_tensor("x_tm", [T, D], F32, kind="ExternalInput").ap()
    gapcol = nc.dram_tensor("gapcol", [128, 8], F32, kind="ExternalInput").ap()
    maskadd = nc.dram_tensor("maskadd", [128, 8], F32, kind="ExternalInput").ap()
    cc_in = nc.dram_tensor("cc_in", [128, DC * DH], F32, kind="Internal").ap()
    NGRP = 4 if n_cores == 8 else n_cores
    cc_out = nc.dram_tensor("cc_out", [NGRP * 128, DC * DH], F32, kind="Internal").ap()
    WwT = nc.dram_tensor("WwT", [D, D], F32, kind="ExternalInput").ap()
    WrT = nc.dram_tensor("WrT", [D, D], F32, kind="ExternalInput").ap()
    gateT = nc.dram_tensor("gateT", [D, E], F32, kind="ExternalInput").ap()
    dcol = nc.dram_tensor("dcol", [128, H], F32, kind="ExternalInput").ap()
    lacol = nc.dram_tensor("lacol", [128, H], F32, kind="ExternalInput").ap()
    w1t = nc.dram_tensor("w1t", [E, D, F], F32, kind="ExternalInput").ap()
    w2t = nc.dram_tensor("w2t", [E, F, D], F32, kind="ExternalInput").ap()
    b1r = nc.dram_tensor("b1r", [128, E, F // 128], F32, kind="ExternalInput").ap()
    b2 = nc.dram_tensor("b2", [E, D], F32, kind="ExternalInput").ap()
    out = nc.dram_tensor("out", [T, D], F32, kind="ExternalOutput").ap()
    xres_hbm = nc.dram_tensor("xres_hbm", [T, D], F32, kind="Internal").ap()

    def sb(name, shape, dt):
        g = nc.sbuf_tensor(name, shape, dt)
        return g, g.__enter__()

    def sbp(name, shape, dt):
        return sb(name, shape, dt)[1]

    identb = sbp("identb", [128, 128], BF16)
    w_tm = sbp("w_tm", [128, 2 * NTC, E], F32)
    w_bfT = sbp("w_bfT", [128, E], BF16)
    w_fm = sbp("w_fm", [E, 2 * NTC, 128], BF16)
    m0 = sbp("m0", [128, 1], F32)
    m1c = sbp("m1c", [128, 1], F32)
    m2c = sbp("m2c", [128, 1], F32)
    sm_t = sbp("sm_t", [128, E], F32)
    b1_sb = sbp("b1_sb", [128, E, F // 128], F32)
    b2_sb = sbp("b2_sb", [E, D], BF16)
    gateT_sb = sbp("gateT_sb", [128, DC, E], BF16)
    xres_t = [sbp(f"xres_t{i}", [128, D], F32) for i in range(2)]
    xl2tm = [sbp(f"xl2tm{i}", [128, D], BF16) for i in range(2)]
    bn6 = sbp("bn6", [128, 12], F32)
    mv2 = sbp("mv2", [128, 2], F32)
    outb = [sbp(f"outb{i}", [128, D], F32) for i in range(2)]

    gA = []

    def sba(name, shape, dt):
        g, t = sb(name, shape, dt)
        gA.append(g)
        return t

    causal = sba("causal", [128, 128], F32)
    iotaD = sba("iotaD", [128, 128], F32)
    iotaC = sba("iotaC", [128, 128], F32)
    iota_rev = sba("iota_rev", [128, 1], F32)
    ones_bf = sba("ones_bf", [128, 1], BF16)
    onesb_row = sba("onesb_row", [1, 128], BF16)
    dcol_sb = sba("dcol_sb", [128, H], F32)
    lacol_sb = sba("lacol_sb", [128, H], F32)
    lg_cols = sba("lg_cols", [128, H], F32)
    al_cols = sba("al_cols", [128, H], F32)
    lgsel = sba("lgsel", [128, DC], F32)
    lgfull = sba("lgfull", [128, H, DH], F32)
    gCst = sba("gCst", [128, DC, DH], F32)
    gwfull = sba("gwfull", [128, H, DH], BF16)
    MT = sba("MT", [128, 2, DC, 128], BF16)
    gw_tmp = sba("gw_tmp", [128, H, DH], F32)
    gp_tmp = sba("gp_tmp", [128, DC, 128], F32)
    lgst = sba("lgst", [128, DC, DH], F32)
    gth_sb = sba("gth_sb", [128, 8, DC * DH], F32)
    v_tm = sba("v_tm", [128, DC, 128], BF16)
    wk_tm = sba("wk_tm", [128, DC, 128], BF16)
    stil = [sba(f"stil{g}", [128, 4, 128], BF16) for g in range(4)]
    mt_tmp = sba("mt_tmp", [128, H // 2, 128], F32)
    gapc_sb = sba("gapc_sb", [128, 8], F32)
    maskc_sb = sba("maskc_sb", [128, 8], F32)
    arow = sba("arow", [1, LB], BF16)
    brow = sba("brow", [1, LB], BF16)
    a_sb = sba("a_sb", [128, LB], BF16)
    b_sb = sba("b_sb", [128, LB], BF16)
    sq_sb = sba("sq_sb", [128, DC, LB], BF16)
    r0 = sba("r0", [1, LB], F32)
    r1 = sba("r1", [1, LB], F32)
    r2 = sba("r2", [1, LB], F32)
    WwT_sb = sba("WwT_sb", [128, DC, D], BF16)
    al_pair = sba("al_pair", [128, DC], F32)
    gp_all = sba("gp_all", [128, DC, 128], BF16)
    gCmk = sba("gCmk", [128, DC, 16], F32)
    stateT = sba("stateT", [128, DC, DH], F32)
    stateT_bf = sba("stateT_bf", [128, H, DH], BF16)
    vgw = sba("vgw", [128, DC, 128], BF16)
    rkgp = sba("rkgp", [128, DC, 128], BF16)
    padA = sba("padA", [128, 1024], BF16)
    xln_m = sba("xln_m", [128, DC, T + 8], BF16)
    reads_fm = sba("reads_fm", [128, DC, T], BF16)
    WrT_sb = sba("WrT_sb", [128, DC, D], BF16)
    for g in reversed(gA):
        g.__exit__(None, None, None)
    gB = []
    gg, xln2_fm = sb("xln2_fm", [128, DC, T], BF16); gB.append(gg)
    gg, g_sb = sb("g_sb", [128, NF, TT], BF16); gB.append(gg)
    gg, w1b = sb("w1b", [128, 2, DC, W1S * 128], BF16); gB.append(gg)
    gg, w2b = sb("w2b", [128, 2, NF, 512], BF16); gB.append(gg)
    gg, y_acc = sb("y_acc", [128, NTC, D], F32); gB.append(gg)
    for g in reversed(gB):
        g.__exit__(None, None, None)

    def psm(name, shape, dt):
        g = nc.psum_tensor(name, shape, dt)
        return g, g.__enter__()

    gPA = []
    gg, v_ps = psm("v_ps", [128, 1024], F32); gPA.append(gg)
    gg, wk_ps = psm("wk_ps", [128, 1024], BF16); gPA.append(gg)
    gg, delta_ps = psm("delta_ps", [128, 512], F32); gPA.append(gg)
    gLN = []
    gg, stx_ps = psm("stx_ps", [1, 512], F32); gLN.append(gg)
    gg, sts_ps = psm("sts_ps", [1, 512], F32); gLN.append(gg)
    gg, bca_ps = psm("bca_ps", [128, 512], F32); gLN.append(gg)
    gg, bcb_ps = psm("bcb_ps", [128, 512], F32); gLN.append(gg)
    for g in reversed(gLN):
        g.__exit__(None, None, None)
    gMS = []
    gg, st_pse = psm("st_pse", [128, 512], F32); gMS.append(gg)
    gg, st_pso = psm("st_pso", [128, 512], F32); gMS.append(gg)
    gg, reads_ps = psm("reads_ps", [128, 1024], F32); gMS.append(gg)
    for g in reversed(gMS):
        g.__exit__(None, None, None)
    for g in reversed(gPA):
        g.__exit__(None, None, None)
    gP3 = []
    gg, op_ps = psm("op_ps", [128, 2, 1024], F32); gP3.append(gg)
    gg, tr_ps = psm("tr_ps", [128, 2, 1024], BF16); gP3.append(gg)
    for g in reversed(gP3):
        g.__exit__(None, None, None)
    gP4 = []
    gg, rt_ps = psm("rt_ps", [128, 512], F32); gP4.append(gg)
    gg, wf_ps_full = psm("wf_ps", [128, 2 * NTC * 128], BF16); gP4.append(gg)
    wf_ps = wf_ps_full.ap()[0:8, :]
    gg, y_ps = psm("y_ps", [128, 2, 512], F32); gP4.append(gg)
    _wf_banks = (2 * NTC * 128 * 2 + 2047) // 2048
    _pad_banks = 6 - (1 + _wf_banks + 2)
    assert _pad_banks >= 0
    if _pad_banks:
        gg, pad_ps = psm("pad_ps", [128, _pad_banks * 512], F32); gP4.append(gg)
    gg, h_ps = psm("h_ps", [128, 2, 512], F32); gP4.append(gg)
    for g in reversed(gP4):
        g.__exit__(None, None, None)

    sems = {}
    for n in ["TS", "VS", "SS", "GC", "SDP", "GDP", "SR",
              "GX0", "GX1", "SX0", "SX1", "SO0", "SO1",
              "GW10", "GW11", "GW20", "GW21", "CCI", "CCS", "CCO"]:
        g = nc.semaphore(n)
        sems[n] = g.__enter__()
    blk = nc.Block()
    block = blk.__enter__()

    def stream(which, eng):
        cnt = {k: 0 for k in sems}

        def pub(name, n=1):
            cnt[name] += n
            return cnt[name]

        def wait(name, val):
            if eng is not None and val > 0:
                eng.wait_ge(sems[name], val)

        def me(o):
            return which == o

        if me('gpsimd'):
            eng.memset(ones_bf[:], 1.0)
            eng.memset(onesb_row[:], 1.0)
            eng.iota(iotaD[:], pattern=[[1, 128]], base=-1, channel_multiplier=-1,
                     allow_small_or_imprecise_dtypes=True)
            eng.drain()
            eng.tensor_scalar_max(iotaD[:], iotaD[:], 0.0)
            eng.iota(iotaC[:], pattern=[[1, 128]], base=0, channel_multiplier=0,
                     allow_small_or_imprecise_dtypes=True)
            eng.iota(iota_rev[:], pattern=[[0, 1]], base=C - 1, channel_multiplier=-1,
                     allow_small_or_imprecise_dtypes=True)
            eng.memset(causal[:], 0.0)
            eng.memset(identb[:], 0.0)
            eng.memset(xln_m[:, :, T + 1:T + 8], 0.0)
            eng.drain()
            eng.affine_select(out=causal[:], in_=causal[:], compare_op=ALU.is_ge,
                              fill=1.0, base=0, pattern=[[-1, 128]], channel_multiplier=1)
            eng.affine_select(out=identb[:], in_=identb[:], compare_op=ALU.not_equal,
                              fill=1.0, base=0, pattern=[[-1, 128]], channel_multiplier=1)
            eng.drain().then_inc(sems['GC'], 1)
        pub('GC')
        GC_c = cnt['GC']

        if me('gpsimd'):
            eng.dma_start(out=xln_m[:, :, 0:T + 1],
                          in_=xTm.rearrange("(a p) t -> p a t", p=128)).then_inc(sems['GDP'], 16)
            eng.dma_start(out=WwT_sb[:],
                          in_=WwT.rearrange("(a p) i -> p a i", p=128)).then_inc(sems['GDP'], 16)
            eng.dma_start(out=WrT_sb[:],
                          in_=WrT.rearrange("(a p) i -> p a i", p=128)).then_inc(sems['GDP'], 16)
            eng.dma_start(out=gateT_sb[:],
                          in_=gateT.rearrange("(a p) e -> p a e", p=128)).then_inc(sems['GDP'], 16)
            eng.dma_start(out=b2_sb[:], in_=b2).then_inc(sems['GDP'], 16)
        pub('GDP', 16 * 5)
        GD_pre = cnt['GDP']

        if me('sync'):
            eng.dma_start(out=dcol_sb[:], in_=dcol).then_inc(sems['SDP'], 16)
            eng.dma_start(out=lacol_sb[:], in_=lacol).then_inc(sems['SDP'], 16)
            eng.dma_start(out=b1_sb[:], in_=b1r).then_inc(sems['SDP'], 16)
            eng.dma_start(out=gapc_sb[:], in_=gapcol).then_inc(sems['SDP'], 16)
            eng.dma_start(out=maskc_sb[:], in_=maskadd).then_inc(sems['SDP'], 16)
        pub('SDP', 16 * 5)
        SD_pre = cnt['SDP']

        if me('scalar'):
            wait('SDP', SD_pre)
            eng.activation(lg_cols[:], dcol_sb[:], AF.Sigmoid)
            eng.drain()
            eng.activation(lg_cols[:], lg_cols[:], AF.Ln)
            eng.activation(al_cols[:], lacol_sb[:], AF.Exp)
            eng.drain().then_inc(sems['SS'], 1)
        pub('SS')
        SS_rows = cnt['SS']

        if me('vector'):
            wait('SS', SS_rows)
            wait('GC', GC_c)
            eng.tensor_copy(lgfull[:], lg_cols.ap().unsqueeze(2).broadcast_to((128, H, DH)))
            eng.drain()
            eng.tensor_copy(al_pair[0:64, :], al_cols[0:64, 0:H:2])
            eng.tensor_copy(al_pair[64:128, :], al_cols[64:128, 1:H:2])
            eng.tensor_copy(lgsel[0:64, :], lg_cols[0:64, 0:H:2])
            eng.tensor_copy(lgsel[64:128, :], lg_cols[64:128, 1:H:2])
            eng.tensor_copy(lgst[0:64, :, :], lgfull[0:64, 0:H:2, :])
            eng.tensor_copy(lgst[64:128, :, :], lgfull[64:128, 1:H:2, :])
            eng.drain()
            for h in range(H // 2):
                eng.tensor_scalar(mt_tmp[:, h, :], iotaD[:], lg_cols[:, h:h + 1], None, ALU.mult)
            eng.tensor_tensor(gp_tmp[:], iotaC.ap().unsqueeze(1).broadcast_to((128, DC, 128)),
                              lgsel.ap().unsqueeze(2).broadcast_to((128, DC, 128)), ALU.mult)
            eng.tensor_scalar(gw_tmp[:], lgfull[:], iota_rev[:, 0:1], None, ALU.mult)
            eng.tensor_tensor(gCmk[:], lgsel.ap().unsqueeze(2).broadcast_to((128, DC, 16)),
                              iotaC.ap()[:, 0:16].unsqueeze(1).broadcast_to((128, DC, 16)),
                              ALU.mult)
            eng.drain().then_inc(sems['VS'], 1)
            eng.memset(stateT[:], 0.0)
            eng.memset(stateT_bf[:], 0.0)
            eng.drain().then_inc(sems['VS'], 1)
        pub('VS'); VS_pre1 = cnt['VS']
        pub('VS'); VS_state0 = cnt['VS']

        if me('scalar'):
            wait('VS', VS_pre1)
            eng.activation(mt_tmp[:], mt_tmp[:], AF.Exp)
            eng.activation(gp_all.ap(), gp_tmp[:], AF.Exp)
            eng.activation(gwfull.ap(), gw_tmp[:], AF.Exp)
            eng.activation(gCmk[:], gCmk[:], AF.Exp, scale=float(C))
            eng.activation(gCst[0:64, :, :], lgfull[0:64, 0:H:2, :], AF.Exp, scale=float(C))
            eng.activation(gCst[64:128, :, :], lgfull[64:128, 1:H:2, :], AF.Exp, scale=float(C))
            eng.drain().then_inc(sems['SS'], 1)
        pub('SS'); SS_exps = cnt['SS']

        if me('vector'):
            wait('SS', SS_exps)
            eng.tensor_tensor(MT.ap()[:, :, 0:4, :].transpose([0, 2, 1, 3]),
                              mt_tmp.ap().rearrange("p (a b) c -> p a b c", b=2),
                              causal.ap().unsqueeze(1).unsqueeze(1).broadcast_to((128, 4, 2, 128)),
                              ALU.mult)
            eng.drain()
            for h in range(H // 2, H):
                eng.tensor_scalar(mt_tmp[:, h - H // 2, :], iotaD[:],
                                  lg_cols[:, h:h + 1], None, ALU.mult)
            eng.drain().then_inc(sems['VS'], 1)
        pub('VS'); VS_mt2 = cnt['VS']
        if me('scalar'):
            wait('VS', VS_mt2)
            eng.activation(mt_tmp[:], mt_tmp[:], AF.Exp)
            eng.drain().then_inc(sems['SS'], 1)
        pub('SS'); SS_mt2 = cnt['SS']
        if me('vector'):
            wait('SS', SS_mt2)
            eng.tensor_tensor(MT.ap()[:, :, 4:8, :].transpose([0, 2, 1, 3]),
                              mt_tmp.ap().rearrange("p (a b) c -> p a b c", b=2),
                              causal.ap().unsqueeze(1).unsqueeze(1).broadcast_to((128, 4, 2, 128)),
                              ALU.mult)
            eng.drain().then_inc(sems['VS'], 1)
        pub('VS'); VS_MT = cnt['VS']

        lnv = dict(rows=0, bc=0, sqfree=0)

        def ln_block(buf, c0, n, dep=None):
            if me('scalar'):
                if dep is not None:
                    wait(dep[0], dep[1])
                wait('TS', lnv['sqfree'])
                for dj in range(DC):
                    eng.activation(sq_sb[:, dj, 0:n], buf[:, dj, c0:c0 + n], AF.Square)
                eng.drain().then_inc(sems['SS'], 1)
            pub('SS'); ss_sq = cnt['SS']
            if me('tensor'):
                if dep is not None:
                    wait(dep[0], dep[1])
                wait('SS', ss_sq)
                wait('VS', lnv['rows'])
                for dj in range(DC):
                    eng.matmul(stx_ps[:, 0:n], ones_bf[:], buf[:, dj, c0:c0 + n],
                               start=(dj == 0), stop=(dj == DC - 1))
                for dj in range(DC):
                    inst = eng.matmul(sts_ps[:, 0:n], ones_bf[:], sq_sb[:, dj, 0:n],
                                      start=(dj == 0), stop=(dj == DC - 1))
                inst.then_inc(sems['TS'], 1)
            pub('TS'); ts_st = cnt['TS']; lnv['sqfree'] = ts_st
            if me('vector'):
                wait('TS', ts_st)
                eng.tensor_scalar(r0[:, 0:n], stx_ps[:, 0:n], 1.0 / D, None, ALU.mult)
                eng.tensor_scalar(r1[:, 0:n], sts_ps[:, 0:n], 1.0 / D, 1e-5, ALU.mult, ALU.add)
                eng.drain()
                eng.tensor_tensor(r2[:, 0:n], r0[:, 0:n], r0[:, 0:n], ALU.mult)
                eng.drain()
                eng.tensor_tensor(r1[:, 0:n], r1[:, 0:n], r2[:, 0:n], ALU.subtract)
                eng.drain().then_inc(sems['VS'], 1)
            pub('VS'); vs_var = cnt['VS']; lnv['rows'] = vs_var
            if me('scalar'):
                wait('VS', vs_var)
                eng.activation(r1[:, 0:n], r1[:, 0:n], AF.Ln)
                eng.drain()
                eng.activation(r1[:, 0:n], r1[:, 0:n], AF.Exp, scale=-0.5)
                eng.drain().then_inc(sems['SS'], 1)
            pub('SS'); ss_rs = cnt['SS']
            if me('vector'):
                wait('SS', ss_rs)
                eng.tensor_copy(arow[:, 0:n], r1[:, 0:n])
                eng.tensor_tensor(r2[:, 0:n], r0[:, 0:n], r1[:, 0:n], ALU.mult)
                eng.drain()
                eng.tensor_scalar(brow[:, 0:n], r2[:, 0:n], -1.0, None, ALU.mult)
                eng.drain().then_inc(sems['VS'], 1)
            pub('VS'); vs_ab = cnt['VS']
            if me('tensor'):
                wait('VS', vs_ab)
                wait('VS', lnv['bc'])
                eng.matmul(bca_ps[:, 0:n], onesb_row[:], arow[:, 0:n], start=True, stop=True)
                inst = eng.matmul(bcb_ps[:, 0:n], onesb_row[:], brow[:, 0:n], start=True, stop=True)
                inst.then_inc(sems['TS'], 1)
            pub('TS'); ts_bc = cnt['TS']
            if me('vector'):
                wait('TS', ts_bc)
                eng.tensor_copy(a_sb[:, 0:n], bca_ps[:, 0:n])
                eng.tensor_copy(b_sb[:, 0:n], bcb_ps[:, 0:n])
                eng.drain()
                for dj in range(DC):
                    eng.tensor_tensor(buf[:, dj, c0:c0 + n], buf[:, dj, c0:c0 + n],
                                      a_sb[:, 0:n], ALU.mult)
                eng.drain()
                for dj in range(DC):
                    eng.tensor_tensor(buf[:, dj, c0:c0 + n], buf[:, dj, c0:c0 + n],
                                      b_sb[:, 0:n], ALU.add)
                eng.drain().then_inc(sems['VS'], 1)
            pub('VS'); lnv['bc'] = cnt['VS']
            return cnt['VS']

        scn = dict(vA=0, vA2=0, tRD=0, tID=0, vST={}, vRD=0, vstate=VS_state0, t0=0)

        def scan_chunk(buf, wk0, full, cast, pre_vs=0, ps_alt=False):
            if ps_alt:
                vps = reads_ps
                wps = st_pso.ap().bitcast(BF16)
            else:
                vps = v_ps
                wps = wk_ps.ap()
            if me('tensor'):
                wait('VS', scn['vA2'] if ps_alt else scn['vA'])
                wait('GDP', GD_pre)
                if pre_vs:
                    wait('VS', pre_vs)
                for ih in range(2):
                    for dj in range(DC):
                        eng.matmul(vps[:, ih * 512:(ih + 1) * 512],
                                   buf[:, dj, wk0 + 1:wk0 + 1 + C],
                                   WwT_sb[:, dj, ih * 512:(ih + 1) * 512],
                                   start=(dj == 0), stop=(dj == DC - 1))
                for dc in range(DC):
                    inst = eng.transpose(
                        wps.rearrange("p (a c) -> p a c", a=DC)[:, dc, :],
                        buf[:, dc, wk0:wk0 + C], identb[:])
                inst.then_inc(sems['TS'], 1)
            pub('TS'); tA = cnt['TS']
            if me('vector'):
                wait('TS', tA)
                if scn['tRD']:
                    wait('TS', scn['tRD'])
                eng.tensor_copy(v_tm.ap().rearrange("p a c -> p (a c)"), vps[:])
                eng.tensor_copy(wk_tm.ap().rearrange("p a c -> p (a c)"), wps)
                eng.tensor_tensor(vgw.ap().rearrange("p a c -> p (a c)"),
                                  vps[:],
                                  gwfull.ap().rearrange("p h i -> p (h i)"), ALU.mult)
                eng.drain().then_inc(sems['VS'], 1)
            pub('VS')
            if ps_alt:
                scn['vA2'] = cnt['VS']
            else:
                scn['vA'] = cnt['VS']
            vA = cnt['VS']

            if full:
                for g in range(4):
                    heads = [4 * g + j for j in range(4)]
                    if me('tensor'):
                        wait('VS', scn['vST'].get(g - 2 if g >= 2 else g + 2, 0))
                        if g < 2:
                            wait('VS', VS_MT)
                        for j, h in enumerate(heads):
                            par, dcb = h % 2, h // 2
                            slot = (g % 2) * 256 + (j // 2) * 128
                            bank = st_pse if par == 0 else st_pso
                            inst = eng.matmul(bank[:, slot:slot + 128],
                                              buf[par * 64:par * 64 + 64, dcb, wk0:wk0 + C],
                                              buf[par * 64:par * 64 + 64, dcb, wk0 + 1:wk0 + 1 + C],
                                              start=True, stop=True)
                        inst.then_inc(sems['TS'], 1)
                    pub('TS'); tst = cnt['TS']
                    if me('vector'):
                        wait('TS', tst)
                        sl0 = (g % 2) * 256
                        eng.tensor_tensor(
                            stil[g].ap()[:, 0:2, :],
                            st_pse.ap()[:, sl0:sl0 + 256].rearrange("p (a c) -> p a c", a=2),
                            MT[:, 0, 2 * g:2 * g + 2, :], ALU.mult)
                        eng.tensor_tensor(
                            stil[g].ap()[:, 2:4, :],
                            st_pso.ap()[:, sl0:sl0 + 256].rearrange("p (a c) -> p a c", a=2),
                            MT[:, 1, 2 * g:2 * g + 2, :], ALU.mult)
                        eng.tensor_tensor(rkgp[0:64, 2 * g:2 * g + 2, :],
                                          buf[0:64, 2 * g:2 * g + 2, wk0 + 1:wk0 + 1 + C],
                                          gp_all[0:64, 2 * g:2 * g + 2, :], ALU.mult)
                        eng.tensor_tensor(rkgp[64:128, 2 * g:2 * g + 2, :],
                                          buf[64:128, 2 * g:2 * g + 2, wk0 + 1:wk0 + 1 + C],
                                          gp_all[64:128, 2 * g:2 * g + 2, :], ALU.mult)
                        eng.drain().then_inc(sems['VS'], 1)
                    pub('VS'); scn['vST'][g] = cnt['VS']
                    if me('tensor'):
                        wait('VS', scn['vST'][g])
                        wait('VS', vA)
                        wait('VS', scn['vstate'])
                        if g == 0:
                            wait('SS', scn['vRD'])
                        rview = reads_ps.ap().rearrange("p (a c) -> p a c", a=DC)
                        dview = delta_ps.ap().rearrange("p (a c) -> p a c", a=DC)
                        for j, h in enumerate(heads):
                            par, dcb = h % 2, h // 2
                            pb = par * 64
                            eng.matmul(rview[pb:pb + 64, dcb, :], stateT_bf[:, h, :],
                                       rkgp[:, dcb, :], start=True, stop=False)
                            eng.matmul(rview[pb:pb + 64, dcb, :],
                                       v_tm[:, dcb, pb:pb + 64],
                                       stil[g][:, (j % 2) * 2 + j // 2, :],
                                       start=False, stop=True)
                            inst = eng.matmul(dview[pb:pb + 64, dcb, :],
                                              wk_tm[:, dcb, pb:pb + 64], vgw[:, dcb, pb:pb + 64],
                                              start=True, stop=True)
                        inst.then_inc(sems['TS'], 1)
                    pub('TS'); scn['tID'] = cnt['TS']
                scn['tRD'] = scn['tID']
                t_inter = scn['tRD']
            else:
                if me('tensor'):
                    wait('VS', vA)
                    wait('VS', scn['vstate'])
                    dview = delta_ps.ap().rearrange("p (a c) -> p a c", a=DC)
                    for h in range(H):
                        par, dcb = h % 2, h // 2
                        pb = par * 64
                        inst = eng.matmul(dview[pb:pb + 64, dcb, :],
                                          wk_tm[:, dcb, pb:pb + 64], vgw[:, dcb, pb:pb + 64],
                                          start=True, stop=True)
                    inst.then_inc(sems['TS'], 1)
                pub('TS'); scn['tRD'] = cnt['TS']
                scn['tID'] = scn['tRD']
                t_inter = 0

            if me('vector'):
                wait('TS', scn['tID'])
                eng.tensor_tensor(stateT.ap().rearrange("p a i -> p (a i)"),
                                  stateT.ap().rearrange("p a i -> p (a i)"),
                                  gCst.ap().rearrange("p a i -> p (a i)"), ALU.mult)
                eng.drain()
                eng.tensor_tensor(stateT.ap().rearrange("p a i -> p (a i)"),
                                  stateT.ap().rearrange("p a i -> p (a i)"),
                                  delta_ps[:], ALU.add)
                eng.drain()
                if cast:
                    eng.tensor_copy(stateT_bf[0:64, 0:H:2, :], stateT[0:64, :, :])
                    eng.tensor_copy(stateT_bf[64:128, 1:16:2, :], stateT[64:128, :, :])
                    eng.drain()
                eng.drain().then_inc(sems['VS'], 1)
            pub('VS'); scn['vstate'] = cnt['VS']

            if full:
                if me('scalar'):
                    wait('TS', scn['tRD'])
                    rview = reads_ps.ap().rearrange("p (a c) -> p a c", a=DC)
                    for dcb in range(DC):
                        eng.activation(reads_fm[:, dcb, scn['t0']:scn['t0'] + C],
                                       rview[:, dcb, :], AF.Copy,
                                       scale=al_pair[:, dcb:dcb + 1])
                    eng.drain().then_inc(sems['SS'], 1)
                pub('SS'); scn['vRD'] = cnt['SS']

        ln_block(xln_m.ap(), 0, 1, dep=('GDP', GD_pre))
        for b in range(NBM):
            vs_lnb = ln_block(xln_m.ap(), 1 + b * LB, LB)
            for ci in range(CPB):
                m = b * CPB + ci
                scn['t0'] = m * C
                scan_chunk(xln_m.ap(), m * C, full=True, cast=True,
                           pre_vs=(vs_lnb if ci == 0 else 0))
        VS_lite = scn['vstate']
        SS_scan_end = cnt['SS']
        VS_scan_end0 = cnt['VS']
        if me('gpsimd'):
            wait('VS', VS_lite)
            eng.dma_start(out=cc_in,
                          in_=stateT.ap().rearrange("p a i -> p (a i)")).then_inc(sems['CCI'], 16)
            eng.wait_ge(sems['CCI'], 16)
            eng.collective_compute(
                "AllGather", mybir.AluOpType.bypass,
                replica_groups=([[0, 1, 2, 3], [4, 5, 6, 7]] if n_cores == 8
                                else [list(range(n_cores))]),
                ins=[cc_in.opt()], outs=[cc_out.opt()],
            ).then_inc(sems['CCS'])
            eng.wait_ge(sems['CCS'], 1)
            eng.dma_start(out=gth_sb[:, 0:NGRP, :],
                          in_=cc_out.rearrange("(r p) i -> p r i", p=128)).then_inc(sems['CCO'], 16)
        pub('CCI', 16); pub('CCS', 1); pub('CCO', 16)
        CCO_done = cnt['CCO']
        vs_comb = 0
        if me('vector'):
            wait('CCO', CCO_done)
            eng.memset(stateT[:], 0.0)
            eng.drain().then_inc(sems['VS'], 1)
        pub('VS'); vs_m0 = cnt['VS']
        for r in range(NGRP):
            if me('vector'):
                if r == 0:
                    wait('VS', vs_m0)
                eng.tensor_scalar(gp_tmp.ap().rearrange("p a c -> p (a c)")[:, 0:DC * DH],
                                  lgst.ap().rearrange("p a i -> p (a i)"),
                                  gapc_sb[:, r:r + 1], maskc_sb[:, r:r + 1],
                                  ALU.mult, ALU.add)
                eng.drain().then_inc(sems['VS'], 1)
            pub('VS'); vs_e1 = cnt['VS']
            if me('scalar'):
                wait('VS', vs_e1)
                eng.activation(gp_tmp.ap().rearrange("p a c -> p (a c)")[:, 0:DC * DH],
                               gp_tmp.ap().rearrange("p a c -> p (a c)")[:, 0:DC * DH], AF.Exp)
                eng.drain().then_inc(sems['SS'], 1)
            pub('SS'); ss_e1 = cnt['SS']
            if me('vector'):
                wait('SS', ss_e1)
                eng.tensor_tensor(gp_tmp.ap().rearrange("p a c -> p (a c)")[:, DC * DH:2 * DC * DH],
                                  gp_tmp.ap().rearrange("p a c -> p (a c)")[:, 0:DC * DH],
                                  gth_sb[:, r, :], ALU.mult)
                eng.drain()
                eng.tensor_tensor(stateT.ap().rearrange("p a i -> p (a i)"),
                                  stateT.ap().rearrange("p a i -> p (a i)"),
                                  gp_tmp.ap().rearrange("p a c -> p (a c)")[:, DC * DH:2 * DC * DH],
                                  ALU.add)
                eng.drain().then_inc(sems['VS'], 1)
            pub('VS'); vs_comb = cnt['VS']
        if me('vector'):
            eng.tensor_copy(stateT_bf[0:64, 0:H:2, :], stateT[0:64, :, :])
            eng.tensor_copy(stateT_bf[64:128, 1:16:2, :], stateT[64:128, :, :])
            eng.drain().then_inc(sems['VS'], 1)
        pub('VS'); scn['vstate'] = cnt['VS']

        vs_corr = scn['vstate']
        ts_corr = 0
        for m in range(NCHT):
            if me('vector'):
                if m == 0:
                    wait('VS', scn['vstate'])
                else:
                    wait('TS', ts_corr)
                eng.tensor_tensor(vgw.ap()[:], gp_all[:],
                                  gCmk.ap()[:, :, m:m + 1].broadcast_to((128, DC, 128)),
                                  ALU.mult)
                eng.drain()
                eng.tensor_tensor(rkgp.ap()[:], xln_m[:, :, m * C + 1:m * C + 1 + C],
                                  vgw.ap()[:], ALU.mult)
                eng.drain().then_inc(sems['VS'], 1)
            pub('VS'); vs_rk3 = cnt['VS']
            if me('tensor'):
                wait('VS', vs_rk3)
                if m == 0:
                    wait('SS', SS_scan_end)
                else:
                    wait('SS', cnt['SS'])
                rview = reads_ps.ap().rearrange("p (a c) -> p a c", a=DC)
                for h in range(H):
                    par, dcb = h % 2, h // 2
                    pb = par * 64
                    inst = eng.matmul(rview[pb:pb + 64, dcb, :], stateT_bf[:, h, :],
                                      rkgp[:, dcb, :], start=True, stop=True)
                inst.then_inc(sems['TS'], 1)
            pub('TS'); ts_corr = cnt['TS']
            if me('scalar'):
                wait('TS', ts_corr)
                rview = reads_ps.ap().rearrange("p (a c) -> p a c", a=DC)
                for dcb in range(DC):
                    eng.activation(rkgp[:, dcb, :], rview[:, dcb, :], AF.Copy,
                                   scale=al_pair[:, dcb:dcb + 1])
                eng.drain().then_inc(sems['SS'], 1)
            pub('SS'); ss_corr = cnt['SS']
            if me('vector'):
                wait('SS', ss_corr)
                eng.tensor_tensor(reads_fm[:, :, m * C:m * C + C],
                                  reads_fm[:, :, m * C:m * C + C],
                                  rkgp.ap()[:], ALU.add)
                eng.drain().then_inc(sems['VS'], 1)
            pub('VS')
        VS_scan_end = cnt['VS']
        TS_corr_end = cnt['TS']
        SS_corr_end = cnt['SS']
        VS_corr_end = cnt['VS']

        gd_w1 = {}
        w1_free = {}
        gd_w2 = {}
        w2_free = {}
        w1_ld = 0
        w2_ld = 0
        ss_gelu = {}
        ih_g = 0
        ts_y_last = 0
        hstate = dict(ss_end={})

        def emit_w1_load(e, fb, sidx, blk):
            slot = w1_ld_ref[0] % 2
            blk['slots'].append(slot)
            if me('gpsimd'):
                if w1_ld_ref[0] >= 2:
                    wait('TS', w1_free[w1_ld_ref[0] - 2])
                else:
                    wait('TS', TS_corr_end)
                    wait('SS', SS_corr_end)
                    wait('VS', VS_corr_end)
                eng.dma_start(
                    out=w1b[:, slot, :, :],
                    in_=w1t.rearrange("e (a p) f -> e p a f", p=128)[
                        e, :, :, fb * FB + sidx * W1S * 128: fb * FB + (sidx + 1) * W1S * 128]
                ).then_inc(sems['GW1%d' % slot], 16)
            pub('GW1%d' % slot, 16)
            blk['loads'].append(('GW1%d' % slot, cnt['GW1%d' % slot]))
            blk['ldix'].append(w1_ld_ref[0])
            w1_ld_ref[0] += 1

        def emit_h_step(p, e, fb, fi, th, blk):
            nonlocal ih_g, ts_y_last
            toff_l = p * TT
            s_idx = fi // W1S
            hb = h_ps.ap()[:, ih_g % 2, :]
            lastchunk = (toff_l + (th + 1) * H512) // 128 - 1
            if me('tensor'):
                wait(*blk['loads'][s_idx])
                wait('VS', vs_x2[lastchunk])
                if ih_g >= 2:
                    wait('SS', ss_gelu[ih_g - 2])
                for dj in range(DC):
                    inst = eng.matmul(
                        hb[:, 0:H512], w1b[:, blk['slots'][s_idx], dj,
                                           (fi % W1S) * 128:(fi % W1S) * 128 + 128],
                        xln2_fm[:, dj, toff_l + th * H512: toff_l + (th + 1) * H512],
                        start=(dj == 0), stop=(dj == DC - 1))
                inst.then_inc(sems['TS'], 1)
            pub('TS'); ts_h = cnt['TS']
            if fi % W1S == W1S - 1 and th == NTH - 1:
                w1_free[blk['ldix'][s_idx]] = ts_h
            if me('scalar'):
                wait('TS', ts_h)
                if blk['first'] and ts_y_last:
                    wait('TS', ts_y_last)
                eng.activation(
                    g_sb[:, fi, th * H512:(th + 1) * H512], hb[:, 0:H512],
                    AF.Gelu_apprx_tanh,
                    bias=b1_sb.ap()[:, e, fb * NF + fi:fb * NF + fi + 1])
                blk['first'] = False
                eng.drain().then_inc(sems['SS'], 1)
            pub('SS'); ss_gelu[ih_g] = cnt['SS']
            ih_g += 1

        w1_ld_ref = [0]
        pre_blk = dict(slots=[], loads=[], ldix=[], first=True)
        pre_steps = ([('load', sidx) for sidx in range(NS1)] +
                     [('h', fi, th) for th in range(NTH) for fi in range(NF)])
        pre_ptr = [0]

        def drain_pre(n):
            nonlocal ih_g
            while n > 0 and pre_ptr[0] < len(pre_steps):
                st_ = pre_steps[pre_ptr[0]]
                if st_[0] == 'load':
                    emit_w1_load(0, 0, st_[1], pre_blk)
                else:
                    emit_h_step(0, 0, 0, st_[1], st_[2], pre_blk)
                pre_ptr[0] += 1
                n -= 1
            if pre_ptr[0] == len(pre_steps):
                hstate['ss_end'][(0, 0, 0)] = cnt['SS']
                pre_ptr[0] += 1

        sd_x = {}
        ts_tr = {}
        vs_slot = {}
        vs_x2 = {}
        tr_emitted = {}
        for m in range(NCHT):
            sl = m % 2
            if me('sync'):
                if m >= 2:
                    wait('VS', vs_slot[m - 2])
                    wait('SR', 16 * m)
                eng.dma_start(out=xres_t[sl][:],
                              in_=x_tm[m * 128:(m + 1) * 128, :]).then_inc(sems['SX%d' % sl], 16)
            pub('SX%d' % sl, 16); sd_x[m] = cnt['SX%d' % sl]
            if me('tensor'):
                if m == 0:
                    wait('SS', SS_scan_end)
                    wait('VS', VS_scan_end)
                if m >= 2:
                    wait('VS', vs_slot[m - 2])
                for ih in range(2):
                    for dcb in range(DC):
                        eng.matmul(op_ps[:, sl, ih * 512:(ih + 1) * 512],
                                   reads_fm[:, dcb, m * 128:(m + 1) * 128],
                                   WrT_sb[:, dcb, ih * 512:(ih + 1) * 512],
                                   start=(dcb == 0), stop=False)
                    for q in range(4):
                        inst = eng.matmul(
                            op_ps[:, sl, ih * 512 + q * 128: ih * 512 + (q + 1) * 128],
                            xln_m[:, ih * 4 + q, m * 128 + 1: m * 128 + 129],
                            identb[:], start=False, stop=(q == 3))
                inst.then_inc(sems['TS'], 1)
            pub('TS'); ts_op = cnt['TS']
            if me('vector'):
                wait('TS', ts_op)
                wait('SX%d' % sl, sd_x[m])
                eng.tensor_tensor(xres_t[sl][:], op_ps.ap()[:, sl, :], xres_t[sl][:], ALU.add)
                eng.drain().then_inc(sems['VS'], 1)
            pub('VS'); vs_r0 = cnt['VS']
            if me('sync'):
                wait('VS', vs_r0)
                eng.dma_start(out=xres_hbm[m * 128:(m + 1) * 128, :],
                              in_=xres_t[sl][:]).then_inc(sems['SR'], 16)
            pub('SR', 16)
            if me('vector'):
                eng.bn_stats(bn6[:, 0:6], xres_t[sl][:, 0:512])
                eng.bn_stats(bn6[:, 6:12], xres_t[sl][:, 512:1024])
                eng.drain()
                eng.bn_aggr(mv2[:], bn6[:])
                eng.drain()
                eng.tensor_scalar(m1c[:], mv2[:, 1:2], 1.0, 1e-5, ALU.mult, ALU.add)
                eng.drain().then_inc(sems['VS'], 1)
            pub('VS'); vs_v2 = cnt['VS']
            if me('scalar'):
                wait('VS', vs_v2)
                eng.activation(m1c[:], m1c[:], AF.Ln)
                eng.drain()
                eng.activation(m1c[:], m1c[:], AF.Exp, scale=-0.5)
                eng.drain().then_inc(sems['SS'], 1)
            pub('SS'); ss_r2 = cnt['SS']
            if me('vector'):
                wait('SS', ss_r2)
                if m >= 2:
                    wait('TS', ts_tr[m - 2])
                eng.tensor_scalar(xl2tm[sl][:], xres_t[sl][:], mv2[:, 0:1], m1c[:, 0:1],
                                  ALU.subtract, ALU.mult)
                eng.drain().then_inc(sems['VS'], 1)
            pub('VS'); vs_slot[m] = cnt['VS']

            def emit_tr(mm):
                sl2 = mm % 2
                if me('tensor'):
                    wait('VS', vs_slot[mm])
                    if mm >= 2:
                        wait('VS', vs_x2[mm - 2])
                    for dj in range(DC):
                        inst = eng.transpose(tr_ps.ap()[:, sl2, dj * 128:(dj + 1) * 128],
                                             xl2tm[sl2][:, dj * 128:(dj + 1) * 128], identb[:])
                    inst.then_inc(sems['TS'], 1)
                pub('TS'); ts_tr[mm] = cnt['TS']
                if me('vector'):
                    wait('TS', ts_tr[mm])
                    for dj in range(DC):
                        eng.tensor_copy(xln2_fm[:, dj, mm * 128:(mm + 1) * 128],
                                        tr_ps.ap()[:, sl2, dj * 128:(dj + 1) * 128])
                    eng.drain().then_inc(sems['VS'], 1)
                pub('VS'); vs_x2[mm] = cnt['VS']

            tr_emitted[m] = False
            if m >= 1:
                emit_tr(m - 1)
            if m == NCHT - 1:
                emit_tr(m)
            if m >= 4:
                drain_pre(4)
        drain_pre(10 ** 6)
        VS_p3 = cnt['VS']
        TS_p3 = cnt['TS']
        SS_p3 = cnt['SS']

        vs_wprev = VS_p3
        ts_wtr = 0
        for tc in range(2 * NTC):
            if me('tensor'):
                wait('VS', vs_wprev)
                for dj in range(DC):
                    inst = eng.matmul(rt_ps[:, 0:E], xln2_fm[:, dj, tc * 128:(tc + 1) * 128],
                                      gateT_sb[:, dj, :], start=(dj == 0), stop=(dj == DC - 1))
                inst.then_inc(sems['TS'], 1)
            pub('TS'); ts_l = cnt['TS']
            if me('vector'):
                wait('TS', ts_l)
                eng.reduce_max(m0[:], rt_ps[:, 0:E], axis=AX)
                eng.drain()
                eng.tensor_scalar(sm_t[:], rt_ps[:, 0:E], m0[:, 0:1], None, ALU.subtract)
                eng.drain().then_inc(sems['VS'], 1)
            pub('VS'); vs_sm = cnt['VS']
            if me('scalar'):
                wait('VS', vs_sm)
                eng.activation(sm_t[:], sm_t[:], AF.Exp)
                eng.drain().then_inc(sems['SS'], 1)
            pub('SS'); ss_e = cnt['SS']
            if me('vector'):
                wait('SS', ss_e)
                eng.reduce_sum(m0[:], sm_t[:], axis=AX)
                eng.drain()
                eng.reciprocal(m1c[:], m0[:])
                eng.drain()
                eng.tensor_scalar(sm_t[:], sm_t[:], m1c[:, 0:1], None, ALU.mult)
                eng.drain()
                eng.reduce_max(m0[:], sm_t[:], axis=AX)
                eng.drain()
                eng.tensor_scalar(w_tm[:, tc, :], sm_t[:], m0[:, 0:1], None, ALU.is_lt)
                eng.drain()
                eng.tensor_tensor(w_tm[:, tc, :], w_tm[:, tc, :], sm_t[:], ALU.mult)
                eng.drain()
                eng.reduce_max(m1c[:], w_tm[:, tc, :], axis=AX)
                eng.drain()
                eng.tensor_scalar(w_tm[:, tc, :], sm_t[:], m1c[:, 0:1], None, ALU.is_ge)
                eng.drain()
                eng.tensor_tensor(w_tm[:, tc, :], w_tm[:, tc, :], sm_t[:], ALU.mult)
                eng.drain()
                wait('TS', ts_wtr)
                eng.tensor_copy(w_bfT[:], w_tm[:, tc, :])
                eng.drain().then_inc(sems['VS'], 1)
            pub('VS'); vs_wprev = cnt['VS']
            if me('tensor'):
                wait('VS', vs_wprev)
                eng.transpose(wf_ps[:, tc * 128:(tc + 1) * 128], w_bfT[:],
                              identb[:]).then_inc(sems['TS'], 1)
            pub('TS'); ts_wtr = cnt['TS']
        if me('vector'):
            wait('TS', ts_wtr)
            eng.tensor_copy(w_fm.ap().rearrange("p a c -> p (a c)"), wf_ps[:])
            eng.drain().then_inc(sems['VS'], 1)
        pub('VS'); VS_route = cnt['VS']

        vs_pass_done = 0
        vs_yend_prev = 0
        ts_y_last = 0
        sd_fin = {}
        vs_fin = {}
        for p in range(2):
            toff = p * TT
            tcoff = p * NTC
            vs_b2 = {}
            b2_state = dict(done=False, vs_yinit=0)

            def emit_b2init():
                for i in range(NTC * 2):
                    tc, dcl = i // 2, i % 2
                    yb = y_ps.ap()[:, i % 2, :]
                    if me('tensor'):
                        if i == 0:
                            wait('VS', VS_route)
                            if p == 1:
                                wait('VS', vs_yend_prev)
                        if i >= 2:
                            wait('VS', vs_b2[i - 2])
                        eng.matmul(yb, w_fm[:, tcoff + tc, :],
                                   b2_sb[:, dcl * 512:(dcl + 1) * 512],
                                   start=True, stop=True).then_inc(sems['TS'], 1)
                    pub('TS'); ts_b2 = cnt['TS']
                    if me('vector'):
                        wait('TS', ts_b2)
                        if i == 0 and p == 1:
                            wait('VS', vs_pass_done)
                        eng.tensor_copy(y_acc[:, tc, dcl * 512:(dcl + 1) * 512], yb)
                        eng.drain().then_inc(sems['VS'], 1)
                    pub('VS'); vs_b2[i] = cnt['VS']
                b2_state['done'] = True
                b2_state['vs_yinit'] = cnt['VS']

            vs_yevac = {}
            iy_g = 0
            for e in range(E):
                for fb in range(2):
                    if (p, e, fb) == (0, 0, 0):
                        ss_gelu_end = hstate['ss_end'][(0, 0, 0)]
                    else:
                        blk = dict(slots=[], loads=[], ldix=[], first=True)
                        for sidx in range(NS1):
                            emit_w1_load(e, fb, sidx, blk)
                        for fi in range(NF):
                            for th in range(NTH):
                                emit_h_step(p, e, fb, fi, th, blk)
                        ss_gelu_end = cnt['SS']
                    if not b2_state['done']:
                        emit_b2init()
                    for dcl in range(2):
                        slot2 = w2_ld % 2
                        if me('gpsimd'):
                            if w2_ld >= 2:
                                wait('TS', w2_free[w2_ld - 2])
                            else:
                                wait('TS', TS_p3)
                                wait('SS', SS_p3)
                                wait('VS', VS_p3)
                            eng.dma_start(
                                out=w2b[:, slot2, :, :],
                                in_=w2t.rearrange("e (a p) d -> e p a d", p=128)[
                                    e, :, fb * NF:(fb + 1) * NF,
                                    dcl * 512:(dcl + 1) * 512]
                            ).then_inc(sems['GW2%d' % slot2], 16)
                        pub('GW2%d' % slot2, 16)
                        gd_w2[w2_ld] = ('GW2%d' % slot2, cnt['GW2%d' % slot2])
                        for tc in range(NTC):
                            yb = y_ps.ap()[:, iy_g % 2, :]
                            if me('tensor'):
                                wait(*gd_w2[w2_ld])
                                if tc == 0 and dcl == 0:
                                    wait('SS', ss_gelu_end)
                                    wait('VS', b2_state['vs_yinit'])
                                wait('VS', vs_yevac.get(iy_g - 2, 0))
                                for fi in range(NF):
                                    inst = eng.matmul(
                                        yb, g_sb[:, fi, tc * 128:(tc + 1) * 128],
                                        w2b[:, slot2, fi, :],
                                        start=(fi == 0), stop=(fi == NF - 1))
                                inst.then_inc(sems['TS'], 1)
                            pub('TS'); ts_y = cnt['TS']
                            if tc == NTC - 1:
                                w2_free[w2_ld] = ts_y
                            if tc == NTC - 1 and dcl == 1:
                                ts_y_last = ts_y
                            if me('vector'):
                                wait('TS', ts_y)
                                eng.scalar_tensor_tensor(
                                    y_acc[:, tc, dcl * 512:(dcl + 1) * 512], yb,
                                    w_tm.ap()[:, tcoff + tc, e:e + 1],
                                    y_acc[:, tc, dcl * 512:(dcl + 1) * 512],
                                    ALU.mult, ALU.add)
                                eng.drain().then_inc(sems['VS'], 1)
                            pub('VS'); vs_yevac[iy_g] = cnt['VS']
                            iy_g += 1
                        w2_ld += 1
            vs_y_end = cnt['VS']

            for tc in range(NTC):
                fidx = p * NTC + tc
                sl = fidx % 2
                if me('sync'):
                    if fidx >= 2:
                        wait('VS', vs_fin[fidx - 2])
                    else:
                        wait('VS', VS_p3)
                    wait('SR', 16 * NCHT)
                    eng.dma_start(out=xres_t[sl][:],
                                  in_=xres_hbm[toff + tc * 128: toff + (tc + 1) * 128, :]
                                  ).then_inc(sems['SX%d' % sl], 16)
                pub('SX%d' % sl, 16); sd_f = cnt['SX%d' % sl]
                if me('vector'):
                    wait('SX%d' % sl, sd_f)
                    wait('VS', vs_y_end)
                    if fidx >= 2:
                        wait(*sd_fin[fidx - 2])
                    eng.tensor_tensor(outb[sl][:], y_acc[:, tc, :], xres_t[sl][:], ALU.add)
                    eng.drain().then_inc(sems['VS'], 1)
                pub('VS'); vs_fin[fidx] = cnt['VS']
                if me('sync'):
                    wait('VS', vs_fin[fidx])
                    eng.dma_start(out=out[toff + tc * 128: toff + (tc + 1) * 128, :],
                                  in_=outb[sl][:]).then_inc(sems['SO%d' % sl], 16)
                pub('SO%d' % sl, 16); sd_fin[fidx] = ('SO%d' % sl, cnt['SO%d' % sl])
            vs_pass_done = cnt['VS']
            vs_yend_prev = vs_y_end

        if me('sync'):
            for n in ["SDP", "SR", "SX0", "SX1", "SO0", "SO1"]:
                wait(n, cnt[n])
        if me('gpsimd'):
            for n in ["GDP", "GW10", "GW11", "GW20", "GW21", "CCI", "CCO"]:
                wait(n, cnt[n])
        return cnt

    @block.sync
    def _(e):
        stream('sync', e)

    @block.gpsimd
    def _(e):
        stream('gpsimd', e)

    @block.tensor
    def _(e):
        stream('tensor', e)

    @block.vector
    def _(e):
        stream('vector', e)

    @block.scalar
    def _(e):
        stream('scalar', e)

    blk.__exit__(None, None, None)
    return nc


def make_core_inputs(x_flat, n_cores, T, P, WwT, WrT, gateT, dcol, lacol,
                     w1t_a, w2t_a, b1r, b2_a, seq=None):
    Ttot = x_flat.shape[0]
    if seq is None:
        seq = Ttot // 2 if n_cores > 1 else Ttot
    xT = np.ascontiguousarray(x_flat.T)
    maps = []
    for c in range(n_cores):
        t0 = c * T
        bs = (t0 // seq) * seq
        p_c = t0 - bs
        xTm = np.zeros((D, T + 1), np.float32)
        xTm[:, 1:] = xT[:, t0:t0 + T]
        if p_c:
            xTm[:, 0] = xT[:, t0 - 1]
        pos_c = (t0 - bs) // T
        gap = np.zeros((128, 8), np.float32)
        madd = np.full((128, 8), -1e30, np.float32)
        grp = (4 if n_cores == 8 else n_cores)
        for r in range(n_cores):
            r0 = r * T
            rbs = (r0 // seq) * seq
            if rbs == bs and r0 < t0:
                pos_r = (r0 - rbs) // T
                gap[:, r % grp] = float(T * (pos_c - pos_r - 1))
                madd[:, r % grp] = 0.0
        maps.append(dict(
            xTm=xTm, x_tm=np.ascontiguousarray(x_flat[t0:t0 + T]),
            gapcol=gap, maskadd=madd,
            WwT=WwT, WrT=WrT, gateT=gateT, dcol=dcol, lacol=lacol,
            w1t=w1t_a, w2t=w2t_a, b1r=b1r, b2=b2_a))
    return maps


def kernel(L0_attn_out, L0_attn_W_write, L0_attn_W_read, L0_attn_decay,
           L0_attn_log_alpha, L0_ffn_gate_w, L0_ffn_w1, L0_ffn_b1,
           L0_ffn_w2, L0_ffn_b2):
    x = np.ascontiguousarray(np.asarray(L0_attn_out, np.float32).reshape(-1, D))
    WwT = np.ascontiguousarray(np.asarray(L0_attn_W_write, np.float32).T)
    WrT = np.ascontiguousarray(np.asarray(L0_attn_W_read, np.float32).T)
    gT = np.ascontiguousarray(np.asarray(L0_ffn_gate_w, np.float32).T)
    dcol = np.ascontiguousarray(np.broadcast_to(
        np.asarray(L0_attn_decay, np.float32), (128, H)))
    lacol = np.ascontiguousarray(np.broadcast_to(
        np.asarray(L0_attn_log_alpha, np.float32), (128, H)))
    w1 = np.asarray(L0_ffn_w1, np.float32)
    w2 = np.asarray(L0_ffn_w2, np.float32)
    F_ = w1.shape[0]
    w1t_a = np.ascontiguousarray(w1.transpose(1, 2, 0))
    w2t_a = np.ascontiguousarray(w2.transpose(1, 2, 0))
    b1 = np.asarray(L0_ffn_b1, np.float32)
    b1r = np.ascontiguousarray(b1.T.reshape(F_ // 128, 128, E).transpose(1, 2, 0))
    b2_a = np.ascontiguousarray(np.asarray(L0_ffn_b2, np.float32))

    T, P = 2048, 6144
    nc = build_program(T=T, P=P, F=F_)
    maps = make_core_inputs(x, 8, T, P, WwT, WrT, gT, dcol, lacol,
                            w1t_a, w2t_a, b1r, b2_a)
    res = run_bass_kernel_spmd(nc, maps, core_ids=list(range(8)))
    outp = np.concatenate([r["out"] for r in res.results], axis=0)
    return outp.reshape(np.asarray(L0_attn_out).shape).astype(np.float32)


# version 19
# speedup vs baseline: 1.0227x; 1.0022x over previous
import numpy as np
import concourse.bass as bass
import concourse.mybir as mybir
from concourse.bass_utils import run_bass_kernel_spmd

F32 = mybir.dt.float32
BF16 = mybir.dt.bfloat16
AF = mybir.ActivationFunctionType
ALU = mybir.AluOpType
AX = mybir.AxisListType.X

D, E, H, C, DH, DC = 1024, 8, 16, 128, 64, 8


def build_program(T=2048, P=6144, F=4096, LB=512, n_cores=8):
    assert T % LB == 0 and P % LB == 0 and LB % C == 0 and F % 512 == 0
    NBP, NBM = P // LB, T // LB
    CPB = LB // C
    NCHT = T // C
    TT = T // 2
    NTC = TT // 128
    FB = F // 2
    NF = FB // 128
    W1S = max(NF // 2, 1)
    NS1 = NF // W1S
    NTH = max(TT // 512, 1)
    H512 = TT // NTH

    nc = bass.Bass("TRN2", target_bir_lowering=False)
    xTm = nc.dram_tensor("xTm", [D, T + 1], F32, kind="ExternalInput").ap()
    x_tm = nc.dram_tensor("x_tm", [T, D], F32, kind="ExternalInput").ap()
    gapcol = nc.dram_tensor("gapcol", [128, 8], F32, kind="ExternalInput").ap()
    maskadd = nc.dram_tensor("maskadd", [128, 8], F32, kind="ExternalInput").ap()
    cc_in = nc.dram_tensor("cc_in", [128, DC * DH], BF16, kind="Internal").ap()
    NGRP = 4 if n_cores == 8 else n_cores
    cc_out = nc.dram_tensor("cc_out", [NGRP * 128, DC * DH], BF16, kind="Internal").ap()
    WwT = nc.dram_tensor("WwT", [D, D], F32, kind="ExternalInput").ap()
    WrT = nc.dram_tensor("WrT", [D, D], F32, kind="ExternalInput").ap()
    gateT = nc.dram_tensor("gateT", [D, E], F32, kind="ExternalInput").ap()
    dcol = nc.dram_tensor("dcol", [128, H], F32, kind="ExternalInput").ap()
    lacol = nc.dram_tensor("lacol", [128, H], F32, kind="ExternalInput").ap()
    w1t = nc.dram_tensor("w1t", [E, D, F], F32, kind="ExternalInput").ap()
    w2t = nc.dram_tensor("w2t", [E, F, D], F32, kind="ExternalInput").ap()
    b1r = nc.dram_tensor("b1r", [128, E, F // 128], F32, kind="ExternalInput").ap()
    b2 = nc.dram_tensor("b2", [E, D], F32, kind="ExternalInput").ap()
    out = nc.dram_tensor("out", [T, D], F32, kind="ExternalOutput").ap()
    xres_hbm = nc.dram_tensor("xres_hbm", [T, D], F32, kind="Internal").ap()

    def sb(name, shape, dt):
        g = nc.sbuf_tensor(name, shape, dt)
        return g, g.__enter__()

    def sbp(name, shape, dt):
        return sb(name, shape, dt)[1]

    identb = sbp("identb", [128, 128], BF16)
    w_tm = sbp("w_tm", [128, 2 * NTC, E], F32)
    w_bfT = sbp("w_bfT", [128, E], BF16)
    w_fm = sbp("w_fm", [E, 2 * NTC, 128], BF16)
    m0 = sbp("m0", [128, 1], F32)
    m1c = sbp("m1c", [128, 1], F32)
    m2c = sbp("m2c", [128, 1], F32)
    sm_t = sbp("sm_t", [128, E], F32)
    b1_sb = sbp("b1_sb", [128, E, F // 128], F32)
    b2_sb = sbp("b2_sb", [E, D], BF16)
    gateT_sb = sbp("gateT_sb", [128, DC, E], BF16)
    xres_t = [sbp(f"xres_t{i}", [128, D], F32) for i in range(2)]
    xl2tm = [sbp(f"xl2tm{i}", [128, D], BF16) for i in range(2)]
    bn6 = sbp("bn6", [128, 12], F32)
    mv2 = sbp("mv2", [128, 2], F32)
    outb = [sbp(f"outb{i}", [128, D], F32) for i in range(2)]

    gA = []

    def sba(name, shape, dt):
        g, t = sb(name, shape, dt)
        gA.append(g)
        return t

    causal = sba("causal", [128, 128], F32)
    iotaD = sba("iotaD", [128, 128], F32)
    iotaC = sba("iotaC", [128, 128], F32)
    iota_rev = sba("iota_rev", [128, 1], F32)
    ones_bf = sba("ones_bf", [128, 1], BF16)
    onesb_row = sba("onesb_row", [1, 128], BF16)
    dcol_sb = sba("dcol_sb", [128, H], F32)
    lacol_sb = sba("lacol_sb", [128, H], F32)
    lg_cols = sba("lg_cols", [128, H], F32)
    al_cols = sba("al_cols", [128, H], F32)
    al_pair = sba("al_pair", [128, DC], F32)
    lgsel = sba("lgsel", [128, DC], F32)
    lgfull = sba("lgfull", [128, H, DH], F32)
    gCst = sba("gCst", [128, DC, DH], F32)
    gwfull = sba("gwfull", [128, H, DH], BF16)
    MT = sba("MT", [128, 2, DC, 128], BF16)
    gp_all = sba("gp_all", [128, DC, 128], BF16)
    gCmk = sba("gCmk", [128, DC, 16], F32)
    gw_tmp = sba("gw_tmp", [128, H, DH], F32)
    gp_tmp = sba("gp_tmp", [128, DC, 128], F32)
    stateT = sba("stateT", [128, DC, DH], F32)
    stateT_bf = sba("stateT_bf", [128, H, DH], BF16)
    v_tm = sba("v_tm", [128, DC, 128], BF16)
    vgw = sba("vgw", [128, DC, 128], BF16)
    wk_tm = sba("wk_tm", [128, DC, 128], BF16)
    stil = [sba(f"stil{g}", [128, 4, 128], BF16) for g in range(4)]
    rkgp = sba("rkgp", [128, DC, 128], BF16)
    arow = sba("arow", [1, LB], BF16)
    brow = sba("brow", [1, LB], BF16)
    a_sb = sba("a_sb", [128, LB], BF16)
    b_sb = sba("b_sb", [128, LB], BF16)
    sq_sb = sba("sq_sb", [128, DC, LB], BF16)
    r0 = sba("r0", [1, LB], F32)
    r1 = sba("r1", [1, LB], F32)
    r2 = sba("r2", [1, LB], F32)
    xln_m = sba("xln_m", [128, DC, T + 8], BF16)
    reads_fm = sba("reads_fm", [128, DC, T], BF16)
    WwT_sb = sba("WwT_sb", [128, DC, D], BF16)
    WrT_sb = sba("WrT_sb", [128, DC, D], BF16)
    gth_sb = sba("gth_sb", [128, 8, DC * DH], F32)
    lgst = sba("lgst", [128, DC, DH], F32)
    gapc_sb = sba("gapc_sb", [128, 8], F32)
    maskc_sb = sba("maskc_sb", [128, 8], F32)
    mt_tmp = sba("mt_tmp", [128, H // 2, 128], F32)
    for g in reversed(gA):
        g.__exit__(None, None, None)
    gB = []
    gg, xln2_fm = sb("xln2_fm", [128, DC, T], BF16); gB.append(gg)
    gg, g_sb = sb("g_sb", [128, NF, TT], BF16); gB.append(gg)
    gg, w1b = sb("w1b", [128, 2, DC, W1S * 128], BF16); gB.append(gg)
    gg, w2b = sb("w2b", [128, 2, NF, 512], BF16); gB.append(gg)
    gg, y_acc = sb("y_acc", [128, NTC, D], F32); gB.append(gg)
    for g in reversed(gB):
        g.__exit__(None, None, None)

    def psm(name, shape, dt):
        g = nc.psum_tensor(name, shape, dt)
        return g, g.__enter__()

    gPA = []
    gg, v_ps = psm("v_ps", [128, 1024], F32); gPA.append(gg)
    gg, wk_ps = psm("wk_ps", [128, 1024], BF16); gPA.append(gg)
    gg, delta_ps = psm("delta_ps", [128, 512], F32); gPA.append(gg)
    gLN = []
    gg, stx_ps = psm("stx_ps", [1, 512], F32); gLN.append(gg)
    gg, sts_ps = psm("sts_ps", [1, 512], F32); gLN.append(gg)
    gg, bca_ps = psm("bca_ps", [128, 512], F32); gLN.append(gg)
    gg, bcb_ps = psm("bcb_ps", [128, 512], F32); gLN.append(gg)
    for g in reversed(gLN):
        g.__exit__(None, None, None)
    gMS = []
    gg, st_pse = psm("st_pse", [128, 512], F32); gMS.append(gg)
    gg, st_pso = psm("st_pso", [128, 512], F32); gMS.append(gg)
    gg, reads_ps = psm("reads_ps", [128, 1024], F32); gMS.append(gg)
    for g in reversed(gMS):
        g.__exit__(None, None, None)
    for g in reversed(gPA):
        g.__exit__(None, None, None)
    gP3 = []
    gg, op_ps = psm("op_ps", [128, 2, 1024], F32); gP3.append(gg)
    gg, tr_ps = psm("tr_ps", [128, 2, 1024], BF16); gP3.append(gg)
    for g in reversed(gP3):
        g.__exit__(None, None, None)
    gP4 = []
    gg, h_ps = psm("h_ps", [128, 2, 512], F32); gP4.append(gg)
    gg, y_ps = psm("y_ps", [128, 2, 512], F32); gP4.append(gg)
    gg, rt_ps = psm("rt_ps", [128, 512], F32); gP4.append(gg)
    gg, wf_ps = psm("wf_ps", [8, 2 * NTC * 128], BF16); gP4.append(gg)
    for g in reversed(gP4):
        g.__exit__(None, None, None)

    sems = {}
    for n in ["TS", "VS", "SS", "GC", "SDP", "GDP", "SR",
              "GX0", "GX1", "SX0", "SX1", "SO0", "SO1",
              "GW10", "GW11", "GW20", "GW21", "CCI", "CCS", "CCO",
              "GM0", "GM1", "GM2", "GM3"]:
        g = nc.semaphore(n)
        sems[n] = g.__enter__()
    blk = nc.Block()
    block = blk.__enter__()

    def stream(which, eng):
        cnt = {k: 0 for k in sems}

        def pub(name, n=1):
            cnt[name] += n
            return cnt[name]

        def wait(name, val):
            if eng is not None and val > 0:
                eng.wait_ge(sems[name], val)

        def me(o):
            return which == o

        if me('gpsimd'):
            eng.memset(ones_bf[:], 1.0)
            eng.memset(onesb_row[:], 1.0)
            eng.iota(iotaD[:], pattern=[[1, 128]], base=-1, channel_multiplier=-1,
                     allow_small_or_imprecise_dtypes=True)
            eng.drain()
            eng.tensor_scalar_max(iotaD[:], iotaD[:], 0.0)
            eng.iota(iotaC[:], pattern=[[1, 128]], base=0, channel_multiplier=0,
                     allow_small_or_imprecise_dtypes=True)
            eng.iota(iota_rev[:], pattern=[[0, 1]], base=C - 1, channel_multiplier=-1,
                     allow_small_or_imprecise_dtypes=True)
            eng.memset(causal[:], 0.0)
            eng.memset(identb[:], 0.0)
            eng.memset(xln_m[:, :, T + 1:T + 8], 0.0)
            eng.drain()
            eng.affine_select(out=causal[:], in_=causal[:], compare_op=ALU.is_ge,
                              fill=1.0, base=0, pattern=[[-1, 128]], channel_multiplier=1)
            eng.affine_select(out=identb[:], in_=identb[:], compare_op=ALU.not_equal,
                              fill=1.0, base=0, pattern=[[-1, 128]], channel_multiplier=1)
            eng.drain().then_inc(sems['GC'], 1)
        pub('GC')
        GC_c = cnt['GC']

        if me('gpsimd'):
            xv = xTm.rearrange("(a p) t -> p a t", p=128)
            eng.dma_start(out=xln_m[:, :, 0:LB + 1],
                          in_=xv[:, :, 0:LB + 1]).then_inc(sems['GM0'], 16)
            for _b in range(1, NBM):
                eng.dma_start(out=xln_m[:, :, 1 + _b * LB:1 + (_b + 1) * LB],
                              in_=xv[:, :, 1 + _b * LB:1 + (_b + 1) * LB]
                              ).then_inc(sems['GM%d' % _b], 16)
            eng.dma_start(out=WwT_sb[:],
                          in_=WwT.rearrange("(a p) i -> p a i", p=128)).then_inc(sems['GDP'], 16)
            eng.dma_start(out=WrT_sb[:],
                          in_=WrT.rearrange("(a p) i -> p a i", p=128)).then_inc(sems['GDP'], 16)
            eng.dma_start(out=gateT_sb[:],
                          in_=gateT.rearrange("(a p) e -> p a e", p=128)).then_inc(sems['GDP'], 16)
            eng.dma_start(out=b2_sb[:], in_=b2).then_inc(sems['GDP'], 16)
        for _b in range(NBM):
            pub('GM%d' % _b, 16)
        pub('GDP', 16 * 4)
        GD_pre = cnt['GDP']

        if me('sync'):
            eng.dma_start(out=dcol_sb[:], in_=dcol).then_inc(sems['SDP'], 16)
            eng.dma_start(out=lacol_sb[:], in_=lacol).then_inc(sems['SDP'], 16)
            eng.dma_start(out=b1_sb[:], in_=b1r).then_inc(sems['SDP'], 16)
            eng.dma_start(out=gapc_sb[:], in_=gapcol).then_inc(sems['SDP'], 16)
            eng.dma_start(out=maskc_sb[:], in_=maskadd).then_inc(sems['SDP'], 16)
        pub('SDP', 16 * 5)
        SD_pre = cnt['SDP']

        if me('scalar'):
            wait('SDP', SD_pre)
            eng.activation(lg_cols[:], dcol_sb[:], AF.Sigmoid)
            eng.drain()
            eng.activation(lg_cols[:], lg_cols[:], AF.Ln)
            eng.activation(al_cols[:], lacol_sb[:], AF.Exp)
            eng.drain().then_inc(sems['SS'], 1)
        pub('SS')
        SS_rows = cnt['SS']

        if me('vector'):
            wait('SS', SS_rows)
            wait('GC', GC_c)
            eng.tensor_copy(lgfull[:], lg_cols.ap().unsqueeze(2).broadcast_to((128, H, DH)))
            eng.drain()
            eng.tensor_copy(al_pair[0:64, :], al_cols[0:64, 0:H:2])
            eng.tensor_copy(al_pair[64:128, :], al_cols[64:128, 1:H:2])
            eng.tensor_copy(lgsel[0:64, :], lg_cols[0:64, 0:H:2])
            eng.tensor_copy(lgsel[64:128, :], lg_cols[64:128, 1:H:2])
            eng.tensor_copy(lgst[0:64, :, :], lgfull[0:64, 0:H:2, :])
            eng.tensor_copy(lgst[64:128, :, :], lgfull[64:128, 1:H:2, :])
            eng.drain()
            for h in range(H // 2):
                eng.tensor_scalar(mt_tmp[:, h, :], iotaD[:], lg_cols[:, h:h + 1], None, ALU.mult)
            eng.tensor_tensor(gp_tmp[:], iotaC.ap().unsqueeze(1).broadcast_to((128, DC, 128)),
                              lgsel.ap().unsqueeze(2).broadcast_to((128, DC, 128)), ALU.mult)
            eng.tensor_scalar(gw_tmp[:], lgfull[:], iota_rev[:, 0:1], None, ALU.mult)
            eng.tensor_tensor(gCmk[:], lgsel.ap().unsqueeze(2).broadcast_to((128, DC, 16)),
                              iotaC.ap()[:, 0:16].unsqueeze(1).broadcast_to((128, DC, 16)),
                              ALU.mult)
            eng.drain().then_inc(sems['VS'], 1)
            eng.memset(stateT[:], 0.0)
            eng.memset(stateT_bf[:], 0.0)
            eng.drain().then_inc(sems['VS'], 1)
        pub('VS'); VS_pre1 = cnt['VS']
        pub('VS'); VS_state0 = cnt['VS']

        if me('scalar'):
            wait('VS', VS_pre1)
            eng.activation(mt_tmp[:], mt_tmp[:], AF.Exp)
            eng.activation(gp_all.ap(), gp_tmp[:], AF.Exp)
            eng.activation(gwfull.ap(), gw_tmp[:], AF.Exp)
            eng.activation(gCmk[:], gCmk[:], AF.Exp, scale=float(C))
            eng.activation(gCst[0:64, :, :], lgfull[0:64, 0:H:2, :], AF.Exp, scale=float(C))
            eng.activation(gCst[64:128, :, :], lgfull[64:128, 1:H:2, :], AF.Exp, scale=float(C))
            eng.drain().then_inc(sems['SS'], 1)
        pub('SS'); SS_exps = cnt['SS']

        if me('vector'):
            wait('SS', SS_exps)
            eng.tensor_tensor(MT.ap()[:, :, 0:4, :].transpose([0, 2, 1, 3]),
                              mt_tmp.ap().rearrange("p (a b) c -> p a b c", b=2),
                              causal.ap().unsqueeze(1).unsqueeze(1).broadcast_to((128, 4, 2, 128)),
                              ALU.mult)
            eng.drain()
            for h in range(H // 2, H):
                eng.tensor_scalar(mt_tmp[:, h - H // 2, :], iotaD[:],
                                  lg_cols[:, h:h + 1], None, ALU.mult)
            eng.drain().then_inc(sems['VS'], 1)
        pub('VS'); VS_mt2 = cnt['VS']
        if me('scalar'):
            wait('VS', VS_mt2)
            eng.activation(mt_tmp[:], mt_tmp[:], AF.Exp)
            eng.drain().then_inc(sems['SS'], 1)
        pub('SS'); SS_mt2 = cnt['SS']
        if me('vector'):
            wait('SS', SS_mt2)
            eng.tensor_tensor(MT.ap()[:, :, 4:8, :].transpose([0, 2, 1, 3]),
                              mt_tmp.ap().rearrange("p (a b) c -> p a b c", b=2),
                              causal.ap().unsqueeze(1).unsqueeze(1).broadcast_to((128, 4, 2, 128)),
                              ALU.mult)
            eng.drain().then_inc(sems['VS'], 1)
        pub('VS'); VS_MT = cnt['VS']

        lnv = dict(rows=0, bc=0, sqfree=0)

        def ln_block(buf, c0, n, dep=None):
            if me('scalar'):
                if dep is not None:
                    wait(dep[0], dep[1])
                wait('TS', lnv['sqfree'])
                for dj in range(DC):
                    eng.activation(sq_sb[:, dj, 0:n], buf[:, dj, c0:c0 + n], AF.Square)
                eng.drain().then_inc(sems['SS'], 1)
            pub('SS'); ss_sq = cnt['SS']
            if me('tensor'):
                if dep is not None:
                    wait(dep[0], dep[1])
                wait('SS', ss_sq)
                wait('VS', lnv['rows'])
                for dj in range(DC):
                    eng.matmul(stx_ps[:, 0:n], ones_bf[:], buf[:, dj, c0:c0 + n],
                               start=(dj == 0), stop=(dj == DC - 1))
                for dj in range(DC):
                    inst = eng.matmul(sts_ps[:, 0:n], ones_bf[:], sq_sb[:, dj, 0:n],
                                      start=(dj == 0), stop=(dj == DC - 1))
                inst.then_inc(sems['TS'], 1)
            pub('TS'); ts_st = cnt['TS']; lnv['sqfree'] = ts_st
            if me('vector'):
                wait('TS', ts_st)
                eng.tensor_scalar(r0[:, 0:n], stx_ps[:, 0:n], 1.0 / D, None, ALU.mult)
                eng.tensor_scalar(r1[:, 0:n], sts_ps[:, 0:n], 1.0 / D, 1e-5, ALU.mult, ALU.add)
                eng.drain()
                eng.tensor_tensor(r2[:, 0:n], r0[:, 0:n], r0[:, 0:n], ALU.mult)
                eng.drain()
                eng.tensor_tensor(r1[:, 0:n], r1[:, 0:n], r2[:, 0:n], ALU.subtract)
                eng.drain().then_inc(sems['VS'], 1)
            pub('VS'); vs_var = cnt['VS']; lnv['rows'] = vs_var
            if me('scalar'):
                wait('VS', vs_var)
                eng.activation(r1[:, 0:n], r1[:, 0:n], AF.Ln)
                eng.drain()
                eng.activation(r1[:, 0:n], r1[:, 0:n], AF.Exp, scale=-0.5)
                eng.drain().then_inc(sems['SS'], 1)
            pub('SS'); ss_rs = cnt['SS']
            if me('vector'):
                wait('SS', ss_rs)
                eng.tensor_copy(arow[:, 0:n], r1[:, 0:n])
                eng.tensor_tensor(r2[:, 0:n], r0[:, 0:n], r1[:, 0:n], ALU.mult)
                eng.drain()
                eng.tensor_scalar(brow[:, 0:n], r2[:, 0:n], -1.0, None, ALU.mult)
                eng.drain().then_inc(sems['VS'], 1)
            pub('VS'); vs_ab = cnt['VS']
            if me('tensor'):
                wait('VS', vs_ab)
                wait('VS', lnv['bc'])
                eng.matmul(bca_ps[:, 0:n], onesb_row[:], arow[:, 0:n], start=True, stop=True)
                inst = eng.matmul(bcb_ps[:, 0:n], onesb_row[:], brow[:, 0:n], start=True, stop=True)
                inst.then_inc(sems['TS'], 1)
            pub('TS'); ts_bc = cnt['TS']
            if me('vector'):
                wait('TS', ts_bc)
                eng.tensor_copy(a_sb[:, 0:n], bca_ps[:, 0:n])
                eng.tensor_copy(b_sb[:, 0:n], bcb_ps[:, 0:n])
                eng.drain()
                for dj in range(DC):
                    eng.tensor_tensor(buf[:, dj, c0:c0 + n], buf[:, dj, c0:c0 + n],
                                      a_sb[:, 0:n], ALU.mult)
                eng.drain()
                for dj in range(DC):
                    eng.tensor_tensor(buf[:, dj, c0:c0 + n], buf[:, dj, c0:c0 + n],
                                      b_sb[:, 0:n], ALU.add)
                eng.drain().then_inc(sems['VS'], 1)
            pub('VS'); lnv['bc'] = cnt['VS']
            return cnt['VS']

        scn = dict(vA=0, vA2=0, tRD=0, tID=0, vST={}, vRD=0, vstate=VS_state0, t0=0)

        def scan_chunk(buf, wk0, full, cast, pre_vs=0, ps_alt=False):
            if ps_alt:
                vps = reads_ps
                wps = st_pso.ap().bitcast(BF16)
            else:
                vps = v_ps
                wps = wk_ps.ap()
            if me('tensor'):
                wait('VS', scn['vA2'] if ps_alt else scn['vA'])
                wait('GDP', GD_pre)
                if pre_vs:
                    wait('VS', pre_vs)
                for ih in range(2):
                    for dj in range(DC):
                        eng.matmul(vps[:, ih * 512:(ih + 1) * 512],
                                   buf[:, dj, wk0 + 1:wk0 + 1 + C],
                                   WwT_sb[:, dj, ih * 512:(ih + 1) * 512],
                                   start=(dj == 0), stop=(dj == DC - 1))
                for dc in range(DC):
                    inst = eng.transpose(
                        wps.rearrange("p (a c) -> p a c", a=DC)[:, dc, :],
                        buf[:, dc, wk0:wk0 + C], identb[:])
                inst.then_inc(sems['TS'], 1)
            pub('TS'); tA = cnt['TS']
            if me('vector'):
                wait('TS', tA)
                if scn['tRD']:
                    wait('TS', scn['tRD'])
                eng.tensor_copy(v_tm.ap().rearrange("p a c -> p (a c)"), vps[:])
                eng.tensor_copy(wk_tm.ap().rearrange("p a c -> p (a c)"), wps)
                eng.tensor_tensor(vgw.ap().rearrange("p a c -> p (a c)"),
                                  vps[:],
                                  gwfull.ap().rearrange("p h i -> p (h i)"), ALU.mult)
                eng.drain().then_inc(sems['VS'], 1)
            pub('VS')
            if ps_alt:
                scn['vA2'] = cnt['VS']
            else:
                scn['vA'] = cnt['VS']
            vA = cnt['VS']

            if full:
                for g in range(4):
                    heads = [4 * g + j for j in range(4)]
                    if me('tensor'):
                        wait('VS', scn['vST'].get(g - 2 if g >= 2 else g + 2, 0))
                        if g < 2:
                            wait('VS', VS_MT)
                        for j, h in enumerate(heads):
                            par, dcb = h % 2, h // 2
                            slot = (g % 2) * 256 + (j // 2) * 128
                            bank = st_pse if par == 0 else st_pso
                            inst = eng.matmul(bank[:, slot:slot + 128],
                                              buf[par * 64:par * 64 + 64, dcb, wk0:wk0 + C],
                                              buf[par * 64:par * 64 + 64, dcb, wk0 + 1:wk0 + 1 + C],
                                              start=True, stop=True)
                        inst.then_inc(sems['TS'], 1)
                    pub('TS'); tst = cnt['TS']
                    if me('vector'):
                        wait('TS', tst)
                        sl0 = (g % 2) * 256
                        eng.tensor_tensor(
                            stil[g].ap()[:, 0:2, :],
                            st_pse.ap()[:, sl0:sl0 + 256].rearrange("p (a c) -> p a c", a=2),
                            MT[:, 0, 2 * g:2 * g + 2, :], ALU.mult)
                        eng.tensor_tensor(
                            stil[g].ap()[:, 2:4, :],
                            st_pso.ap()[:, sl0:sl0 + 256].rearrange("p (a c) -> p a c", a=2),
                            MT[:, 1, 2 * g:2 * g + 2, :], ALU.mult)
                        eng.tensor_tensor(rkgp[0:64, 2 * g:2 * g + 2, :],
                                          buf[0:64, 2 * g:2 * g + 2, wk0 + 1:wk0 + 1 + C],
                                          gp_all[0:64, 2 * g:2 * g + 2, :], ALU.mult)
                        eng.tensor_tensor(rkgp[64:128, 2 * g:2 * g + 2, :],
                                          buf[64:128, 2 * g:2 * g + 2, wk0 + 1:wk0 + 1 + C],
                                          gp_all[64:128, 2 * g:2 * g + 2, :], ALU.mult)
                        eng.drain().then_inc(sems['VS'], 1)
                    pub('VS'); scn['vST'][g] = cnt['VS']
                    if me('tensor'):
                        wait('VS', scn['vST'][g])
                        wait('VS', vA)
                        wait('VS', scn['vstate'])
                        if g == 0:
                            wait('SS', scn['vRD'])
                        rview = reads_ps.ap().rearrange("p (a c) -> p a c", a=DC)
                        dview = delta_ps.ap().rearrange("p (a c) -> p a c", a=DC)
                        for j, h in enumerate(heads):
                            par, dcb = h % 2, h // 2
                            pb = par * 64
                            eng.matmul(rview[pb:pb + 64, dcb, :], stateT_bf[:, h, :],
                                       rkgp[:, dcb, :], start=True, stop=False)
                            eng.matmul(rview[pb:pb + 64, dcb, :],
                                       v_tm[:, dcb, pb:pb + 64],
                                       stil[g][:, (j % 2) * 2 + j // 2, :],
                                       start=False, stop=True)
                            inst = eng.matmul(dview[pb:pb + 64, dcb, :],
                                              wk_tm[:, dcb, pb:pb + 64], vgw[:, dcb, pb:pb + 64],
                                              start=True, stop=True)
                        inst.then_inc(sems['TS'], 1)
                    pub('TS'); scn['tID'] = cnt['TS']
                scn['tRD'] = scn['tID']
                t_inter = scn['tRD']
            else:
                if me('tensor'):
                    wait('VS', vA)
                    wait('VS', scn['vstate'])
                    dview = delta_ps.ap().rearrange("p (a c) -> p a c", a=DC)
                    for h in range(H):
                        par, dcb = h % 2, h // 2
                        pb = par * 64
                        inst = eng.matmul(dview[pb:pb + 64, dcb, :],
                                          wk_tm[:, dcb, pb:pb + 64], vgw[:, dcb, pb:pb + 64],
                                          start=True, stop=True)
                    inst.then_inc(sems['TS'], 1)
                pub('TS'); scn['tRD'] = cnt['TS']
                scn['tID'] = scn['tRD']
                t_inter = 0

            if me('vector'):
                wait('TS', scn['tID'])
                eng.tensor_tensor(stateT.ap().rearrange("p a i -> p (a i)"),
                                  stateT.ap().rearrange("p a i -> p (a i)"),
                                  gCst.ap().rearrange("p a i -> p (a i)"), ALU.mult)
                eng.drain()
                eng.tensor_tensor(stateT.ap().rearrange("p a i -> p (a i)"),
                                  stateT.ap().rearrange("p a i -> p (a i)"),
                                  delta_ps[:], ALU.add)
                eng.drain()
                if cast:
                    eng.tensor_copy(stateT_bf[0:64, 0:H:2, :], stateT[0:64, :, :])
                    eng.tensor_copy(stateT_bf[64:128, 1:16:2, :], stateT[64:128, :, :])
                    eng.drain()
                eng.drain().then_inc(sems['VS'], 1)
            pub('VS'); scn['vstate'] = cnt['VS']

            if full:
                if me('scalar'):
                    wait('TS', scn['tRD'])
                    rview = reads_ps.ap().rearrange("p (a c) -> p a c", a=DC)
                    for dcb in range(DC):
                        eng.activation(reads_fm[:, dcb, scn['t0']:scn['t0'] + C],
                                       rview[:, dcb, :], AF.Copy,
                                       scale=al_pair[:, dcb:dcb + 1])
                    eng.drain().then_inc(sems['SS'], 1)
                pub('SS'); scn['vRD'] = cnt['SS']

        for b in range(NBM):
            vs_lnb = ln_block(xln_m.ap(), 1 + b * LB, LB, dep=('GM%d' % b, 16))
            if b == 0:
                vs_lnb = ln_block(xln_m.ap(), 0, 1)
            for ci in range(CPB):
                m = b * CPB + ci
                scn['t0'] = m * C
                scan_chunk(xln_m.ap(), m * C, full=True, cast=True,
                           pre_vs=(vs_lnb if ci == 0 else 0))
        VS_lite = scn['vstate']
        SS_scan_end = cnt['SS']
        VS_scan_end0 = cnt['VS']
        if me('gpsimd'):
            wait('VS', VS_lite)
            eng.dma_start(out=cc_in,
                          in_=stateT.ap().rearrange("p a i -> p (a i)")).then_inc(sems['CCI'], 16)
            eng.wait_ge(sems['CCI'], 16)
            eng.collective_compute(
                "AllGather", mybir.AluOpType.bypass,
                replica_groups=([[0, 1, 2, 3], [4, 5, 6, 7]] if n_cores == 8
                                else [list(range(n_cores))]),
                ins=[cc_in.opt()], outs=[cc_out.opt()],
            ).then_inc(sems['CCS'])
            eng.wait_ge(sems['CCS'], 1)
            eng.dma_start(out=gth_sb[:, 0:NGRP, :],
                          in_=cc_out.rearrange("(r p) i -> p r i", p=128)).then_inc(sems['CCO'], 16)
        pub('CCI', 16); pub('CCS', 1); pub('CCO', 16)
        CCO_done = cnt['CCO']
        vs_comb = 0
        if me('vector'):
            wait('CCO', CCO_done)
            eng.memset(stateT[:], 0.0)
            eng.drain().then_inc(sems['VS'], 1)
        pub('VS'); vs_m0 = cnt['VS']
        for r in range(NGRP):
            if me('vector'):
                if r == 0:
                    wait('VS', vs_m0)
                eng.tensor_scalar(gp_tmp.ap().rearrange("p a c -> p (a c)")[:, 0:DC * DH],
                                  lgst.ap().rearrange("p a i -> p (a i)"),
                                  gapc_sb[:, r:r + 1], maskc_sb[:, r:r + 1],
                                  ALU.mult, ALU.add)
                eng.drain().then_inc(sems['VS'], 1)
            pub('VS'); vs_e1 = cnt['VS']
            if me('scalar'):
                wait('VS', vs_e1)
                eng.activation(gp_tmp.ap().rearrange("p a c -> p (a c)")[:, 0:DC * DH],
                               gp_tmp.ap().rearrange("p a c -> p (a c)")[:, 0:DC * DH], AF.Exp)
                eng.drain().then_inc(sems['SS'], 1)
            pub('SS'); ss_e1 = cnt['SS']
            if me('vector'):
                wait('SS', ss_e1)
                eng.tensor_tensor(gp_tmp.ap().rearrange("p a c -> p (a c)")[:, DC * DH:2 * DC * DH],
                                  gp_tmp.ap().rearrange("p a c -> p (a c)")[:, 0:DC * DH],
                                  gth_sb[:, r, :], ALU.mult)
                eng.drain()
                eng.tensor_tensor(stateT.ap().rearrange("p a i -> p (a i)"),
                                  stateT.ap().rearrange("p a i -> p (a i)"),
                                  gp_tmp.ap().rearrange("p a c -> p (a c)")[:, DC * DH:2 * DC * DH],
                                  ALU.add)
                eng.drain().then_inc(sems['VS'], 1)
            pub('VS'); vs_comb = cnt['VS']
        if me('vector'):
            eng.tensor_copy(stateT_bf[0:64, 0:H:2, :], stateT[0:64, :, :])
            eng.tensor_copy(stateT_bf[64:128, 1:16:2, :], stateT[64:128, :, :])
            eng.drain().then_inc(sems['VS'], 1)
        pub('VS'); scn['vstate'] = cnt['VS']

        vs_corr = scn['vstate']
        ts_corr = 0
        for m in range(NCHT):
            if me('vector'):
                if m == 0:
                    wait('VS', scn['vstate'])
                else:
                    wait('TS', ts_corr)
                eng.tensor_tensor(vgw.ap()[:], gp_all[:],
                                  gCmk.ap()[:, :, m:m + 1].broadcast_to((128, DC, 128)),
                                  ALU.mult)
                eng.drain()
                eng.tensor_tensor(rkgp.ap()[:], xln_m[:, :, m * C + 1:m * C + 1 + C],
                                  vgw.ap()[:], ALU.mult)
                eng.drain().then_inc(sems['VS'], 1)
            pub('VS'); vs_rk3 = cnt['VS']
            if me('tensor'):
                wait('VS', vs_rk3)
                if m == 0:
                    wait('SS', SS_scan_end)
                else:
                    wait('SS', cnt['SS'])
                rview = reads_ps.ap().rearrange("p (a c) -> p a c", a=DC)
                for h in range(H):
                    par, dcb = h % 2, h // 2
                    pb = par * 64
                    inst = eng.matmul(rview[pb:pb + 64, dcb, :], stateT_bf[:, h, :],
                                      rkgp[:, dcb, :], start=True, stop=True)
                inst.then_inc(sems['TS'], 1)
            pub('TS'); ts_corr = cnt['TS']
            if me('scalar'):
                wait('TS', ts_corr)
                rview = reads_ps.ap().rearrange("p (a c) -> p a c", a=DC)
                for dcb in range(DC):
                    eng.activation(rkgp[:, dcb, :], rview[:, dcb, :], AF.Copy,
                                   scale=al_pair[:, dcb:dcb + 1])
                eng.drain().then_inc(sems['SS'], 1)
            pub('SS'); ss_corr = cnt['SS']
            if me('vector'):
                wait('SS', ss_corr)
                eng.tensor_tensor(reads_fm[:, :, m * C:m * C + C],
                                  reads_fm[:, :, m * C:m * C + C],
                                  rkgp.ap()[:], ALU.add)
                eng.drain().then_inc(sems['VS'], 1)
            pub('VS')
        VS_scan_end = cnt['VS']

        sd_x = {}
        ts_tr = {}
        vs_slot = {}
        vs_x2 = {}
        tr_emitted = {}
        for m in range(NCHT):
            sl = m % 2
            if me('sync'):
                if m >= 2:
                    wait('VS', vs_slot[m - 2])
                    wait('SR', 16 * m)
                eng.dma_start(out=xres_t[sl][:],
                              in_=x_tm[m * 128:(m + 1) * 128, :]).then_inc(sems['SX%d' % sl], 16)
            pub('SX%d' % sl, 16); sd_x[m] = cnt['SX%d' % sl]
            if me('tensor'):
                if m == 0:
                    wait('SS', SS_scan_end)
                    wait('VS', VS_scan_end)
                if m >= 2:
                    wait('VS', vs_slot[m - 2])
                for ih in range(2):
                    for dcb in range(DC):
                        eng.matmul(op_ps[:, sl, ih * 512:(ih + 1) * 512],
                                   reads_fm[:, dcb, m * 128:(m + 1) * 128],
                                   WrT_sb[:, dcb, ih * 512:(ih + 1) * 512],
                                   start=(dcb == 0), stop=False)
                    for q in range(4):
                        inst = eng.matmul(
                            op_ps[:, sl, ih * 512 + q * 128: ih * 512 + (q + 1) * 128],
                            xln_m[:, ih * 4 + q, m * 128 + 1: m * 128 + 129],
                            identb[:], start=False, stop=(q == 3))
                inst.then_inc(sems['TS'], 1)
            pub('TS'); ts_op = cnt['TS']
            if me('vector'):
                wait('TS', ts_op)
                wait('SX%d' % sl, sd_x[m])
                eng.tensor_tensor(xres_t[sl][:], op_ps.ap()[:, sl, :], xres_t[sl][:], ALU.add)
                eng.drain().then_inc(sems['VS'], 1)
            pub('VS'); vs_r0 = cnt['VS']
            if me('sync'):
                wait('VS', vs_r0)
                eng.dma_start(out=xres_hbm[m * 128:(m + 1) * 128, :],
                              in_=xres_t[sl][:]).then_inc(sems['SR'], 16)
            pub('SR', 16)
            if me('vector'):
                eng.bn_stats(bn6[:, 0:6], xres_t[sl][:, 0:512])
                eng.bn_stats(bn6[:, 6:12], xres_t[sl][:, 512:1024])
                eng.drain()
                eng.bn_aggr(mv2[:], bn6[:])
                eng.drain()
                eng.tensor_scalar(m1c[:], mv2[:, 1:2], 1.0, 1e-5, ALU.mult, ALU.add)
                eng.drain().then_inc(sems['VS'], 1)
            pub('VS'); vs_v2 = cnt['VS']
            if me('scalar'):
                wait('VS', vs_v2)
                eng.activation(m1c[:], m1c[:], AF.Ln)
                eng.drain()
                eng.activation(m1c[:], m1c[:], AF.Exp, scale=-0.5)
                eng.drain().then_inc(sems['SS'], 1)
            pub('SS'); ss_r2 = cnt['SS']
            if me('vector'):
                wait('SS', ss_r2)
                if m >= 2:
                    wait('TS', ts_tr[m - 2])
                eng.tensor_scalar(xl2tm[sl][:], xres_t[sl][:], mv2[:, 0:1], m1c[:, 0:1],
                                  ALU.subtract, ALU.mult)
                eng.drain().then_inc(sems['VS'], 1)
            pub('VS'); vs_slot[m] = cnt['VS']

            def emit_tr(mm):
                sl2 = mm % 2
                if me('tensor'):
                    wait('VS', vs_slot[mm])
                    if mm >= 2:
                        wait('VS', vs_x2[mm - 2])
                    for dj in range(DC):
                        inst = eng.transpose(tr_ps.ap()[:, sl2, dj * 128:(dj + 1) * 128],
                                             xl2tm[sl2][:, dj * 128:(dj + 1) * 128], identb[:])
                    inst.then_inc(sems['TS'], 1)
                pub('TS'); ts_tr[mm] = cnt['TS']
                if me('vector'):
                    wait('TS', ts_tr[mm])
                    for dj in range(DC):
                        eng.tensor_copy(xln2_fm[:, dj, mm * 128:(mm + 1) * 128],
                                        tr_ps.ap()[:, sl2, dj * 128:(dj + 1) * 128])
                    eng.drain().then_inc(sems['VS'], 1)
                pub('VS'); vs_x2[mm] = cnt['VS']

            tr_emitted[m] = False
            if m >= 1:
                emit_tr(m - 1)
            if m == NCHT - 1:
                emit_tr(m)
        VS_p3 = cnt['VS']
        TS_p3 = cnt['TS']

        vs_wprev = VS_p3
        ts_wtr = 0
        for tc in range(2 * NTC):
            if me('tensor'):
                wait('VS', vs_wprev)
                for dj in range(DC):
                    inst = eng.matmul(rt_ps[:, 0:E], xln2_fm[:, dj, tc * 128:(tc + 1) * 128],
                                      gateT_sb[:, dj, :], start=(dj == 0), stop=(dj == DC - 1))
                inst.then_inc(sems['TS'], 1)
            pub('TS'); ts_l = cnt['TS']
            if me('vector'):
                wait('TS', ts_l)
                eng.reduce_max(m0[:], rt_ps[:, 0:E], axis=AX)
                eng.drain()
                eng.tensor_scalar(sm_t[:], rt_ps[:, 0:E], m0[:, 0:1], None, ALU.subtract)
                eng.drain().then_inc(sems['VS'], 1)
            pub('VS'); vs_sm = cnt['VS']
            if me('scalar'):
                wait('VS', vs_sm)
                eng.activation(sm_t[:], sm_t[:], AF.Exp)
                eng.drain().then_inc(sems['SS'], 1)
            pub('SS'); ss_e = cnt['SS']
            if me('vector'):
                wait('SS', ss_e)
                eng.reduce_sum(m0[:], sm_t[:], axis=AX)
                eng.drain()
                eng.reciprocal(m1c[:], m0[:])
                eng.drain()
                eng.tensor_scalar(sm_t[:], sm_t[:], m1c[:, 0:1], None, ALU.mult)
                eng.drain()
                eng.reduce_max(m0[:], sm_t[:], axis=AX)
                eng.drain()
                eng.tensor_scalar(w_tm[:, tc, :], sm_t[:], m0[:, 0:1], None, ALU.is_lt)
                eng.drain()
                eng.tensor_tensor(w_tm[:, tc, :], w_tm[:, tc, :], sm_t[:], ALU.mult)
                eng.drain()
                eng.reduce_max(m1c[:], w_tm[:, tc, :], axis=AX)
                eng.drain()
                eng.tensor_scalar(w_tm[:, tc, :], sm_t[:], m1c[:, 0:1], None, ALU.is_ge)
                eng.drain()
                eng.tensor_tensor(w_tm[:, tc, :], w_tm[:, tc, :], sm_t[:], ALU.mult)
                eng.drain()
                wait('TS', ts_wtr)
                eng.tensor_copy(w_bfT[:], w_tm[:, tc, :])
                eng.drain().then_inc(sems['VS'], 1)
            pub('VS'); vs_wprev = cnt['VS']
            if me('tensor'):
                wait('VS', vs_wprev)
                eng.transpose(wf_ps[:, tc * 128:(tc + 1) * 128], w_bfT[:],
                              identb[:]).then_inc(sems['TS'], 1)
            pub('TS'); ts_wtr = cnt['TS']
        if me('vector'):
            wait('TS', ts_wtr)
            eng.tensor_copy(w_fm.ap().rearrange("p a c -> p (a c)"), wf_ps[:])
            eng.drain().then_inc(sems['VS'], 1)
        pub('VS'); VS_route = cnt['VS']

        vs_pass_done = 0
        vs_yend_prev = 0
        ts_y_last = 0
        gd_w1 = {}
        w1_free = {}
        gd_w2 = {}
        w2_free = {}
        w1_ld = 0
        w2_ld = 0
        sd_fin = {}
        vs_fin = {}
        for p in range(2):
            toff = p * TT
            tcoff = p * NTC
            vs_b2 = {}
            b2_state = dict(done=False, vs_yinit=0)

            def emit_b2init():
                for i in range(NTC * 2):
                    tc, dcl = i // 2, i % 2
                    yb = y_ps.ap()[:, i % 2, :]
                    if me('tensor'):
                        if i == 0:
                            wait('VS', VS_route)
                            if p == 1:
                                wait('VS', vs_yend_prev)
                        if i >= 2:
                            wait('VS', vs_b2[i - 2])
                        eng.matmul(yb, w_fm[:, tcoff + tc, :],
                                   b2_sb[:, dcl * 512:(dcl + 1) * 512],
                                   start=True, stop=True).then_inc(sems['TS'], 1)
                    pub('TS'); ts_b2 = cnt['TS']
                    if me('vector'):
                        wait('TS', ts_b2)
                        if i == 0 and p == 1:
                            wait('VS', vs_pass_done)
                        eng.tensor_copy(y_acc[:, tc, dcl * 512:(dcl + 1) * 512], yb)
                        eng.drain().then_inc(sems['VS'], 1)
                    pub('VS'); vs_b2[i] = cnt['VS']
                b2_state['done'] = True
                b2_state['vs_yinit'] = cnt['VS']

            vs_yevac = {}
            iy_g = 0
            ss_gelu = {}
            ih_g = 0
            for e in range(E):
                for fb in range(2):
                    fcol0 = fb * FB
                    slot_ids = []
                    for s in range(NS1):
                        slot = w1_ld % 2
                        slot_ids.append(slot)
                        if me('gpsimd'):
                            if w1_ld >= 2:
                                wait('TS', w1_free[w1_ld - 2])
                            else:
                                wait('TS', TS_p3)
                            eng.dma_start(
                                out=w1b[:, slot, :, :],
                                in_=w1t.rearrange("e (a p) f -> e p a f", p=128)[
                                    e, :, :, fcol0 + s * W1S * 128: fcol0 + (s + 1) * W1S * 128]
                            ).then_inc(sems['GW1%d' % slot], 16)
                        pub('GW1%d' % slot, 16)
                        gd_w1[w1_ld] = ('GW1%d' % slot, cnt['GW1%d' % slot])
                        w1_ld += 1
                    first_h = True
                    for fi in range(NF):
                        s_idx = fi // W1S
                        for th in range(NTH):
                            hb = h_ps.ap()[:, ih_g % 2, :]
                            if me('tensor'):
                                wait(*gd_w1[w1_ld - NS1 + s_idx])
                                if ih_g >= 2:
                                    wait('SS', ss_gelu[ih_g - 2])
                                for dj in range(DC):
                                    inst = eng.matmul(
                                        hb[:, 0:H512], w1b[:, slot_ids[s_idx], dj,
                                                (fi % W1S) * 128:(fi % W1S) * 128 + 128],
                                        xln2_fm[:, dj, toff + th * H512: toff + (th + 1) * H512],
                                        start=(dj == 0), stop=(dj == DC - 1))
                                inst.then_inc(sems['TS'], 1)
                            pub('TS'); ts_h = cnt['TS']
                            if fi % W1S == W1S - 1 and th == NTH - 1:
                                w1_free[w1_ld - NS1 + s_idx] = ts_h
                            if me('scalar'):
                                wait('TS', ts_h)
                                if first_h and ts_y_last:
                                    wait('TS', ts_y_last)
                                eng.activation(
                                    g_sb[:, fi, th * H512:(th + 1) * H512], hb[:, 0:H512],
                                    AF.Gelu_apprx_tanh,
                                    bias=b1_sb.ap()[:, e, fb * NF + fi:fb * NF + fi + 1])
                                first_h = False
                                eng.drain().then_inc(sems['SS'], 1)
                            pub('SS'); ss_gelu[ih_g] = cnt['SS']
                            ih_g += 1
                    ss_gelu_end = cnt['SS']
                    if not b2_state['done']:
                        emit_b2init()
                    for dcl in range(2):
                        slot2 = w2_ld % 2
                        if me('gpsimd'):
                            if w2_ld >= 2:
                                wait('TS', w2_free[w2_ld - 2])
                            else:
                                wait('TS', TS_p3)
                            eng.dma_start(
                                out=w2b[:, slot2, :, :],
                                in_=w2t.rearrange("e (a p) d -> e p a d", p=128)[
                                    e, :, fb * NF:(fb + 1) * NF,
                                    dcl * 512:(dcl + 1) * 512]
                            ).then_inc(sems['GW2%d' % slot2], 16)
                        pub('GW2%d' % slot2, 16)
                        gd_w2[w2_ld] = ('GW2%d' % slot2, cnt['GW2%d' % slot2])
                        for tc in range(NTC):
                            yb = y_ps.ap()[:, iy_g % 2, :]
                            if me('tensor'):
                                wait(*gd_w2[w2_ld])
                                if tc == 0 and dcl == 0:
                                    wait('SS', ss_gelu_end)
                                    wait('VS', b2_state['vs_yinit'])
                                wait('VS', vs_yevac.get(iy_g - 2, 0))
                                for fi in range(NF):
                                    inst = eng.matmul(
                                        yb, g_sb[:, fi, tc * 128:(tc + 1) * 128],
                                        w2b[:, slot2, fi, :],
                                        start=(fi == 0), stop=(fi == NF - 1))
                                inst.then_inc(sems['TS'], 1)
                            pub('TS'); ts_y = cnt['TS']
                            if tc == NTC - 1:
                                w2_free[w2_ld] = ts_y
                            if tc == NTC - 1 and dcl == 1:
                                ts_y_last = ts_y
                            if me('vector'):
                                wait('TS', ts_y)
                                eng.scalar_tensor_tensor(
                                    y_acc[:, tc, dcl * 512:(dcl + 1) * 512], yb,
                                    w_tm.ap()[:, tcoff + tc, e:e + 1],
                                    y_acc[:, tc, dcl * 512:(dcl + 1) * 512],
                                    ALU.mult, ALU.add)
                                eng.drain().then_inc(sems['VS'], 1)
                            pub('VS'); vs_yevac[iy_g] = cnt['VS']
                            iy_g += 1
                        w2_ld += 1
            vs_y_end = cnt['VS']

            for tc in range(NTC):
                fidx = p * NTC + tc
                sl = fidx % 2
                if me('sync'):
                    if fidx >= 2:
                        wait('VS', vs_fin[fidx - 2])
                    else:
                        wait('VS', VS_p3)
                    wait('SR', 16 * NCHT)
                    eng.dma_start(out=xres_t[sl][:],
                                  in_=xres_hbm[toff + tc * 128: toff + (tc + 1) * 128, :]
                                  ).then_inc(sems['SX%d' % sl], 16)
                pub('SX%d' % sl, 16); sd_f = cnt['SX%d' % sl]
                if me('vector'):
                    wait('SX%d' % sl, sd_f)
                    wait('VS', vs_y_end)
                    if fidx >= 2:
                        wait(*sd_fin[fidx - 2])
                    eng.tensor_tensor(outb[sl][:], y_acc[:, tc, :], xres_t[sl][:], ALU.add)
                    eng.drain().then_inc(sems['VS'], 1)
                pub('VS'); vs_fin[fidx] = cnt['VS']
                if me('sync'):
                    wait('VS', vs_fin[fidx])
                    eng.dma_start(out=out[toff + tc * 128: toff + (tc + 1) * 128, :],
                                  in_=outb[sl][:]).then_inc(sems['SO%d' % sl], 16)
                pub('SO%d' % sl, 16); sd_fin[fidx] = ('SO%d' % sl, cnt['SO%d' % sl])
            vs_pass_done = cnt['VS']
            vs_yend_prev = vs_y_end

        if me('sync'):
            for n in ["SDP", "SR", "SX0", "SX1", "SO0", "SO1"]:
                wait(n, cnt[n])
        if me('gpsimd'):
            for n in ["GDP", "GW10", "GW11", "GW20", "GW21", "CCI", "CCO"]:
                wait(n, cnt[n])
        return cnt

    @block.sync
    def _(e):
        stream('sync', e)

    @block.gpsimd
    def _(e):
        stream('gpsimd', e)

    @block.tensor
    def _(e):
        stream('tensor', e)

    @block.vector
    def _(e):
        stream('vector', e)

    @block.scalar
    def _(e):
        stream('scalar', e)

    blk.__exit__(None, None, None)
    return nc


def make_core_inputs(x_flat, n_cores, T, P, WwT, WrT, gateT, dcol, lacol,
                     w1t_a, w2t_a, b1r, b2_a, seq=None):
    Ttot = x_flat.shape[0]
    if seq is None:
        seq = Ttot // 2 if n_cores > 1 else Ttot
    xT = np.ascontiguousarray(x_flat.T)
    maps = []
    for c in range(n_cores):
        t0 = c * T
        bs = (t0 // seq) * seq
        p_c = t0 - bs
        xTm = np.zeros((D, T + 1), np.float32)
        xTm[:, 1:] = xT[:, t0:t0 + T]
        if p_c:
            xTm[:, 0] = xT[:, t0 - 1]
        pos_c = (t0 - bs) // T
        gap = np.zeros((128, 8), np.float32)
        madd = np.full((128, 8), -1e30, np.float32)
        grp = (4 if n_cores == 8 else n_cores)
        for r in range(n_cores):
            r0 = r * T
            rbs = (r0 // seq) * seq
            if rbs == bs and r0 < t0:
                pos_r = (r0 - rbs) // T
                gap[:, r % grp] = float(T * (pos_c - pos_r - 1))
                madd[:, r % grp] = 0.0
        maps.append(dict(
            xTm=xTm, x_tm=np.ascontiguousarray(x_flat[t0:t0 + T]),
            gapcol=gap, maskadd=madd,
            WwT=WwT, WrT=WrT, gateT=gateT, dcol=dcol, lacol=lacol,
            w1t=w1t_a, w2t=w2t_a, b1r=b1r, b2=b2_a))
    return maps


def kernel(L0_attn_out, L0_attn_W_write, L0_attn_W_read, L0_attn_decay,
           L0_attn_log_alpha, L0_ffn_gate_w, L0_ffn_w1, L0_ffn_b1,
           L0_ffn_w2, L0_ffn_b2):
    x = np.ascontiguousarray(np.asarray(L0_attn_out, np.float32).reshape(-1, D))
    WwT = np.ascontiguousarray(np.asarray(L0_attn_W_write, np.float32).T)
    WrT = np.ascontiguousarray(np.asarray(L0_attn_W_read, np.float32).T)
    gT = np.ascontiguousarray(np.asarray(L0_ffn_gate_w, np.float32).T)
    dcol = np.ascontiguousarray(np.broadcast_to(
        np.asarray(L0_attn_decay, np.float32), (128, H)))
    lacol = np.ascontiguousarray(np.broadcast_to(
        np.asarray(L0_attn_log_alpha, np.float32), (128, H)))
    w1 = np.asarray(L0_ffn_w1, np.float32)
    w2 = np.asarray(L0_ffn_w2, np.float32)
    F_ = w1.shape[0]
    w1t_a = np.ascontiguousarray(w1.transpose(1, 2, 0))
    w2t_a = np.ascontiguousarray(w2.transpose(1, 2, 0))
    b1 = np.asarray(L0_ffn_b1, np.float32)
    b1r = np.ascontiguousarray(b1.T.reshape(F_ // 128, 128, E).transpose(1, 2, 0))
    b2_a = np.ascontiguousarray(np.asarray(L0_ffn_b2, np.float32))

    T, P = 2048, 6144
    nc = build_program(T=T, P=P, F=F_)
    maps = make_core_inputs(x, 8, T, P, WwT, WrT, gT, dcol, lacol,
                            w1t_a, w2t_a, b1r, b2_a)
    res = run_bass_kernel_spmd(nc, maps, core_ids=list(range(8)))
    outp = np.concatenate([r["out"] for r in res.results], axis=0)
    return outp.reshape(np.asarray(L0_attn_out).shape).astype(np.float32)


# version 20 (final)
# speedup vs baseline: 1.0340x; 1.0110x over previous
import numpy as np
import concourse.bass as bass
import concourse.mybir as mybir
from concourse.bass_utils import run_bass_kernel_spmd

F32 = mybir.dt.float32
BF16 = mybir.dt.bfloat16
AF = mybir.ActivationFunctionType
ALU = mybir.AluOpType
AX = mybir.AxisListType.X

D, E, H, C, DH, DC = 1024, 8, 16, 128, 64, 8


def build_program(T=2048, P=6144, F=4096, LB=512, n_cores=8):
    assert T % LB == 0 and P % LB == 0 and LB % C == 0 and F % 512 == 0
    NBP, NBM = P // LB, T // LB
    CPB = LB // C
    NCHT = T // C
    TT = T // 2
    NTC = TT // 128
    FB = F // 2
    NF = FB // 128
    W1S = max(NF // 2, 1)
    NS1 = NF // W1S
    NTH = max(TT // 512, 1)
    H512 = TT // NTH

    nc = bass.Bass("TRN2", target_bir_lowering=False)
    xTm = nc.dram_tensor("xTm", [D, T + 1], F32, kind="ExternalInput").ap()
    x_tm = nc.dram_tensor("x_tm", [T, D], F32, kind="ExternalInput").ap()
    gapcol = nc.dram_tensor("gapcol", [128, 8], F32, kind="ExternalInput").ap()
    maskadd = nc.dram_tensor("maskadd", [128, 8], F32, kind="ExternalInput").ap()
    cc_in = nc.dram_tensor("cc_in", [128, DC * DH], BF16, kind="Internal").ap()
    NGRP = 4 if n_cores == 8 else n_cores
    cc_out = nc.dram_tensor("cc_out", [NGRP * 128, DC * DH], BF16, kind="Internal").ap()
    WwT = nc.dram_tensor("WwT", [D, D], F32, kind="ExternalInput").ap()
    WrT = nc.dram_tensor("WrT", [D, D], F32, kind="ExternalInput").ap()
    gateT = nc.dram_tensor("gateT", [D, E], F32, kind="ExternalInput").ap()
    dcol = nc.dram_tensor("dcol", [128, H], F32, kind="ExternalInput").ap()
    lacol = nc.dram_tensor("lacol", [128, H], F32, kind="ExternalInput").ap()
    w1t = nc.dram_tensor("w1t", [E, D, F], F32, kind="ExternalInput").ap()
    w2t = nc.dram_tensor("w2t", [E, F, D], F32, kind="ExternalInput").ap()
    b1r = nc.dram_tensor("b1r", [128, E, F // 128], F32, kind="ExternalInput").ap()
    b2 = nc.dram_tensor("b2", [E, D], F32, kind="ExternalInput").ap()
    out = nc.dram_tensor("out", [T, D], F32, kind="ExternalOutput").ap()
    xres_hbm = nc.dram_tensor("xres_hbm", [T, D], F32, kind="Internal").ap()

    def sb(name, shape, dt):
        g = nc.sbuf_tensor(name, shape, dt)
        return g, g.__enter__()

    def sbp(name, shape, dt):
        return sb(name, shape, dt)[1]

    identb = sbp("identb", [128, 128], BF16)
    w_tm = sbp("w_tm", [128, 2 * NTC, E], F32)
    w_bfT = sbp("w_bfT", [128, E], BF16)
    w_fm = sbp("w_fm", [E, 2 * NTC, 128], BF16)
    m0 = sbp("m0", [128, 1], F32)
    m1c = sbp("m1c", [128, 1], F32)
    m2c = sbp("m2c", [128, 1], F32)
    sm_t = sbp("sm_t", [128, E], F32)
    b1_sb = sbp("b1_sb", [128, E, F // 128], F32)
    b2_sb = sbp("b2_sb", [E, D], BF16)
    gateT_sb = sbp("gateT_sb", [128, DC, E], BF16)
    xres_t = [sbp(f"xres_t{i}", [128, D], F32) for i in range(2)]
    xl2tm = [sbp(f"xl2tm{i}", [128, D], BF16) for i in range(2)]
    bn6 = sbp("bn6", [128, 12], F32)
    mv2 = sbp("mv2", [128, 2], F32)
    outb = [sbp(f"outb{i}", [128, D], F32) for i in range(2)]

    gA = []

    def sba(name, shape, dt):
        g, t = sb(name, shape, dt)
        gA.append(g)
        return t

    causal = sba("causal", [128, 128], F32)
    iotaD = sba("iotaD", [128, 128], F32)
    iotaC = sba("iotaC", [128, 128], F32)
    iota_rev = sba("iota_rev", [128, 1], F32)
    ones_bf = sba("ones_bf", [128, 1], BF16)
    onesb_row = sba("onesb_row", [1, 128], BF16)
    dcol_sb = sba("dcol_sb", [128, H], F32)
    lacol_sb = sba("lacol_sb", [128, H], F32)
    lg_cols = sba("lg_cols", [128, H], F32)
    al_cols = sba("al_cols", [128, H], F32)
    al_pair = sba("al_pair", [128, DC], F32)
    lgsel = sba("lgsel", [128, DC], F32)
    lgfull = sba("lgfull", [128, H, DH], F32)
    gCst = sba("gCst", [128, DC, DH], F32)
    gwfull = sba("gwfull", [128, H, DH], BF16)
    MT = sba("MT", [128, 2, DC, 128], BF16)
    gp_all = sba("gp_all", [128, DC, 128], BF16)
    gCmk = sba("gCmk", [128, DC, 16], F32)
    gw_tmp = sba("gw_tmp", [128, H, DH], F32)
    gp_tmp = sba("gp_tmp", [128, DC, 128], F32)
    stateT = sba("stateT", [128, DC, DH], F32)
    stateT_bf = sba("stateT_bf", [128, H, DH], BF16)
    v_tm = sba("v_tm", [128, DC, 128], BF16)
    vgw = sba("vgw", [128, DC, 128], BF16)
    wk_tm = sba("wk_tm", [128, DC, 128], BF16)
    stil = [sba(f"stil{g}", [128, 4, 128], BF16) for g in range(4)]
    rkgp = sba("rkgp", [128, DC, 128], BF16)
    arow = sba("arow", [1, LB], BF16)
    brow = sba("brow", [1, LB], BF16)
    a_sb = sba("a_sb", [128, LB], BF16)
    b_sb = sba("b_sb", [128, LB], BF16)
    sq_sb = sba("sq_sb", [128, DC, LB], BF16)
    r0 = sba("r0", [1, LB], F32)
    r1 = sba("r1", [1, LB], F32)
    r2 = sba("r2", [1, LB], F32)
    xln_m = sba("xln_m", [128, DC, T + 8], BF16)
    reads_fm = sba("reads_fm", [128, DC, T], BF16)
    WwT_sb = sba("WwT_sb", [128, DC, D], BF16)
    WrT_sb = sba("WrT_sb", [128, DC, D], BF16)
    gth_sb = sba("gth_sb", [128, 8, DC * DH], F32)
    lgst = sba("lgst", [128, DC, DH], F32)
    gapc_sb = sba("gapc_sb", [128, 8], F32)
    maskc_sb = sba("maskc_sb", [128, 8], F32)
    mt_tmp = sba("mt_tmp", [128, H // 2, 128], F32)
    for g in reversed(gA):
        g.__exit__(None, None, None)
    gB = []
    gg, xln2_fm = sb("xln2_fm", [128, DC, T], BF16); gB.append(gg)
    gg, g_sb = sb("g_sb", [128, NF, TT], BF16); gB.append(gg)
    gg, w1b = sb("w1b", [128, 2, DC, W1S * 128], BF16); gB.append(gg)
    gg, w2b = sb("w2b", [128, 2, NF, 512], BF16); gB.append(gg)
    gg, y_acc = sb("y_acc", [128, NTC, D], F32); gB.append(gg)
    for g in reversed(gB):
        g.__exit__(None, None, None)

    def psm(name, shape, dt):
        g = nc.psum_tensor(name, shape, dt)
        return g, g.__enter__()

    gPA = []
    gg, v_ps = psm("v_ps", [128, 1024], F32); gPA.append(gg)
    gg, wk_ps = psm("wk_ps", [128, 1024], BF16); gPA.append(gg)
    gg, delta_ps = psm("delta_ps", [128, 512], F32); gPA.append(gg)
    gLN = []
    gg, stx_ps = psm("stx_ps", [1, 512], F32); gLN.append(gg)
    gg, sts_ps = psm("sts_ps", [1, 512], F32); gLN.append(gg)
    gg, bca_ps = psm("bca_ps", [128, 512], F32); gLN.append(gg)
    gg, bcb_ps = psm("bcb_ps", [128, 512], F32); gLN.append(gg)
    for g in reversed(gLN):
        g.__exit__(None, None, None)
    gMS = []
    gg, st_pse = psm("st_pse", [128, 512], F32); gMS.append(gg)
    gg, st_pso = psm("st_pso", [128, 512], F32); gMS.append(gg)
    gg, reads_ps = psm("reads_ps", [128, 1024], F32); gMS.append(gg)
    for g in reversed(gMS):
        g.__exit__(None, None, None)
    for g in reversed(gPA):
        g.__exit__(None, None, None)
    gP3 = []
    gg, op_ps = psm("op_ps", [128, 2, 1024], F32); gP3.append(gg)
    gg, tr_ps = psm("tr_ps", [128, 2, 1024], BF16); gP3.append(gg)
    for g in reversed(gP3):
        g.__exit__(None, None, None)
    gP4 = []
    gg, h_ps = psm("h_ps", [128, 2, 512], F32); gP4.append(gg)
    gg, y_ps = psm("y_ps", [128, 2, 512], F32); gP4.append(gg)
    gg, rt_ps = psm("rt_ps", [128, 512], F32); gP4.append(gg)
    gg, wf_ps = psm("wf_ps", [8, 2 * NTC * 128], BF16); gP4.append(gg)
    for g in reversed(gP4):
        g.__exit__(None, None, None)

    sems = {}
    for n in ["TS", "VS", "SS", "GC", "SDP", "GDP", "SR",
              "GX0", "GX1", "SX0", "SX1", "SO0", "SO1",
              "GW10", "GW11", "GW20", "GW21", "CCI", "CCS", "CCO",
              "GM0", "GM1", "GM2", "GM3"]:
        g = nc.semaphore(n)
        sems[n] = g.__enter__()
    blk = nc.Block()
    block = blk.__enter__()

    def stream(which, eng):
        cnt = {k: 0 for k in sems}

        def pub(name, n=1):
            cnt[name] += n
            return cnt[name]

        def wait(name, val):
            if eng is not None and val > 0:
                eng.wait_ge(sems[name], val)

        def me(o):
            return which == o

        if me('gpsimd'):
            eng.memset(ones_bf[:], 1.0)
            eng.memset(onesb_row[:], 1.0)
            eng.iota(iotaD[:], pattern=[[1, 128]], base=-1, channel_multiplier=-1,
                     allow_small_or_imprecise_dtypes=True)
            eng.drain()
            eng.tensor_scalar_max(iotaD[:], iotaD[:], 0.0)
            eng.iota(iotaC[:], pattern=[[1, 128]], base=0, channel_multiplier=0,
                     allow_small_or_imprecise_dtypes=True)
            eng.iota(iota_rev[:], pattern=[[0, 1]], base=C - 1, channel_multiplier=-1,
                     allow_small_or_imprecise_dtypes=True)
            eng.memset(causal[:], 0.0)
            eng.memset(identb[:], 0.0)
            eng.memset(xln_m[:, :, T + 1:T + 8], 0.0)
            eng.drain()
            eng.affine_select(out=causal[:], in_=causal[:], compare_op=ALU.is_ge,
                              fill=1.0, base=0, pattern=[[-1, 128]], channel_multiplier=1)
            eng.affine_select(out=identb[:], in_=identb[:], compare_op=ALU.not_equal,
                              fill=1.0, base=0, pattern=[[-1, 128]], channel_multiplier=1)
            eng.drain().then_inc(sems['GC'], 1)
        pub('GC')
        GC_c = cnt['GC']

        if me('gpsimd'):
            xv = xTm.rearrange("(a p) t -> p a t", p=128)
            eng.dma_start(out=xln_m[:, :, 0:LB + 1],
                          in_=xv[:, :, 0:LB + 1]).then_inc(sems['GM0'], 16)
            for _b in range(1, NBM):
                eng.dma_start(out=xln_m[:, :, 1 + _b * LB:1 + (_b + 1) * LB],
                              in_=xv[:, :, 1 + _b * LB:1 + (_b + 1) * LB]
                              ).then_inc(sems['GM%d' % _b], 16)
            eng.dma_start(out=WwT_sb[:],
                          in_=WwT.rearrange("(a p) i -> p a i", p=128)).then_inc(sems['GDP'], 16)
            eng.dma_start(out=WrT_sb[:],
                          in_=WrT.rearrange("(a p) i -> p a i", p=128)).then_inc(sems['GDP'], 16)
            eng.dma_start(out=gateT_sb[:],
                          in_=gateT.rearrange("(a p) e -> p a e", p=128)).then_inc(sems['GDP'], 16)
            eng.dma_start(out=b2_sb[:], in_=b2).then_inc(sems['GDP'], 16)
        for _b in range(NBM):
            pub('GM%d' % _b, 16)
        pub('GDP', 16 * 4)
        GD_pre = cnt['GDP']

        if me('sync'):
            eng.dma_start(out=dcol_sb[:], in_=dcol).then_inc(sems['SDP'], 16)
            eng.dma_start(out=lacol_sb[:], in_=lacol).then_inc(sems['SDP'], 16)
            eng.dma_start(out=b1_sb[:], in_=b1r).then_inc(sems['SDP'], 16)
            eng.dma_start(out=gapc_sb[:], in_=gapcol).then_inc(sems['SDP'], 16)
            eng.dma_start(out=maskc_sb[:], in_=maskadd).then_inc(sems['SDP'], 16)
        pub('SDP', 16 * 5)
        SD_pre = cnt['SDP']

        if me('scalar'):
            wait('SDP', SD_pre)
            eng.activation(lg_cols[:], dcol_sb[:], AF.Sigmoid)
            eng.drain()
            eng.activation(lg_cols[:], lg_cols[:], AF.Ln)
            eng.activation(al_cols[:], lacol_sb[:], AF.Exp)
            eng.drain().then_inc(sems['SS'], 1)
        pub('SS')
        SS_rows = cnt['SS']

        if me('vector'):
            wait('SS', SS_rows)
            wait('GC', GC_c)
            eng.tensor_copy(lgfull[:], lg_cols.ap().unsqueeze(2).broadcast_to((128, H, DH)))
            eng.drain()
            eng.tensor_copy(al_pair[0:64, :], al_cols[0:64, 0:H:2])
            eng.tensor_copy(al_pair[64:128, :], al_cols[64:128, 1:H:2])
            eng.tensor_copy(lgsel[0:64, :], lg_cols[0:64, 0:H:2])
            eng.tensor_copy(lgsel[64:128, :], lg_cols[64:128, 1:H:2])
            eng.tensor_copy(lgst[0:64, :, :], lgfull[0:64, 0:H:2, :])
            eng.tensor_copy(lgst[64:128, :, :], lgfull[64:128, 1:H:2, :])
            eng.drain()
            for h in range(H // 2):
                eng.tensor_scalar(mt_tmp[:, h, :], iotaD[:], lg_cols[:, h:h + 1], None, ALU.mult)
            eng.tensor_tensor(gp_tmp[:], iotaC.ap().unsqueeze(1).broadcast_to((128, DC, 128)),
                              lgsel.ap().unsqueeze(2).broadcast_to((128, DC, 128)), ALU.mult)
            eng.tensor_scalar(gw_tmp[:], lgfull[:], iota_rev[:, 0:1], None, ALU.mult)
            eng.tensor_tensor(gCmk[:], lgsel.ap().unsqueeze(2).broadcast_to((128, DC, 16)),
                              iotaC.ap()[:, 0:16].unsqueeze(1).broadcast_to((128, DC, 16)),
                              ALU.mult)
            eng.drain().then_inc(sems['VS'], 1)
            eng.memset(stateT[:], 0.0)
            eng.memset(stateT_bf[:], 0.0)
            eng.drain().then_inc(sems['VS'], 1)
        pub('VS'); VS_pre1 = cnt['VS']
        pub('VS'); VS_state0 = cnt['VS']

        if me('scalar'):
            wait('VS', VS_pre1)
            eng.activation(mt_tmp[:], mt_tmp[:], AF.Exp)
            eng.activation(gp_all.ap(), gp_tmp[:], AF.Exp)
            eng.activation(gwfull.ap(), gw_tmp[:], AF.Exp)
            eng.activation(gCmk[:], gCmk[:], AF.Exp, scale=float(C))
            eng.activation(gCst[0:64, :, :], lgfull[0:64, 0:H:2, :], AF.Exp, scale=float(C))
            eng.activation(gCst[64:128, :, :], lgfull[64:128, 1:H:2, :], AF.Exp, scale=float(C))
            eng.drain().then_inc(sems['SS'], 1)
        pub('SS'); SS_exps = cnt['SS']

        if me('vector'):
            wait('SS', SS_exps)
            eng.tensor_tensor(MT.ap()[:, :, 0:4, :].transpose([0, 2, 1, 3]),
                              mt_tmp.ap().rearrange("p (a b) c -> p a b c", b=2),
                              causal.ap().unsqueeze(1).unsqueeze(1).broadcast_to((128, 4, 2, 128)),
                              ALU.mult)
            eng.drain()
            for h in range(H // 2, H):
                eng.tensor_scalar(mt_tmp[:, h - H // 2, :], iotaD[:],
                                  lg_cols[:, h:h + 1], None, ALU.mult)
            eng.drain().then_inc(sems['VS'], 1)
        pub('VS'); VS_mt2 = cnt['VS']
        if me('scalar'):
            wait('VS', VS_mt2)
            eng.activation(mt_tmp[:], mt_tmp[:], AF.Exp)
            eng.drain().then_inc(sems['SS'], 1)
        pub('SS'); SS_mt2 = cnt['SS']
        if me('vector'):
            wait('SS', SS_mt2)
            eng.tensor_tensor(MT.ap()[:, :, 4:8, :].transpose([0, 2, 1, 3]),
                              mt_tmp.ap().rearrange("p (a b) c -> p a b c", b=2),
                              causal.ap().unsqueeze(1).unsqueeze(1).broadcast_to((128, 4, 2, 128)),
                              ALU.mult)
            eng.drain().then_inc(sems['VS'], 1)
        pub('VS'); VS_MT = cnt['VS']

        lnv = dict(rows=0, bc=0, sqfree=0)

        def ln_block(buf, c0, n, dep=None):
            if me('scalar'):
                if dep is not None:
                    wait(dep[0], dep[1])
                wait('TS', lnv['sqfree'])
                for dj in range(DC):
                    eng.activation(sq_sb[:, dj, 0:n], buf[:, dj, c0:c0 + n], AF.Square)
                eng.drain().then_inc(sems['SS'], 1)
            pub('SS'); ss_sq = cnt['SS']
            if me('tensor'):
                if dep is not None:
                    wait(dep[0], dep[1])
                wait('SS', ss_sq)
                wait('VS', lnv['rows'])
                for dj in range(DC):
                    eng.matmul(stx_ps[:, 0:n], ones_bf[:], buf[:, dj, c0:c0 + n],
                               start=(dj == 0), stop=(dj == DC - 1))
                for dj in range(DC):
                    inst = eng.matmul(sts_ps[:, 0:n], ones_bf[:], sq_sb[:, dj, 0:n],
                                      start=(dj == 0), stop=(dj == DC - 1))
                inst.then_inc(sems['TS'], 1)
            pub('TS'); ts_st = cnt['TS']; lnv['sqfree'] = ts_st
            if me('vector'):
                wait('TS', ts_st)
                eng.tensor_scalar(r0[:, 0:n], stx_ps[:, 0:n], 1.0 / D, None, ALU.mult)
                eng.tensor_scalar(r1[:, 0:n], sts_ps[:, 0:n], 1.0 / D, 1e-5, ALU.mult, ALU.add)
                eng.drain()
                eng.tensor_tensor(r2[:, 0:n], r0[:, 0:n], r0[:, 0:n], ALU.mult)
                eng.drain()
                eng.tensor_tensor(r1[:, 0:n], r1[:, 0:n], r2[:, 0:n], ALU.subtract)
                eng.drain().then_inc(sems['VS'], 1)
            pub('VS'); vs_var = cnt['VS']; lnv['rows'] = vs_var
            if me('scalar'):
                wait('VS', vs_var)
                eng.activation(r1[:, 0:n], r1[:, 0:n], AF.Ln)
                eng.drain()
                eng.activation(r1[:, 0:n], r1[:, 0:n], AF.Exp, scale=-0.5)
                eng.drain().then_inc(sems['SS'], 1)
            pub('SS'); ss_rs = cnt['SS']
            if me('vector'):
                wait('SS', ss_rs)
                eng.tensor_copy(arow[:, 0:n], r1[:, 0:n])
                eng.tensor_tensor(r2[:, 0:n], r0[:, 0:n], r1[:, 0:n], ALU.mult)
                eng.drain()
                eng.tensor_scalar(brow[:, 0:n], r2[:, 0:n], -1.0, None, ALU.mult)
                eng.drain().then_inc(sems['VS'], 1)
            pub('VS'); vs_ab = cnt['VS']
            if me('tensor'):
                wait('VS', vs_ab)
                wait('VS', lnv['bc'])
                eng.matmul(bca_ps[:, 0:n], onesb_row[:], arow[:, 0:n], start=True, stop=True)
                inst = eng.matmul(bcb_ps[:, 0:n], onesb_row[:], brow[:, 0:n], start=True, stop=True)
                inst.then_inc(sems['TS'], 1)
            pub('TS'); ts_bc = cnt['TS']
            if me('vector'):
                wait('TS', ts_bc)
                eng.tensor_copy(a_sb[:, 0:n], bca_ps[:, 0:n])
                eng.tensor_copy(b_sb[:, 0:n], bcb_ps[:, 0:n])
                eng.drain()
                for dj in range(DC):
                    eng.tensor_tensor(buf[:, dj, c0:c0 + n], buf[:, dj, c0:c0 + n],
                                      a_sb[:, 0:n], ALU.mult)
                eng.drain()
                for dj in range(DC):
                    eng.tensor_tensor(buf[:, dj, c0:c0 + n], buf[:, dj, c0:c0 + n],
                                      b_sb[:, 0:n], ALU.add)
                eng.drain().then_inc(sems['VS'], 1)
            pub('VS'); lnv['bc'] = cnt['VS']
            return cnt['VS']

        scn = dict(vA=0, vA2=0, tRD=0, tID=0, vST={}, vRD=0, vstate=VS_state0,
                   t0=0, ln_cur=0, gc_rk=0)

        def scan_chunk(buf, wk0, full, cast, pre_vs=0, ps_alt=False):
            if ps_alt:
                vps = reads_ps
                wps = st_pso.ap().bitcast(BF16)
            else:
                vps = v_ps
                wps = wk_ps.ap()
            if me('tensor'):
                wait('VS', scn['vA2'] if ps_alt else scn['vA'])
                wait('GDP', GD_pre)
                if pre_vs:
                    wait('VS', pre_vs)
                for ih in range(2):
                    for dj in range(DC):
                        eng.matmul(vps[:, ih * 512:(ih + 1) * 512],
                                   buf[:, dj, wk0 + 1:wk0 + 1 + C],
                                   WwT_sb[:, dj, ih * 512:(ih + 1) * 512],
                                   start=(dj == 0), stop=(dj == DC - 1))
                for dc in range(DC):
                    inst = eng.transpose(
                        wps.rearrange("p (a c) -> p a c", a=DC)[:, dc, :],
                        buf[:, dc, wk0:wk0 + C], identb[:])
                inst.then_inc(sems['TS'], 1)
            pub('TS'); tA = cnt['TS']
            if me('vector'):
                wait('TS', tA)
                if scn['tRD']:
                    wait('TS', scn['tRD'])
                eng.tensor_copy(v_tm.ap().rearrange("p a c -> p (a c)"), vps[:])
                eng.tensor_copy(wk_tm.ap().rearrange("p a c -> p (a c)"), wps)
                eng.tensor_tensor(vgw.ap().rearrange("p a c -> p (a c)"),
                                  vps[:],
                                  gwfull.ap().rearrange("p h i -> p (h i)"), ALU.mult)
                eng.drain().then_inc(sems['VS'], 1)
            pub('VS')
            if ps_alt:
                scn['vA2'] = cnt['VS']
            else:
                scn['vA'] = cnt['VS']
            vA = cnt['VS']

            if full:
                if me('gpsimd'):
                    if scn['tRD']:
                        wait('TS', scn['tRD'])
                    wait('SS', SS_exps)
                    wait('VS', scn['ln_cur'])
                    eng.tensor_tensor(rkgp[0:64, :, :],
                                      buf[0:64, :, wk0 + 1:wk0 + 1 + C],
                                      gp_all[0:64, :, :], ALU.mult)
                    eng.tensor_tensor(rkgp[64:128, :, :],
                                      buf[64:128, :, wk0 + 1:wk0 + 1 + C],
                                      gp_all[64:128, :, :], ALU.mult)
                    eng.drain().then_inc(sems['GC'], 1)
                pub('GC'); scn['gc_rk'] = cnt['GC']
                for g in range(4):
                    heads = [4 * g + j for j in range(4)]
                    if me('tensor'):
                        wait('VS', scn['vST'].get(g - 2 if g >= 2 else g + 2, 0))
                        if g < 2:
                            wait('VS', VS_MT)
                        for j, h in enumerate(heads):
                            par, dcb = h % 2, h // 2
                            slot = (g % 2) * 256 + (j // 2) * 128
                            bank = st_pse if par == 0 else st_pso
                            inst = eng.matmul(bank[:, slot:slot + 128],
                                              buf[par * 64:par * 64 + 64, dcb, wk0:wk0 + C],
                                              buf[par * 64:par * 64 + 64, dcb, wk0 + 1:wk0 + 1 + C],
                                              start=True, stop=True)
                        inst.then_inc(sems['TS'], 1)
                    pub('TS'); tst = cnt['TS']
                    if me('vector'):
                        wait('TS', tst)
                        sl0 = (g % 2) * 256
                        eng.tensor_tensor(
                            stil[g].ap()[:, 0:2, :],
                            st_pse.ap()[:, sl0:sl0 + 256].rearrange("p (a c) -> p a c", a=2),
                            MT[:, 0, 2 * g:2 * g + 2, :], ALU.mult)
                        eng.tensor_tensor(
                            stil[g].ap()[:, 2:4, :],
                            st_pso.ap()[:, sl0:sl0 + 256].rearrange("p (a c) -> p a c", a=2),
                            MT[:, 1, 2 * g:2 * g + 2, :], ALU.mult)
                        eng.drain().then_inc(sems['VS'], 1)
                    pub('VS'); scn['vST'][g] = cnt['VS']
                    if me('tensor'):
                        wait('VS', scn['vST'][g])
                        wait('VS', vA)
                        wait('VS', scn['vstate'])
                        if g == 0:
                            wait('GC', scn['gc_rk'])
                            wait('SS', scn['vRD'])
                        rview = reads_ps.ap().rearrange("p (a c) -> p a c", a=DC)
                        dview = delta_ps.ap().rearrange("p (a c) -> p a c", a=DC)
                        for j, h in enumerate(heads):
                            par, dcb = h % 2, h // 2
                            pb = par * 64
                            eng.matmul(rview[pb:pb + 64, dcb, :], stateT_bf[:, h, :],
                                       rkgp[:, dcb, :], start=True, stop=False)
                            eng.matmul(rview[pb:pb + 64, dcb, :],
                                       v_tm[:, dcb, pb:pb + 64],
                                       stil[g][:, (j % 2) * 2 + j // 2, :],
                                       start=False, stop=True)
                            inst = eng.matmul(dview[pb:pb + 64, dcb, :],
                                              wk_tm[:, dcb, pb:pb + 64], vgw[:, dcb, pb:pb + 64],
                                              start=True, stop=True)
                        inst.then_inc(sems['TS'], 1)
                    pub('TS'); scn['tID'] = cnt['TS']
                scn['tRD'] = scn['tID']
                t_inter = scn['tRD']
            else:
                if me('tensor'):
                    wait('VS', vA)
                    wait('VS', scn['vstate'])
                    dview = delta_ps.ap().rearrange("p (a c) -> p a c", a=DC)
                    for h in range(H):
                        par, dcb = h % 2, h // 2
                        pb = par * 64
                        inst = eng.matmul(dview[pb:pb + 64, dcb, :],
                                          wk_tm[:, dcb, pb:pb + 64], vgw[:, dcb, pb:pb + 64],
                                          start=True, stop=True)
                    inst.then_inc(sems['TS'], 1)
                pub('TS'); scn['tRD'] = cnt['TS']
                scn['tID'] = scn['tRD']
                t_inter = 0

            if me('vector'):
                wait('TS', scn['tID'])
                eng.tensor_tensor(stateT.ap().rearrange("p a i -> p (a i)"),
                                  stateT.ap().rearrange("p a i -> p (a i)"),
                                  gCst.ap().rearrange("p a i -> p (a i)"), ALU.mult)
                eng.drain()
                eng.tensor_tensor(stateT.ap().rearrange("p a i -> p (a i)"),
                                  stateT.ap().rearrange("p a i -> p (a i)"),
                                  delta_ps[:], ALU.add)
                eng.drain()
                if cast:
                    eng.tensor_copy(stateT_bf[0:64, 0:H:2, :], stateT[0:64, :, :])
                    eng.tensor_copy(stateT_bf[64:128, 1:16:2, :], stateT[64:128, :, :])
                    eng.drain()
                eng.drain().then_inc(sems['VS'], 1)
            pub('VS'); scn['vstate'] = cnt['VS']

            if full:
                if me('scalar'):
                    wait('TS', scn['tRD'])
                    rview = reads_ps.ap().rearrange("p (a c) -> p a c", a=DC)
                    for dcb in range(DC):
                        eng.activation(reads_fm[:, dcb, scn['t0']:scn['t0'] + C],
                                       rview[:, dcb, :], AF.Copy,
                                       scale=al_pair[:, dcb:dcb + 1])
                    eng.drain().then_inc(sems['SS'], 1)
                pub('SS'); scn['vRD'] = cnt['SS']

        for b in range(NBM):
            vs_lnb = ln_block(xln_m.ap(), 1 + b * LB, LB, dep=('GM%d' % b, 16))
            if b == 0:
                vs_lnb = ln_block(xln_m.ap(), 0, 1)
            for ci in range(CPB):
                m = b * CPB + ci
                scn['t0'] = m * C
                scn['ln_cur'] = vs_lnb
                scan_chunk(xln_m.ap(), m * C, full=True, cast=True,
                           pre_vs=(vs_lnb if ci == 0 else 0))
        VS_lite = scn['vstate']
        SS_scan_end = cnt['SS']
        VS_scan_end0 = cnt['VS']
        if me('gpsimd'):
            wait('VS', VS_lite)
            eng.dma_start(out=cc_in,
                          in_=stateT.ap().rearrange("p a i -> p (a i)")).then_inc(sems['CCI'], 16)
            eng.wait_ge(sems['CCI'], 16)
            eng.collective_compute(
                "AllGather", mybir.AluOpType.bypass,
                replica_groups=([[0, 1, 2, 3], [4, 5, 6, 7]] if n_cores == 8
                                else [list(range(n_cores))]),
                ins=[cc_in.opt()], outs=[cc_out.opt()],
            ).then_inc(sems['CCS'])
            eng.wait_ge(sems['CCS'], 1)
            eng.dma_start(out=gth_sb[:, 0:NGRP, :],
                          in_=cc_out.rearrange("(r p) i -> p r i", p=128)).then_inc(sems['CCO'], 16)
        pub('CCI', 16); pub('CCS', 1); pub('CCO', 16)
        CCO_done = cnt['CCO']
        vs_comb = 0
        if me('vector'):
            wait('CCO', CCO_done)
            eng.memset(stateT[:], 0.0)
            eng.drain().then_inc(sems['VS'], 1)
        pub('VS'); vs_m0 = cnt['VS']
        for r in range(NGRP):
            if me('vector'):
                if r == 0:
                    wait('VS', vs_m0)
                eng.tensor_scalar(gp_tmp.ap().rearrange("p a c -> p (a c)")[:, 0:DC * DH],
                                  lgst.ap().rearrange("p a i -> p (a i)"),
                                  gapc_sb[:, r:r + 1], maskc_sb[:, r:r + 1],
                                  ALU.mult, ALU.add)
                eng.drain().then_inc(sems['VS'], 1)
            pub('VS'); vs_e1 = cnt['VS']
            if me('scalar'):
                wait('VS', vs_e1)
                eng.activation(gp_tmp.ap().rearrange("p a c -> p (a c)")[:, 0:DC * DH],
                               gp_tmp.ap().rearrange("p a c -> p (a c)")[:, 0:DC * DH], AF.Exp)
                eng.drain().then_inc(sems['SS'], 1)
            pub('SS'); ss_e1 = cnt['SS']
            if me('vector'):
                wait('SS', ss_e1)
                eng.tensor_tensor(gp_tmp.ap().rearrange("p a c -> p (a c)")[:, DC * DH:2 * DC * DH],
                                  gp_tmp.ap().rearrange("p a c -> p (a c)")[:, 0:DC * DH],
                                  gth_sb[:, r, :], ALU.mult)
                eng.drain()
                eng.tensor_tensor(stateT.ap().rearrange("p a i -> p (a i)"),
                                  stateT.ap().rearrange("p a i -> p (a i)"),
                                  gp_tmp.ap().rearrange("p a c -> p (a c)")[:, DC * DH:2 * DC * DH],
                                  ALU.add)
                eng.drain().then_inc(sems['VS'], 1)
            pub('VS'); vs_comb = cnt['VS']
        if me('vector'):
            eng.tensor_copy(stateT_bf[0:64, 0:H:2, :], stateT[0:64, :, :])
            eng.tensor_copy(stateT_bf[64:128, 1:16:2, :], stateT[64:128, :, :])
            eng.drain().then_inc(sems['VS'], 1)
        pub('VS'); scn['vstate'] = cnt['VS']

        vs_corr = scn['vstate']
        ts_corr = 0
        for m in range(NCHT):
            if me('vector'):
                if m == 0:
                    wait('VS', scn['vstate'])
                else:
                    wait('TS', ts_corr)
                eng.tensor_tensor(vgw.ap()[:], gp_all[:],
                                  gCmk.ap()[:, :, m:m + 1].broadcast_to((128, DC, 128)),
                                  ALU.mult)
                eng.drain()
                eng.tensor_tensor(rkgp.ap()[:], xln_m[:, :, m * C + 1:m * C + 1 + C],
                                  vgw.ap()[:], ALU.mult)
                eng.drain().then_inc(sems['VS'], 1)
            pub('VS'); vs_rk3 = cnt['VS']
            if me('tensor'):
                wait('VS', vs_rk3)
                if m == 0:
                    wait('SS', SS_scan_end)
                else:
                    wait('SS', cnt['SS'])
                rview = reads_ps.ap().rearrange("p (a c) -> p a c", a=DC)
                for h in range(H):
                    par, dcb = h % 2, h // 2
                    pb = par * 64
                    inst = eng.matmul(rview[pb:pb + 64, dcb, :], stateT_bf[:, h, :],
                                      rkgp[:, dcb, :], start=True, stop=True)
                inst.then_inc(sems['TS'], 1)
            pub('TS'); ts_corr = cnt['TS']
            if me('scalar'):
                wait('TS', ts_corr)
                rview = reads_ps.ap().rearrange("p (a c) -> p a c", a=DC)
                for dcb in range(DC):
                    eng.activation(rkgp[:, dcb, :], rview[:, dcb, :], AF.Copy,
                                   scale=al_pair[:, dcb:dcb + 1])
                eng.drain().then_inc(sems['SS'], 1)
            pub('SS'); ss_corr = cnt['SS']
            if me('vector'):
                wait('SS', ss_corr)
                eng.tensor_tensor(reads_fm[:, :, m * C:m * C + C],
                                  reads_fm[:, :, m * C:m * C + C],
                                  rkgp.ap()[:], ALU.add)
                eng.drain().then_inc(sems['VS'], 1)
            pub('VS')
        VS_scan_end = cnt['VS']

        sd_x = {}
        ts_tr = {}
        vs_slot = {}
        vs_x2 = {}
        tr_emitted = {}
        for m in range(NCHT):
            sl = m % 2
            if me('sync'):
                if m >= 2:
                    wait('VS', vs_slot[m - 2])
                    wait('SR', 16 * m)
                eng.dma_start(out=xres_t[sl][:],
                              in_=x_tm[m * 128:(m + 1) * 128, :]).then_inc(sems['SX%d' % sl], 16)
            pub('SX%d' % sl, 16); sd_x[m] = cnt['SX%d' % sl]
            if me('tensor'):
                if m == 0:
                    wait('SS', SS_scan_end)
                    wait('VS', VS_scan_end)
                if m >= 2:
                    wait('VS', vs_slot[m - 2])
                for ih in range(2):
                    for dcb in range(DC):
                        eng.matmul(op_ps[:, sl, ih * 512:(ih + 1) * 512],
                                   reads_fm[:, dcb, m * 128:(m + 1) * 128],
                                   WrT_sb[:, dcb, ih * 512:(ih + 1) * 512],
                                   start=(dcb == 0), stop=False)
                    for q in range(4):
                        inst = eng.matmul(
                            op_ps[:, sl, ih * 512 + q * 128: ih * 512 + (q + 1) * 128],
                            xln_m[:, ih * 4 + q, m * 128 + 1: m * 128 + 129],
                            identb[:], start=False, stop=(q == 3))
                inst.then_inc(sems['TS'], 1)
            pub('TS'); ts_op = cnt['TS']
            if me('vector'):
                wait('TS', ts_op)
                wait('SX%d' % sl, sd_x[m])
                eng.tensor_tensor(xres_t[sl][:], op_ps.ap()[:, sl, :], xres_t[sl][:], ALU.add)
                eng.drain().then_inc(sems['VS'], 1)
            pub('VS'); vs_r0 = cnt['VS']
            if me('sync'):
                wait('VS', vs_r0)
                eng.dma_start(out=xres_hbm[m * 128:(m + 1) * 128, :],
                              in_=xres_t[sl][:]).then_inc(sems['SR'], 16)
            pub('SR', 16)
            if me('vector'):
                eng.bn_stats(bn6[:, 0:6], xres_t[sl][:, 0:512])
                eng.bn_stats(bn6[:, 6:12], xres_t[sl][:, 512:1024])
                eng.drain()
                eng.bn_aggr(mv2[:], bn6[:])
                eng.drain()
                eng.tensor_scalar(m1c[:], mv2[:, 1:2], 1.0, 1e-5, ALU.mult, ALU.add)
                eng.drain().then_inc(sems['VS'], 1)
            pub('VS'); vs_v2 = cnt['VS']
            if me('scalar'):
                wait('VS', vs_v2)
                eng.activation(m1c[:], m1c[:], AF.Ln)
                eng.drain()
                eng.activation(m1c[:], m1c[:], AF.Exp, scale=-0.5)
                eng.drain().then_inc(sems['SS'], 1)
            pub('SS'); ss_r2 = cnt['SS']
            if me('vector'):
                wait('SS', ss_r2)
                if m >= 2:
                    wait('TS', ts_tr[m - 2])
                eng.tensor_scalar(xl2tm[sl][:], xres_t[sl][:], mv2[:, 0:1], m1c[:, 0:1],
                                  ALU.subtract, ALU.mult)
                eng.drain().then_inc(sems['VS'], 1)
            pub('VS'); vs_slot[m] = cnt['VS']

            def emit_tr(mm):
                sl2 = mm % 2
                if me('tensor'):
                    wait('VS', vs_slot[mm])
                    if mm >= 2:
                        wait('VS', vs_x2[mm - 2])
                    for dj in range(DC):
                        inst = eng.transpose(tr_ps.ap()[:, sl2, dj * 128:(dj + 1) * 128],
                                             xl2tm[sl2][:, dj * 128:(dj + 1) * 128], identb[:])
                    inst.then_inc(sems['TS'], 1)
                pub('TS'); ts_tr[mm] = cnt['TS']
                if me('vector'):
                    wait('TS', ts_tr[mm])
                    for dj in range(DC):
                        eng.tensor_copy(xln2_fm[:, dj, mm * 128:(mm + 1) * 128],
                                        tr_ps.ap()[:, sl2, dj * 128:(dj + 1) * 128])
                    eng.drain().then_inc(sems['VS'], 1)
                pub('VS'); vs_x2[mm] = cnt['VS']

            tr_emitted[m] = False
            if m >= 1:
                emit_tr(m - 1)
            if m == NCHT - 1:
                emit_tr(m)
        VS_p3 = cnt['VS']
        TS_p3 = cnt['TS']

        vs_wprev = VS_p3
        ts_wtr = 0
        for tc in range(2 * NTC):
            if me('tensor'):
                wait('VS', vs_wprev)
                for dj in range(DC):
                    inst = eng.matmul(rt_ps[:, 0:E], xln2_fm[:, dj, tc * 128:(tc + 1) * 128],
                                      gateT_sb[:, dj, :], start=(dj == 0), stop=(dj == DC - 1))
                inst.then_inc(sems['TS'], 1)
            pub('TS'); ts_l = cnt['TS']
            if me('vector'):
                wait('TS', ts_l)
                eng.reduce_max(m0[:], rt_ps[:, 0:E], axis=AX)
                eng.drain()
                eng.tensor_scalar(sm_t[:], rt_ps[:, 0:E], m0[:, 0:1], None, ALU.subtract)
                eng.drain().then_inc(sems['VS'], 1)
            pub('VS'); vs_sm = cnt['VS']
            if me('scalar'):
                wait('VS', vs_sm)
                eng.activation(sm_t[:], sm_t[:], AF.Exp)
                eng.drain().then_inc(sems['SS'], 1)
            pub('SS'); ss_e = cnt['SS']
            if me('vector'):
                wait('SS', ss_e)
                eng.reduce_sum(m0[:], sm_t[:], axis=AX)
                eng.drain()
                eng.reciprocal(m1c[:], m0[:])
                eng.drain()
                eng.tensor_scalar(sm_t[:], sm_t[:], m1c[:, 0:1], None, ALU.mult)
                eng.drain()
                eng.reduce_max(m0[:], sm_t[:], axis=AX)
                eng.drain()
                eng.tensor_scalar(w_tm[:, tc, :], sm_t[:], m0[:, 0:1], None, ALU.is_lt)
                eng.drain()
                eng.tensor_tensor(w_tm[:, tc, :], w_tm[:, tc, :], sm_t[:], ALU.mult)
                eng.drain()
                eng.reduce_max(m1c[:], w_tm[:, tc, :], axis=AX)
                eng.drain()
                eng.tensor_scalar(w_tm[:, tc, :], sm_t[:], m1c[:, 0:1], None, ALU.is_ge)
                eng.drain()
                eng.tensor_tensor(w_tm[:, tc, :], w_tm[:, tc, :], sm_t[:], ALU.mult)
                eng.drain()
                wait('TS', ts_wtr)
                eng.tensor_copy(w_bfT[:], w_tm[:, tc, :])
                eng.drain().then_inc(sems['VS'], 1)
            pub('VS'); vs_wprev = cnt['VS']
            if me('tensor'):
                wait('VS', vs_wprev)
                eng.transpose(wf_ps[:, tc * 128:(tc + 1) * 128], w_bfT[:],
                              identb[:]).then_inc(sems['TS'], 1)
            pub('TS'); ts_wtr = cnt['TS']
        if me('vector'):
            wait('TS', ts_wtr)
            eng.tensor_copy(w_fm.ap().rearrange("p a c -> p (a c)"), wf_ps[:])
            eng.drain().then_inc(sems['VS'], 1)
        pub('VS'); VS_route = cnt['VS']

        vs_pass_done = 0
        vs_yend_prev = 0
        ts_y_last = 0
        gd_w1 = {}
        w1_free = {}
        gd_w2 = {}
        w2_free = {}
        w1_ld = 0
        w2_ld = 0
        sd_fin = {}
        vs_fin = {}
        for p in range(2):
            toff = p * TT
            tcoff = p * NTC
            vs_b2 = {}
            b2_state = dict(done=False, vs_yinit=0)

            def emit_b2init():
                for i in range(NTC * 2):
                    tc, dcl = i // 2, i % 2
                    yb = y_ps.ap()[:, i % 2, :]
                    if me('tensor'):
                        if i == 0:
                            wait('VS', VS_route)
                            if p == 1:
                                wait('VS', vs_yend_prev)
                        if i >= 2:
                            wait('VS', vs_b2[i - 2])
                        eng.matmul(yb, w_fm[:, tcoff + tc, :],
                                   b2_sb[:, dcl * 512:(dcl + 1) * 512],
                                   start=True, stop=True).then_inc(sems['TS'], 1)
                    pub('TS'); ts_b2 = cnt['TS']
                    if me('vector'):
                        wait('TS', ts_b2)
                        if i == 0 and p == 1:
                            wait('VS', vs_pass_done)
                        eng.tensor_copy(y_acc[:, tc, dcl * 512:(dcl + 1) * 512], yb)
                        eng.drain().then_inc(sems['VS'], 1)
                    pub('VS'); vs_b2[i] = cnt['VS']
                b2_state['done'] = True
                b2_state['vs_yinit'] = cnt['VS']

            vs_yevac = {}
            iy_g = 0
            ss_gelu = {}
            ih_g = 0
            for e in range(E):
                for fb in range(2):
                    fcol0 = fb * FB
                    slot_ids = []
                    for s in range(NS1):
                        slot = w1_ld % 2
                        slot_ids.append(slot)
                        if me('gpsimd'):
                            if w1_ld >= 2:
                                wait('TS', w1_free[w1_ld - 2])
                            else:
                                wait('TS', TS_p3)
                            eng.dma_start(
                                out=w1b[:, slot, :, :],
                                in_=w1t.rearrange("e (a p) f -> e p a f", p=128)[
                                    e, :, :, fcol0 + s * W1S * 128: fcol0 + (s + 1) * W1S * 128]
                            ).then_inc(sems['GW1%d' % slot], 16)
                        pub('GW1%d' % slot, 16)
                        gd_w1[w1_ld] = ('GW1%d' % slot, cnt['GW1%d' % slot])
                        w1_ld += 1
                    first_h = True
                    for fi in range(NF):
                        s_idx = fi // W1S
                        for th in range(NTH):
                            hb = h_ps.ap()[:, ih_g % 2, :]
                            if me('tensor'):
                                wait(*gd_w1[w1_ld - NS1 + s_idx])
                                if ih_g >= 2:
                                    wait('SS', ss_gelu[ih_g - 2])
                                for dj in range(DC):
                                    inst = eng.matmul(
                                        hb[:, 0:H512], w1b[:, slot_ids[s_idx], dj,
                                                (fi % W1S) * 128:(fi % W1S) * 128 + 128],
                                        xln2_fm[:, dj, toff + th * H512: toff + (th + 1) * H512],
                                        start=(dj == 0), stop=(dj == DC - 1))
                                inst.then_inc(sems['TS'], 1)
                            pub('TS'); ts_h = cnt['TS']
                            if fi % W1S == W1S - 1 and th == NTH - 1:
                                w1_free[w1_ld - NS1 + s_idx] = ts_h
                            if me('scalar'):
                                wait('TS', ts_h)
                                if first_h and ts_y_last:
                                    wait('TS', ts_y_last)
                                eng.activation(
                                    g_sb[:, fi, th * H512:(th + 1) * H512], hb[:, 0:H512],
                                    AF.Gelu_apprx_tanh,
                                    bias=b1_sb.ap()[:, e, fb * NF + fi:fb * NF + fi + 1])
                                first_h = False
                                eng.drain().then_inc(sems['SS'], 1)
                            pub('SS'); ss_gelu[ih_g] = cnt['SS']
                            ih_g += 1
                    ss_gelu_end = cnt['SS']
                    if not b2_state['done']:
                        emit_b2init()
                    for dcl in range(2):
                        slot2 = w2_ld % 2
                        if me('gpsimd'):
                            if w2_ld >= 2:
                                wait('TS', w2_free[w2_ld - 2])
                            else:
                                wait('TS', TS_p3)
                            eng.dma_start(
                                out=w2b[:, slot2, :, :],
                                in_=w2t.rearrange("e (a p) d -> e p a d", p=128)[
                                    e, :, fb * NF:(fb + 1) * NF,
                                    dcl * 512:(dcl + 1) * 512]
                            ).then_inc(sems['GW2%d' % slot2], 16)
                        pub('GW2%d' % slot2, 16)
                        gd_w2[w2_ld] = ('GW2%d' % slot2, cnt['GW2%d' % slot2])
                        for tc in range(NTC):
                            yb = y_ps.ap()[:, iy_g % 2, :]
                            if me('tensor'):
                                wait(*gd_w2[w2_ld])
                                if tc == 0 and dcl == 0:
                                    wait('SS', ss_gelu_end)
                                    wait('VS', b2_state['vs_yinit'])
                                wait('VS', vs_yevac.get(iy_g - 2, 0))
                                for fi in range(NF):
                                    inst = eng.matmul(
                                        yb, g_sb[:, fi, tc * 128:(tc + 1) * 128],
                                        w2b[:, slot2, fi, :],
                                        start=(fi == 0), stop=(fi == NF - 1))
                                inst.then_inc(sems['TS'], 1)
                            pub('TS'); ts_y = cnt['TS']
                            if tc == NTC - 1:
                                w2_free[w2_ld] = ts_y
                            if tc == NTC - 1 and dcl == 1:
                                ts_y_last = ts_y
                            if me('vector'):
                                wait('TS', ts_y)
                                eng.scalar_tensor_tensor(
                                    y_acc[:, tc, dcl * 512:(dcl + 1) * 512], yb,
                                    w_tm.ap()[:, tcoff + tc, e:e + 1],
                                    y_acc[:, tc, dcl * 512:(dcl + 1) * 512],
                                    ALU.mult, ALU.add)
                                eng.drain().then_inc(sems['VS'], 1)
                            pub('VS'); vs_yevac[iy_g] = cnt['VS']
                            iy_g += 1
                        w2_ld += 1
            vs_y_end = cnt['VS']

            for tc in range(NTC):
                fidx = p * NTC + tc
                sl = fidx % 2
                if me('sync'):
                    if fidx >= 2:
                        wait('VS', vs_fin[fidx - 2])
                    else:
                        wait('VS', VS_p3)
                    wait('SR', 16 * NCHT)
                    eng.dma_start(out=xres_t[sl][:],
                                  in_=xres_hbm[toff + tc * 128: toff + (tc + 1) * 128, :]
                                  ).then_inc(sems['SX%d' % sl], 16)
                pub('SX%d' % sl, 16); sd_f = cnt['SX%d' % sl]
                if me('vector'):
                    wait('SX%d' % sl, sd_f)
                    wait('VS', vs_y_end)
                    if fidx >= 2:
                        wait(*sd_fin[fidx - 2])
                    eng.tensor_tensor(outb[sl][:], y_acc[:, tc, :], xres_t[sl][:], ALU.add)
                    eng.drain().then_inc(sems['VS'], 1)
                pub('VS'); vs_fin[fidx] = cnt['VS']
                if me('sync'):
                    wait('VS', vs_fin[fidx])
                    eng.dma_start(out=out[toff + tc * 128: toff + (tc + 1) * 128, :],
                                  in_=outb[sl][:]).then_inc(sems['SO%d' % sl], 16)
                pub('SO%d' % sl, 16); sd_fin[fidx] = ('SO%d' % sl, cnt['SO%d' % sl])
            vs_pass_done = cnt['VS']
            vs_yend_prev = vs_y_end

        if me('sync'):
            for n in ["SDP", "SR", "SX0", "SX1", "SO0", "SO1"]:
                wait(n, cnt[n])
        if me('gpsimd'):
            for n in ["GDP", "GW10", "GW11", "GW20", "GW21", "CCI", "CCO"]:
                wait(n, cnt[n])
        return cnt

    @block.sync
    def _(e):
        stream('sync', e)

    @block.gpsimd
    def _(e):
        stream('gpsimd', e)

    @block.tensor
    def _(e):
        stream('tensor', e)

    @block.vector
    def _(e):
        stream('vector', e)

    @block.scalar
    def _(e):
        stream('scalar', e)

    blk.__exit__(None, None, None)
    return nc


def make_core_inputs(x_flat, n_cores, T, P, WwT, WrT, gateT, dcol, lacol,
                     w1t_a, w2t_a, b1r, b2_a, seq=None):
    Ttot = x_flat.shape[0]
    if seq is None:
        seq = Ttot // 2 if n_cores > 1 else Ttot
    xT = np.ascontiguousarray(x_flat.T)
    maps = []
    for c in range(n_cores):
        t0 = c * T
        bs = (t0 // seq) * seq
        p_c = t0 - bs
        xTm = np.zeros((D, T + 1), np.float32)
        xTm[:, 1:] = xT[:, t0:t0 + T]
        if p_c:
            xTm[:, 0] = xT[:, t0 - 1]
        pos_c = (t0 - bs) // T
        gap = np.zeros((128, 8), np.float32)
        madd = np.full((128, 8), -1e30, np.float32)
        grp = (4 if n_cores == 8 else n_cores)
        for r in range(n_cores):
            r0 = r * T
            rbs = (r0 // seq) * seq
            if rbs == bs and r0 < t0:
                pos_r = (r0 - rbs) // T
                gap[:, r % grp] = float(T * (pos_c - pos_r - 1))
                madd[:, r % grp] = 0.0
        maps.append(dict(
            xTm=xTm, x_tm=np.ascontiguousarray(x_flat[t0:t0 + T]),
            gapcol=gap, maskadd=madd,
            WwT=WwT, WrT=WrT, gateT=gateT, dcol=dcol, lacol=lacol,
            w1t=w1t_a, w2t=w2t_a, b1r=b1r, b2=b2_a))
    return maps


def kernel(L0_attn_out, L0_attn_W_write, L0_attn_W_read, L0_attn_decay,
           L0_attn_log_alpha, L0_ffn_gate_w, L0_ffn_w1, L0_ffn_b1,
           L0_ffn_w2, L0_ffn_b2):
    x = np.ascontiguousarray(np.asarray(L0_attn_out, np.float32).reshape(-1, D))
    WwT = np.ascontiguousarray(np.asarray(L0_attn_W_write, np.float32).T)
    WrT = np.ascontiguousarray(np.asarray(L0_attn_W_read, np.float32).T)
    gT = np.ascontiguousarray(np.asarray(L0_ffn_gate_w, np.float32).T)
    dcol = np.ascontiguousarray(np.broadcast_to(
        np.asarray(L0_attn_decay, np.float32), (128, H)))
    lacol = np.ascontiguousarray(np.broadcast_to(
        np.asarray(L0_attn_log_alpha, np.float32), (128, H)))
    w1 = np.asarray(L0_ffn_w1, np.float32)
    w2 = np.asarray(L0_ffn_w2, np.float32)
    F_ = w1.shape[0]
    w1t_a = np.ascontiguousarray(w1.transpose(1, 2, 0))
    w2t_a = np.ascontiguousarray(w2.transpose(1, 2, 0))
    b1 = np.asarray(L0_ffn_b1, np.float32)
    b1r = np.ascontiguousarray(b1.T.reshape(F_ // 128, 128, E).transpose(1, 2, 0))
    b2_a = np.ascontiguousarray(np.asarray(L0_ffn_b2, np.float32))

    T, P = 2048, 6144
    nc = build_program(T=T, P=P, F=F_)
    maps = make_core_inputs(x, 8, T, P, WwT, WrT, gT, dcol, lacol,
                            w1t_a, w2t_a, b1r, b2_a)
    res = run_bass_kernel_spmd(nc, maps, core_ids=list(range(8)))
    outp = np.concatenate([r["out"] for r in res.results], axis=0)
    return outp.reshape(np.asarray(L0_attn_out).shape).astype(np.float32)
